# version 01
# baseline (speedup 1.0000x reference)
import math
import numpy as np
import ml_dtypes
from contextlib import ExitStack

import concourse.bass as bass
import concourse.mybir as mybir
from concourse.bass_utils import run_bass_kernel_spmd

F32 = mybir.dt.float32
BF16 = mybir.dt.bfloat16
AF = mybir.ActivationFunctionType
OP = mybir.AluOpType
AX = mybir.AxisListType
BF = ml_dtypes.bfloat16

L = 2048
D = 1024
CH = 512
NCH = 4
NEG = -1.0e30
TWO_PI = 2.0 * math.pi
INV_2PI = 1.0 / TWO_PI
MAGIC = 12582912.0


def C(method, *args, **kw):
    return (method, args, kw)


class PL:
    def __init__(self):
        self.steps = []

    def t(self, *calls):
        self.steps.append(("tensor", calls, 1))

    def v(self, *calls):
        self.steps.append(("vector", calls, 1))

    def s(self, *calls):
        self.steps.append(("scalar", calls, 1))

    def ld(self, *calls):
        self.steps.append(("sync", calls, 16 * len(calls)))

    def st(self, *calls):
        self.steps.append(("gpsimd", calls, 16 * len(calls)))


def emit_plan(nc, chain, state, pl):
    pres = []
    tot = state["val"]
    for eng, calls, inc in pl.steps:
        pres.append(tot)
        tot += inc
    state["val"] = tot
    with nc.Block() as blk:
        def mk(engname):
            def body(e):
                for i, (eng, calls, inc) in enumerate(pl.steps):
                    if eng != engname:
                        continue
                    if pres[i] > 0:
                        e.wait_ge(chain, pres[i])
                    insts = [getattr(e, m)(*a, **k) for (m, a, k) in calls]
                    if inc >= 16:
                        for ins in insts:
                            ins.then_inc(chain, 16)
                    else:
                        insts[-1].then_inc(chain, inc)
            return body
        blk.tensor(mk("tensor"))
        blk.vector(mk("vector"))
        blk.scalar(mk("scalar"))
        blk.sync(mk("sync"))
        blk.gpsimd(mk("gpsimd"))


def build_common(nc, es, chain, state):
    B = {}
    sb = lambda name, shape, dt: es.enter_context(nc.sbuf_tensor(name, shape, dt))
    ps = lambda name, shape: es.enter_context(nc.psum_tensor(name, shape, F32))
    B["ones_col_b"] = sb("g_ones_col", [128, 1], BF16)
    B["ones_row_b"] = sb("g_ones_row", [1, 128], BF16)
    B["zeros"] = sb("g_zeros", [128, CH], F32)
    B["scr_a"] = sb("g_scr_a", [128, CH], F32)
    B["scr_b"] = sb("g_scr_b", [128, CH], F32)
    B["scr_c"] = sb("g_scr_c", [128, CH], F32)
    B["scr_bf"] = sb("g_scr_bf", [128, CH], BF16)
    B["row_a"] = sb("g_row_a", [1, L], F32)
    B["row_bf_a"] = sb("g_row_bf_a", [1, CH], BF16)
    B["row_bf_b"] = sb("g_row_bf_b", [1, CH], BF16)
    B["xsum"] = [sb(f"g_xsum{g}", [128, CH], F32) for g in range(8)]
    B["p_main"] = [ps(f"g_pm{i}", [128, CH]) for i in range(4)]
    B["p_bc"] = ps("g_pbc", [128, CH])
    B["p_bc2"] = ps("g_pbc2", [128, CH])
    B["p_row"] = ps("g_prow", [32, CH])
    B["p_row2"] = ps("g_prow2", [32, CH])
    consts = {0.0, 1.0, -1.0, 0.125, 1e-4, 1e-5, 1e-12, 1.0 / 64.0,
              1.0 / 1024.0, -1.0 / 1024.0, 1.0 / 4096.0, INV_2PI, MAGIC,
              MAGIC + 0.25, -TWO_PI, math.pi / 2.0, NEG}
    for i in range(16):
        consts.add(-(2.0 ** (-0.5 * (i + 1))))
    for d_ in range(1, 16):
        consts.add(float(128 * d_))
    consts = sorted(consts)
    ctile = sb("g_consts", [128, len(consts)], F32)
    ctileb = sb("g_constsb", [128, len(consts)], BF16)
    pl = PL()
    pl.v(C("memset", B["ones_col_b"][:, :], 1.0))
    pl.v(C("memset", B["ones_row_b"][:, :], 1.0))
    pl.v(C("memset", B["zeros"][:, :], 0.0))
    for i, vv in enumerate(consts):
        pl.v(C("memset", ctile[:, i:i + 1], float(vv)))
        pl.v(C("memset", ctileb[:, i:i + 1], float(vv)))
        nc.const_aps.register(F32, float(vv), ctile[:, i:i + 1])
        nc.const_aps.register(BF16, float(vv), ctileb[:, i:i + 1])
    emit_plan(nc, chain, state, pl)
    return B


def ln_chunk(pl, B, write_calls_fn):
    xb = B["scr_bf"]
    sums = B["p_row"][0:1, :]
    sqs = B["p_row2"][0:1, :]
    mb = B["p_bc"]
    rb = B["p_bc2"]
    t2 = B["scr_b"]
    ra = B["row_a"]
    for g in range(8):
        xg = B["xsum"][g][:, :]
        pl.v(C("tensor_copy", xb[:, :], xg))
        pl.t(C("matmul", sums, B["ones_col_b"][:, :], xb[:, :],
              start=(g == 0), stop=(g == 7)))
        pl.s(C("activation", xb[:, :], xb[:, :], AF.Square))
        pl.t(C("matmul", sqs, B["ones_col_b"][:, :], xb[:, :],
              start=(g == 0), stop=(g == 7)))
    pl.v(C("tensor_scalar", ra[0:1, 0:CH], sums, -1.0 / 1024.0, None, OP.mult))
    pl.v(C("tensor_scalar", ra[0:1, CH:2 * CH], sqs, 1.0 / 1024.0, None, OP.mult))
    pl.s(C("activation", ra[0:1, 2 * CH:3 * CH], ra[0:1, 0:CH], AF.Square))
    pl.v(C("tensor_sub", ra[0:1, CH:2 * CH], ra[0:1, CH:2 * CH],
          ra[0:1, 2 * CH:3 * CH]))
    pl.s(C("activation", ra[0:1, CH:2 * CH], ra[0:1, CH:2 * CH], AF.Sqrt,
          bias=1e-5))
    pl.v(C("reciprocal", ra[0:1, CH:2 * CH], ra[0:1, CH:2 * CH]))
    pl.v(C("tensor_copy", B["row_bf_a"][:, :], ra[0:1, 0:CH]))
    pl.v(C("tensor_copy", B["row_bf_b"][:, :], ra[0:1, CH:2 * CH]))
    pl.t(C("matmul", mb[:, :], B["ones_row_b"][:, :], B["row_bf_a"][:, :]))
    pl.t(C("matmul", rb[:, :], B["ones_row_b"][:, :], B["row_bf_b"][:, :]))
    for g in range(8):
        pl.v(C("tensor_add", t2[:, :], B["xsum"][g][:, :], mb[:, :]))
        pl.v(C("tensor_mul", t2[:, :], t2[:, :], rb[:, :]))
        for eng, call in write_calls_fn(g, t2):
            getattr(pl, eng)(call)


def emit_l0_attn(nc, chain, state, B, dr, aux, xin, xout):
    ZC = 2080
    es = ExitStack()
    sb = lambda name, shape, dt: es.enter_context(nc.sbuf_tensor(name, shape, dt))
    zb = [sb(f"a_zb{g}", [128, L + 3], BF16) for g in range(16)]
    z16 = sb("a_z16", [32, L + 3], F32)
    gate = [sb(f"a_gate{g}", [128, CH], BF16) for g in range(8)]
    nre = [sb(f"a_nre{g}", [128, CH], F32) for g in range(8)]
    nim = [sb(f"a_nim{g}", [128, CH], F32) for g in range(8)]
    car_re = sb("a_car_re", [128, 8], F32)
    car_im = sb("a_car_im", [128, 8], F32)
    car_dn = sb("a_car_dn", [32, 1], F32)
    pw = sb("a_pw", [32, CH], F32)
    pwb = sb("a_pwb", [32, CH], BF16)
    den = sb("a_den", [32, CH], F32)
    cscb = sb("a_cscb", [32, CH], BF16)
    rates = sb("a_rates", [32, 1], F32)
    ssc_col = sb("a_ssc", [32, 1], F32)
    theta_col = sb("a_th", [128, 8], F32)
    nsre_col = sb("a_nsre", [128, 1], F32)
    nsim_col = sb("a_nsim", [128, 1], F32)
    ck_cols = sb("a_ck", [128, 68], F32)
    blk_re = sb("a_blkre", [128, 128], BF16)
    blk_im = sb("a_blkim", [128, 128], BF16)
    blk32 = sb("a_blk32", [128, 128], F32)
    wt32 = sb("a_wt32", [128, 1024], F32)
    wtb = sb("a_wtb", [128, 1024], BF16)
    sdist = sb("a_sdist", [32, L], F32)
    selpw = sb("a_selpw", [32, 1024], BF16)
    selT = sb("a_selT", [128, 256], BF16)
    tsc = sb("a_tsc", [32, 4], F32)
    es_x = ExitStack()
    sbx = lambda name, shape, dt: es_x.enter_context(nc.sbuf_tensor(name, shape, dt))
    xTb = [sbx(f"a_xtb{g}", [128, L], BF16) for g in range(8)]

    P = B["p_main"]
    scr_a, scr_b, scr_c = B["scr_a"], B["scr_b"], B["scr_c"]
    sqb = B["scr_bf"]
    pl = PL()

    pl.ld(C("dma_start", out=sdist[:, :], in_=aux["dist"][:, :]),
          C("dma_start", out=selpw[:, :], in_=aux["selpw"][:, :]),
          C("dma_start", out=selT[:, :], in_=aux["selT"][:, :]),
          C("dma_start", out=rates[0:28, 0:1], in_=dr["L0_attn_decay_slopes"][:, :]),
          C("dma_start", out=rates[28:32, 0:1], in_=dr["L0_attn_anchor_slopes"][:, :]),
          C("dma_start", out=ssc_col[:, 0:1], in_=dr["L0_attn_score_scale"][:, :]))
    pl.ld(*[C("dma_start", out=theta_col[:, g:g + 1],
              in_=dr["L0_attn_theta"][128 * g:128 * (g + 1), :]) for g in range(8)])
    pl.ld(*([C("dma_start", out=nsre_col[32 * r:32 * (r + 1), 0:1],
               in_=dr["L0_attn_norm_scale"][0:32, :]) for r in range(4)] +
            [C("dma_start", out=nsim_col[32 * r:32 * (r + 1), 0:1],
               in_=dr["L0_attn_norm_scale"][32:64, :]) for r in range(4)]))
    ckcalls = []
    for j in range(4):
        for g in range(17):
            n = min(128, ZC - 128 * g)
            ckcalls.append(C("dma_start", out=ck_cols[0:n, j * 17 + g:j * 17 + g + 1],
                             in_=dr["L0_attn_conv_kernel"][j, 128 * g:128 * g + n, :]))
    pl.ld(*ckcalls)
    pl.v(C("memset", blk32[:, :], 0.0))
    pl.ld(*[C("dma_start", out=blk32[32 * r:32 * (r + 1), 32 * r:32 * (r + 1)],
              in_=dr["L0_attn_W_re"][:, :]) for r in range(4)])
    pl.v(C("tensor_copy", blk_re[:, :], blk32[:, :]))
    pl.v(C("memset", blk32[:, :], 0.0))
    pl.ld(*[C("dma_start", out=blk32[32 * r:32 * (r + 1), 32 * r:32 * (r + 1)],
              in_=dr["L0_attn_W_im"][:, :]) for r in range(4)])
    pl.v(C("tensor_copy", blk_im[:, :], blk32[:, :]))
    pl.s(C("activation", tsc[:, 0:1], rates[:, 0:1], AF.Abs))
    pl.s(C("activation", tsc[:, 1:2], tsc[:, 0:1], AF.Exp, scale=-1.0))
    pl.s(C("activation", tsc[:, 2:3], tsc[:, 1:2], AF.Ln, bias=1.0))
    pl.v(C("tensor_scalar_max", tsc[:, 3:4], rates[:, 0:1], 0.0))
    pl.v(C("tensor_add", rates[:, 0:1], tsc[:, 2:3], tsc[:, 3:4]))
    pl.v(C("tensor_scalar_mul", rates[:, 0:1], rates[:, 0:1], -1.0))
    pl.v(*([C("memset", zb[g][:, 0:3], 0.0) for g in range(16)] +
           [C("memset", z16[:, 0:3], 0.0),
            C("memset", car_re[:, :], 0.0),
            C("memset", car_im[:, :], 0.0),
            C("memset", car_dn[:, :], 0.0)]))
    for g in range(8):
        for c in range(NCH):
            pl.ld(C("dma_start", out=B["scr_a"][:, :],
                    in_=xin[128 * g:128 * (g + 1), CH * c:CH * (c + 1)]))
            pl.v(C("tensor_copy", xTb[g][:, CH * c:CH * (c + 1)],
                  B["scr_a"][:, :]))

    win = dr["L0_attn_W_in"]
    for m in range(17):
        mm = min(128, ZC - 128 * m)
        for k in range(8):
            pl.ld(C("dma_start", out=wt32[:, 0:mm],
                    in_=win[128 * k:128 * (k + 1), 128 * m:128 * m + mm]))
            pl.v(C("tensor_copy", wtb[:, 0:mm], wt32[:, 0:mm]))
            pl.t(*[C("matmul", P[c][0:mm, :], wtb[0:128, 0:mm],
                     xTb[k][:, CH * c:CH * (c + 1)],
                     start=(k == 0), stop=(k == 7)) for c in range(NCH)])
        if m < 16:
            pl.v(*[C("tensor_copy", zb[m][0:mm, 3 + CH * c:3 + CH * (c + 1)],
                     P[c][0:mm, :]) for c in range(NCH)])
        else:
            pl.v(*[C("tensor_copy", z16[0:mm, 3 + CH * c:3 + CH * (c + 1)],
                     P[c][0:mm, :]) for c in range(NCH)])
    emit_plan(nc, chain, state, pl)
    es_x.close()
    gy = [sb(f"a_gy{g}", [128, L], BF16) for g in range(8)]
    pl = PL()

    sq_ps = B["p_row"]
    y_ps = B["p_bc2"]
    for c in range(NCH):
        t0 = CH * c
        pl.v(C("tensor_scalar", scr_a[0:32, :], z16[:, t0:t0 + CH],
              ck_cols[0:32, 16:17], None, OP.mult))
        for j in range(1, 4):
            pl.v(C("scalar_tensor_tensor", scr_a[0:32, :], z16[:, t0 + j:t0 + j + CH],
                  ck_cols[0:32, j * 17 + 16:j * 17 + 17], scr_a[0:32, :],
                  OP.mult, OP.add))
        pl.v(C("tensor_scalar", scr_b[0:32, :], sdist[:, t0:t0 + CH],
              rates[:, 0:1], None, OP.mult))
        pl.v(C("scalar_tensor_tensor", scr_c[0:32, :], scr_a[0:32, :],
              ssc_col[:, 0:1], scr_b[0:32, :], OP.mult, OP.add))
        pl.s(C("activation", pw[:, :], scr_c[0:32, :], AF.Exp))
        pl.v(C("tensor_copy", pwb[:, :], pw[:, :]))
        pl.v(C("tensor_tensor_scan", den[:, :], pw[:, :], B["zeros"][0:32, :],
              car_dn[:, 0:1] if c > 0 else 0.0, OP.add, OP.add))
        pl.v(C("tensor_copy", car_dn[:, 0:1], den[:, CH - 1:CH]))
        for g in range(8):
            zt = zb[8 + g]
            pl.v(C("tensor_scalar", scr_a[:, :], zt[:, t0:t0 + CH],
                  ck_cols[:, 8 + g:9 + g], None, OP.mult))
            for j in range(1, 4):
                pl.v(C("scalar_tensor_tensor", scr_a[:, :], zt[:, t0 + j:t0 + j + CH],
                      ck_cols[:, j * 17 + 8 + g:j * 17 + 9 + g], scr_a[:, :],
                      OP.mult, OP.add))
            pl.s(C("activation", gate[g][:, :], scr_a[:, :], AF.Silu))
        for g in range(8):
            zt = zb[g]
            pl.v(C("tensor_scalar", scr_a[:, :], zt[:, t0:t0 + CH],
                  ck_cols[:, g:g + 1], None, OP.mult))
            for j in range(1, 4):
                pl.v(C("scalar_tensor_tensor", scr_a[:, :], zt[:, t0 + j:t0 + j + CH],
                      ck_cols[:, j * 17 + g:j * 17 + 1 + g], scr_a[:, :],
                      OP.mult, OP.add))
            pl.v(C("tensor_scalar", scr_b[:, :], scr_a[:, :],
                  theta_col[:, g:g + 1], None, OP.mult))
            pl.v(C("tensor_scalar", scr_a[:, :], scr_b[:, :], INV_2PI, MAGIC,
                  OP.mult, OP.add))
            pl.v(C("tensor_scalar", scr_a[:, :], scr_a[:, :], MAGIC, None,
                  OP.subtract))
            pl.v(C("scalar_tensor_tensor", scr_c[:, :], scr_a[:, :], -TWO_PI,
                  scr_b[:, :], OP.mult, OP.add))
            pl.s(C("activation", scr_c[:, :], scr_c[:, :], AF.Sin))
            pl.v(C("tensor_scalar", scr_a[:, :], scr_b[:, :], INV_2PI,
                  MAGIC + 0.25, OP.mult, OP.add))
            pl.v(C("tensor_scalar", scr_a[:, :], scr_a[:, :], MAGIC, None,
                  OP.subtract))
            pl.v(C("scalar_tensor_tensor", scr_a[:, :], scr_a[:, :], -TWO_PI,
                  scr_b[:, :], OP.mult, OP.add))
            pl.v(C("tensor_scalar", scr_a[:, :], scr_a[:, :], math.pi / 2.0,
                  None, OP.add))
            pl.s(C("activation", scr_a[:, :], scr_a[:, :], AF.Sin))
            pl.t(C("matmul", B["p_bc"][:, :], selpw[:, 128 * g:128 * (g + 1)],
                  pwb[:, :]))
            pl.v(C("tensor_mul", scr_a[:, :], scr_a[:, :], B["p_bc"][:, :]))
            pl.v(C("tensor_mul", scr_c[:, :], scr_c[:, :], B["p_bc"][:, :]))
            pl.v(C("tensor_tensor_scan", nre[g][:, :], scr_a[:, :],
                  B["zeros"][:, :], car_re[:, g:g + 1] if c > 0 else 0.0,
                  OP.add, OP.add))
            pl.v(C("tensor_tensor_scan", nim[g][:, :], scr_c[:, :],
                  B["zeros"][:, :], car_im[:, g:g + 1] if c > 0 else 0.0,
                  OP.add, OP.add))
            pl.v(C("tensor_copy", car_re[:, g:g + 1], nre[g][:, CH - 1:CH]),
                 C("tensor_copy", car_im[:, g:g + 1], nim[g][:, CH - 1:CH]))
            pl.s(C("activation", sqb[:, :], nre[g][:, :], AF.Square))
            pl.t(C("matmul", sq_ps[:, :], selT[:, 32 * g:32 * (g + 1)], sqb[:, :],
                  start=(g == 0), stop=False))
            pl.s(C("activation", sqb[:, :], nim[g][:, :], AF.Square))
            pl.t(C("matmul", sq_ps[:, :], selT[:, 32 * g:32 * (g + 1)], sqb[:, :],
                  start=False, stop=(g == 7)))
        pl.v(C("tensor_scalar_max", scr_a[0:32, :], den[:, :], 1e-4))
        pl.v(C("reciprocal", scr_a[0:32, :], scr_a[0:32, :]))
        pl.s(C("activation", scr_b[0:32, :], scr_a[0:32, :], AF.Square))
        pl.v(C("tensor_mul", scr_b[0:32, :], scr_b[0:32, :], sq_ps[:, :]))
        pl.v(C("tensor_scalar", scr_b[0:32, :], scr_b[0:32, :], 1.0 / 64.0,
              1e-5, OP.mult, OP.add))
        pl.s(C("activation", scr_b[0:32, :], scr_b[0:32, :], AF.Sqrt))
        pl.v(C("reciprocal", scr_b[0:32, :], scr_b[0:32, :]))
        pl.v(C("tensor_mul", scr_b[0:32, :], scr_a[0:32, :], scr_b[0:32, :]))
        pl.v(C("tensor_copy", cscb[:, :], scr_b[0:32, :]))
        for g in range(8):
            pl.t(C("matmul", B["p_bc"][:, :], selpw[:, 128 * g:128 * (g + 1)],
                  cscb[:, :]))
            pl.v(C("scalar_tensor_tensor", sqb[:, :], nre[g][:, :],
                  nsre_col[:, 0:1], B["p_bc"][:, :], OP.mult, OP.mult))
            pl.t(C("matmul", y_ps[:, :], blk_re[:, :], sqb[:, :],
                  start=True, stop=False))
            pl.v(C("scalar_tensor_tensor", sqb[:, :], nim[g][:, :],
                  nsim_col[:, 0:1], B["p_bc"][:, :], OP.mult, OP.mult))
            pl.t(C("matmul", y_ps[:, :], blk_im[:, :], sqb[:, :],
                  start=False, stop=True))
            pl.v(C("tensor_mul", gy[g][:, t0:t0 + CH], gate[g][:, :],
                  y_ps[:, :]))

    wo = dr["L0_attn_W_out"]
    for c in range(NCH):
        for g in range(8):
            pl.ld(*[C("dma_start", out=wt32[:, 128 * k:128 * (k + 1)],
                      in_=wo[128 * k:128 * (k + 1), 128 * g:128 * (g + 1)])
                    for k in range(8)])
            pl.v(C("tensor_copy", wtb[:, :], wt32[:, :]))
            pl.t(*[C("matmul", P[g % 4][:, :], wtb[:, 128 * k:128 * (k + 1)],
                     gy[k][:, CH * c:CH * (c + 1)],
                     start=(k == 0), stop=(k == 7)) for k in range(8)])
            pl.ld(C("dma_start", out=scr_a[:, :],
                    in_=xin[128 * g:128 * (g + 1), CH * c:CH * (c + 1)]))
            pl.v(C("tensor_add", B["xsum"][g][:, :], scr_a[:, :], P[g % 4][:, :]))
        def wrt(g, src, c=c):
            return [("st", C("dma_start",
                             out=xout[128 * g:128 * (g + 1), CH * c:CH * (c + 1)],
                             in_=src[:, :]))]
        ln_chunk(pl, B, wrt)

    emit_plan(nc, chain, state, pl)
    es.close()


def emit_l0_ffn(nc, chain, state, B, dr, aux, xin, xout, xwork):
    HID = 2730
    P = B["p_main"]
    scr_a, scr_b, scr_c = B["scr_a"], B["scr_b"], B["scr_c"]

    pl = PL()
    for g in range(8):
        for c in range(NCH):
            pl.ld(C("dma_start", out=scr_a[:, :],
                    in_=xin[128 * g:128 * (g + 1), CH * c:CH * (c + 1)]))
            pl.st(C("dma_start", out=xwork[128 * g:128 * (g + 1), CH * c:CH * (c + 1)],
                    in_=scr_a[:, :]))
    emit_plan(nc, chain, state, pl)

    for lyr in range(2):
        es1 = ExitStack()
        sb1 = lambda name, shape, dt: es1.enter_context(nc.sbuf_tensor(f"{name}_{lyr}", shape, dt))
        vtok = [sb1(f"f_vt{t}", [128, 512], BF16) for t in range(16)]
        q2b = [sb1(f"f_q2b{g}", [128, L], BF16) for g in range(4)]
        k2b = [sb1(f"f_k2b{g}", [128, 4096], BF16) for g in range(4)]
        wt32 = sb1("f_wt32", [128, 512], F32)
        wtb = sb1("f_wtb", [128, 512], BF16)
        qs_col = sb1("f_qs", [128, 1], F32)
        qs_rot = sb1("f_qsr", [128, 1], F32)
        ks_col = sb1("f_ks", [128, 1], F32)
        ks_rot = sb1("f_ksr", [128, 1], F32)
        selB = sb1("f_selB", [2, 128], BF16)
        selH = sb1("f_selH", [128, 2], BF16)
        es1a = ExitStack()
        sb1a = lambda name, shape, dt: es1a.enter_context(nc.sbuf_tensor(f"{name}_{lyr}", shape, dt))
        qkb = [sb1a(f"f_qkb{g}", [128, L], BF16) for g in range(8)]
        rin = [sb1a(f"f_rin{i}", [2, L], BF16) for i in range(8)]
        es1a1 = ExitStack()
        shb = [es1a1.enter_context(nc.sbuf_tensor(f"f_shb{g}_{lyr}", [128, L], BF16))
               for g in range(8)]

        pl = PL()
        pl.ld(C("dma_start", out=selB[:, :], in_=aux["selB"][:, :]),
              C("dma_start", out=selH[:, :], in_=aux["selH"][:, :]))
        qsd = dr["L0_ffn_q_scale"]
        ksd = dr["L0_ffn_k_scale"]
        sc_calls = []
        for hl in range(2):
            sc_calls.append(C("dma_start", out=qs_col[64 * hl:64 * hl + 64, 0:1],
                              in_=qsd[lyr, :, :]))
            sc_calls.append(C("dma_start", out=ks_col[64 * hl:64 * hl + 64, 0:1],
                              in_=ksd[lyr, :, :]))
            for hf in range(2):
                sc_calls.append(C("dma_start",
                                  out=qs_rot[64 * hl + 32 * hf:64 * hl + 32 * hf + 32, 0:1],
                                  in_=qsd[lyr, 32 - 32 * hf:64 - 32 * hf, :]))
                sc_calls.append(C("dma_start",
                                  out=ks_rot[64 * hl + 32 * hf:64 * hl + 32 * hf + 32, 0:1],
                                  in_=ksd[lyr, 32 - 32 * hf:64 - 32 * hf, :]))
        pl.ld(*sc_calls)

        for g in range(8):
            for c in range(NCH):
                pl.ld(C("dma_start", out=scr_a[:, :],
                        in_=xwork[128 * g:128 * (g + 1), CH * c:CH * (c + 1)]))
                if g >= 4:
                    if c == 0:
                        pl.v(C("memset", shb[g][:, 0:1], 0.0))
                    else:
                        pl.ld(C("dma_start", out=scr_b[:, 0:1],
                                in_=xwork[128 * g:128 * (g + 1), CH * c - 1:CH * c],
                                allow_slow_non_contiguous=True))
                        pl.v(C("tensor_copy", shb[g][:, CH * c:CH * c + 1],
                              scr_b[:, 0:1]))
                    pl.v(C("tensor_copy", shb[g][:, CH * c + 1:CH * (c + 1)],
                          scr_a[:, 0:CH - 1]))
                else:
                    pl.v(C("tensor_copy", shb[g][:, CH * c:CH * (c + 1)],
                          scr_a[:, :]))

        wq = dr["L0_ffn_wqkv"]
        for m in range(8):
            for k in range(8):
                pl.ld(C("dma_start", out=wt32[:, 0:128],
                        in_=wq[lyr, 128 * k:128 * (k + 1), 128 * m:128 * (m + 1)]))
                pl.v(C("tensor_copy", wtb[:, 0:128], wt32[:, 0:128]))
                pl.t(*[C("matmul", P[c][:, :], wtb[:, 0:128],
                         shb[k][:, CH * c:CH * (c + 1)],
                         start=(k == 0), stop=(k == 7)) for c in range(NCH)])
            for c in range(NCH):
                pl.v(C("tensor_copy", qkb[m][:, CH * c:CH * (c + 1)], P[c][:, :]))
                pl.s(C("activation", B["scr_bf"][:, :], P[c][:, :], AF.Square))
                pl.t(C("matmul", B["p_row2"][0:2, :], selH[:, :], B["scr_bf"][:, :]))
                pl.s(C("activation", scr_a[0:2, :], B["p_row2"][0:2, :], AF.Sqrt))
                pl.v(C("tensor_scalar_max", scr_a[0:2, :], scr_a[0:2, :], 1e-12))
                pl.v(C("reciprocal", scr_a[0:2, :], scr_a[0:2, :]))
                pl.v(C("tensor_copy", rin[m][:, CH * c:CH * (c + 1)],
                      scr_a[0:2, :]))
        for tg in range(4):
            for k in range(8):
                pl.ld(C("dma_start", out=wt32[:, 0:512],
                        in_=wq[lyr, 128 * k:128 * (k + 1), 1024:1536]))
                pl.v(C("tensor_copy", wtb[:, 0:512], wt32[:, 0:512]))
                pl.t(*[C("matmul", P[t % 4][:, :],
                         shb[k][:, 128 * t:128 * (t + 1)], wtb[:, 0:512],
                         start=(k == 0), stop=(k == 7))
                       for t in range(4 * tg, 4 * tg + 4)])
            pl.v(*[C("tensor_copy", vtok[t][:, :], P[t % 4][:, :])
                   for t in range(4 * tg, 4 * tg + 4)])
        emit_plan(nc, chain, state, pl)
        es1a1.close()
        es1a2 = ExitStack()
        sb1a2 = lambda name, shape, dt: es1a2.enter_context(nc.sbuf_tensor(f"{name}_{lyr}", shape, dt))
        prot = sb1a2("f_prot", [128, 128], BF16)
        qcosb = sb1a2("f_qcos", [128, 4 * CH], BF16)
        qsinb = sb1a2("f_qsin", [128, 4 * CH], BF16)
        kcosb = sb1a2("f_kcos", [128, 4096], BF16)
        ksinb = sb1a2("f_ksin", [128, 4096], BF16)
        pl = PL()
        pl.ld(C("dma_start", out=prot[:, :], in_=aux["prot"][:, :]),
              C("dma_start", out=qcosb[:, :], in_=aux["qcos"][:, :]),
              C("dma_start", out=qsinb[:, :], in_=aux["qsin"][:, :]),
              C("dma_start", out=kcosb[:, :], in_=aux["kcos"][:, :]),
              C("dma_start", out=ksinb[:, :], in_=aux["ksin"][:, :]))
        for g in range(4):
            for w in range(4):
                cw = CH * w
                pl.t(C("matmul", B["p_bc"][:, :], prot[:, :], qkb[g][:, cw:cw + CH]))
                pl.t(C("matmul", B["p_bc2"][:, :], selB[:, :],
                      rin[g][:, cw:cw + CH]))
                pl.v(C("scalar_tensor_tensor", scr_a[:, :], qkb[g][:, cw:cw + CH],
                      qs_col[:, 0:1], qcosb[:, cw:cw + CH], OP.mult, OP.mult))
                pl.v(C("scalar_tensor_tensor", scr_b[:, :], B["p_bc"][:, :],
                      qs_rot[:, 0:1], qsinb[:, cw:cw + CH], OP.mult, OP.mult))
                pl.v(C("tensor_add", scr_a[:, :], scr_a[:, :], scr_b[:, :]))
                pl.v(C("tensor_mul", q2b[g][:, cw:cw + CH], scr_a[:, :],
                      B["p_bc2"][:, :]))
                for hf in range(2):
                    j0 = cw - CH + hf * CH
                    if j0 < 0:
                        continue
                    kc = 1024 * w + CH * hf
                    pl.t(C("matmul", B["p_bc"][:, :], prot[:, :],
                          qkb[4 + g][:, j0:j0 + CH]))
                    pl.t(C("matmul", B["p_bc2"][:, :], selB[:, :],
                          rin[4 + g][:, j0:j0 + CH]))
                    pl.v(C("scalar_tensor_tensor", scr_a[:, :],
                          qkb[4 + g][:, j0:j0 + CH], ks_col[:, 0:1],
                          kcosb[:, kc:kc + CH], OP.mult, OP.mult))
                    pl.v(C("scalar_tensor_tensor", scr_b[:, :], B["p_bc"][:, :],
                          ks_rot[:, 0:1], ksinb[:, kc:kc + CH], OP.mult, OP.mult))
                    pl.v(C("tensor_add", scr_a[:, :], scr_a[:, :], scr_b[:, :]))
                    pl.v(C("tensor_mul", k2b[g][:, kc:kc + CH], scr_a[:, :],
                          B["p_bc2"][:, :]))
        emit_plan(nc, chain, state, pl)
        es1a2.close()
        es1a.close()
        es1b = ExitStack()
        sb1b = lambda name, shape, dt: es1b.enter_context(nc.sbuf_tensor(f"{name}_{lyr}", shape, dt))
        Pb = sb1b("f_Pb", [128, 8 * CH], BF16)
        maskm = [sb1b(f"f_mk{j}", [128, CH], F32) for j in range(8)]
        nrow = sb1b("f_nrow", [1, 1024], F32)
        oT = [sb1b(f"f_oT{g}", [128, L], BF16) for g in range(4)]
        pl = PL()
        pl.ld(*[C("dma_start", out=maskm[j][:, :], in_=aux["fmask"][j, :, :])
                for j in range(8)])
        for h in range(8):
            g = h // 2
            r0 = 64 * (h % 2)
            for w in range(4):
                cw = CH * w
                jcs = list(range(8)) if w > 0 else list(range(4, 8))
                pl.s(C("activation", B["scr_bf"][0:64, :],
                      q2b[g][r0:r0 + 64, cw:cw + CH], AF.Square))
                pl.t(C("matmul", B["p_row"][0:1, :], B["ones_col_b"][0:64, :],
                      B["scr_bf"][0:64, :]))
                for idx, jc in enumerate(jcs):
                    pl.s(C("activation", B["scr_bf"][0:64, 0:128],
                          k2b[g][r0:r0 + 64, 1024 * w + 128 * jc:1024 * w + 128 * jc + 128],
                          AF.Square))
                    pl.t(C("matmul", B["p_row2"][0:1, 0:128],
                          B["ones_col_b"][0:64, :], B["scr_bf"][0:64, 0:128]))
                    pl.v(C("tensor_copy", nrow[0:1, 128 * idx:128 * (idx + 1)],
                          B["p_row2"][0:1, 0:128]))
                nj = len(jcs) * 128
                pl.v(C("reduce_max", scr_c[0:1, 0:1], nrow[0:1, 0:nj], AX.X))
                pl.v(C("tensor_scalar", scr_b[0:1, :], B["p_row"][0:1, :],
                      scr_c[0:1, 0:1], None, OP.mult))
                pl.s(C("activation", scr_b[0:1, :], scr_b[0:1, :], AF.Sqrt))
                pl.v(C("tensor_scalar_mul", scr_b[0:1, :], scr_b[0:1, :], -1.0))
                pl.v(C("tensor_copy", B["row_bf_a"][:, :], scr_b[0:1, :]))
                pl.t(C("matmul", B["p_bc2"][:, :], B["ones_row_b"][:, :],
                      B["row_bf_a"][:, :]))
                for jc in jcs:
                    pl.t(C("matmul", P[jc % 4][:, :],
                          k2b[g][r0:r0 + 64, 1024 * w + 128 * jc:1024 * w + 128 * jc + 128],
                          q2b[g][r0:r0 + 64, cw:cw + CH]))
                    pl.v(C("tensor_add", scr_a[:, :], maskm[jc][:, :],
                          B["p_bc2"][:, :]))
                    pl.v(C("tensor_add", scr_a[:, :], scr_a[:, :],
                          P[jc % 4][:, :]))
                    pl.s(C("activation", Pb[:, CH * jc:CH * (jc + 1)],
                          scr_a[:, :], AF.Exp))
                pl.t(*[C("matmul", B["p_row"][0:1, :], B["ones_col_b"][:, :],
                         Pb[:, CH * jc:CH * (jc + 1)],
                         start=(i == 0), stop=(i == len(jcs) - 1))
                       for i, jc in enumerate(jcs)])
                pl.v(C("reciprocal", scr_c[0:1, :], B["p_row"][0:1, :]))
                pl.v(C("tensor_copy", B["row_bf_b"][:, :], scr_c[0:1, :]))
                pl.t(*[C("matmul", B["p_bc"][0:64, :],
                         vtok[4 * (w - 1) + jc][:, 64 * h:64 * (h + 1)],
                         Pb[:, CH * jc:CH * (jc + 1)],
                         start=(i == 0), stop=(i == len(jcs) - 1))
                       for i, jc in enumerate(jcs)])
                pl.t(C("matmul", B["p_bc2"][0:64, :], B["ones_row_b"][:, 0:64],
                      B["row_bf_b"][:, :]))
                pl.v(C("tensor_copy", scr_a[0:64, :], B["p_bc"][0:64, :]))
                pl.v(C("tensor_mul", oT[g][r0:r0 + 64, cw:cw + CH],
                      scr_a[0:64, :], B["p_bc2"][0:64, :]))
        wod = dr["L0_ffn_wo"]
        for m in range(8):
            for k in range(4):
                pl.ld(C("dma_start", out=wt32[:, 0:128],
                        in_=wod[lyr, 128 * k:128 * (k + 1), 128 * m:128 * (m + 1)]))
                pl.v(C("tensor_copy", wtb[:, 0:128], wt32[:, 0:128]))
                pl.t(*[C("matmul", P[c][:, :], wtb[:, 0:128],
                         oT[k][:, CH * c:CH * (c + 1)],
                         start=(k == 0), stop=(k == 3)) for c in range(NCH)])
            for c in range(NCH):
                pl.ld(C("dma_start", out=scr_a[:, :],
                        in_=xwork[128 * m:128 * (m + 1), CH * c:CH * (c + 1)]))
                pl.v(C("tensor_add", scr_b[:, :], scr_a[:, :], P[c][:, :]))
                pl.st(C("dma_start",
                        out=xwork[128 * m:128 * (m + 1), CH * c:CH * (c + 1)],
                        in_=scr_b[:, :]))
        emit_plan(nc, chain, state, pl)
        es1b.close()
        es1.close()

        es2 = ExitStack()
        sb2 = lambda name, shape, dt: es2.enter_context(nc.sbuf_tensor(f"{name}_{lyr}", shape, dt))
        hnb = [sb2(f"f_hn{g}", [128, L], BF16) for g in range(8)]
        fb = [sb2(f"f_fb{m}", [128, L], BF16) for m in range(22)]
        wt32 = sb2("f_wt32b", [128, 128], F32)
        wtb = sb2("f_wtbb", [128, 128], BF16)
        lg_col = sb2("f_lg", [128, 8], F32)
        lb_col = sb2("f_lb", [128, 8], F32)
        b2_col = sb2("f_b2", [128, 8], F32)
        b1a_col = sb2("f_b1a", [128, 22], F32)
        b1g_col = sb2("f_b1g", [128, 22], F32)
        pl = PL()
        pl.ld(*([C("dma_start", out=lg_col[:, g:g + 1],
                   in_=dr["L0_ffn_ln_g"][lyr, 128 * g:128 * (g + 1), :])
                 for g in range(8)] +
                [C("dma_start", out=lb_col[:, g:g + 1],
                   in_=dr["L0_ffn_ln_b"][lyr, 128 * g:128 * (g + 1), :])
                 for g in range(8)] +
                [C("dma_start", out=b2_col[:, g:g + 1],
                   in_=dr["L0_ffn_b2"][lyr, 128 * g:128 * (g + 1), :])
                 for g in range(8)]))
        MT = [(m, min(128, HID - 128 * m)) for m in range(22)]
        pl.ld(*([C("dma_start", out=b1a_col[0:mm, m:m + 1],
                   in_=dr["L0_ffn_b1"][lyr, 128 * m:128 * m + mm, :])
                 for m, mm in MT] +
                [C("dma_start", out=b1g_col[0:mm, m:m + 1],
                   in_=dr["L0_ffn_b1"][lyr, HID + 128 * m:HID + 128 * m + mm, :])
                 for m, mm in MT]))

        for c in range(NCH):
            for g in range(8):
                pl.ld(C("dma_start", out=scr_a[:, :],
                        in_=xwork[128 * g:128 * (g + 1), CH * c:CH * (c + 1)]))
                if g >= 4:
                    if c == 0:
                        pl.v(C("memset", B["xsum"][g][:, 0:1], 0.0))
                    else:
                        pl.ld(C("dma_start", out=scr_b[:, 0:1],
                                in_=xwork[128 * g:128 * (g + 1), CH * c - 1:CH * c],
                                allow_slow_non_contiguous=True))
                        pl.v(C("tensor_copy", B["xsum"][g][:, 0:1], scr_b[:, 0:1]))
                    pl.v(C("tensor_copy", B["xsum"][g][:, 1:CH], scr_a[:, 0:CH - 1]))
                else:
                    pl.v(C("tensor_copy", B["xsum"][g][:, :], scr_a[:, :]))
            def wr_hn(g, src, c=c):
                return [("v", C("tensor_scalar", hnb[g][:, CH * c:CH * (c + 1)],
                                src[:, :], lg_col[:, g:g + 1], lb_col[:, g:g + 1],
                                OP.mult, OP.add))]
            ln_chunk(pl, B, wr_hn)

        w1d = dr["L0_ffn_w1"]
        for m, mm in MT:
            for k in range(8):
                pl.ld(C("dma_start", out=wt32[0:128, 0:mm] if mm < 128 else wt32[:, :],
                        in_=w1d[lyr, 128 * k:128 * (k + 1),
                                HID + 128 * m:HID + 128 * m + mm]))
                pl.v(C("tensor_copy", wtb[:, 0:mm], wt32[:, 0:mm]))
                pl.t(*[C("matmul", P[c][0:mm, :], wtb[:, 0:mm],
                         hnb[k][:, CH * c:CH * (c + 1)],
                         start=(k == 0), stop=(k == 7)) for c in range(NCH)])
            for c in range(NCH):
                pl.v(C("tensor_scalar", scr_a[0:mm, :], P[c][0:mm, :], 1.0,
                      b1g_col[0:mm, m:m + 1], OP.mult, OP.add))
                pl.s(C("activation", scr_a[0:mm, :], scr_a[0:mm, :], AF.Gelu))
                pl.v(C("tensor_copy", fb[m][0:mm, CH * c:CH * (c + 1)],
                      scr_a[0:mm, :]))
        for m, mm in MT:
            for k in range(8):
                pl.ld(C("dma_start", out=wt32[0:128, 0:mm] if mm < 128 else wt32[:, :],
                        in_=w1d[lyr, 128 * k:128 * (k + 1), 128 * m:128 * m + mm]))
                pl.v(C("tensor_copy", wtb[:, 0:mm], wt32[:, 0:mm]))
                pl.t(*[C("matmul", P[c][0:mm, :], wtb[:, 0:mm],
                         hnb[k][:, CH * c:CH * (c + 1)],
                         start=(k == 0), stop=(k == 7)) for c in range(NCH)])
            for c in range(NCH):
                pl.v(C("tensor_scalar", scr_a[0:mm, :], P[c][0:mm, :], 1.0,
                      b1a_col[0:mm, m:m + 1], OP.mult, OP.add))
                pl.v(C("tensor_mul", scr_b[0:mm, :], scr_a[0:mm, :],
                      fb[m][0:mm, CH * c:CH * (c + 1)]))
                pl.v(C("tensor_copy", fb[m][0:mm, CH * c:CH * (c + 1)],
                      scr_b[0:mm, :]))
        w2d = dr["L0_ffn_w2"]
        for dt_ in range(8):
            for ki, (m, mm) in enumerate(MT):
                pl.ld(C("dma_start", out=wt32[0:mm, 0:128],
                        in_=w2d[lyr, 128 * m:128 * m + mm,
                                128 * dt_:128 * (dt_ + 1)]))
                pl.v(C("tensor_copy", wtb[0:mm, 0:128], wt32[0:mm, 0:128]))
                pl.t(*[C("matmul", P[c][:, :], wtb[0:mm, 0:128],
                         fb[m][0:mm, CH * c:CH * (c + 1)],
                         start=(ki == 0), stop=(ki == 21)) for c in range(NCH)])
            for c in range(NCH):
                pl.ld(C("dma_start", out=scr_a[:, :],
                        in_=xwork[128 * dt_:128 * (dt_ + 1), CH * c:CH * (c + 1)]))
                pl.v(C("tensor_scalar", scr_b[:, :], P[c][:, :], 1.0,
                      b2_col[:, dt_:dt_ + 1], OP.mult, OP.add))
                pl.v(C("tensor_add", scr_b[:, :], scr_b[:, :], scr_a[:, :]))
                pl.st(C("dma_start",
                        out=xwork[128 * dt_:128 * (dt_ + 1), CH * c:CH * (c + 1)],
                        in_=scr_b[:, :]))
        emit_plan(nc, chain, state, pl)
        es2.close()

    pl = PL()
    for c in range(NCH):
        for g in range(8):
            pl.ld(C("dma_start", out=scr_a[:, :],
                    in_=xin[128 * g:128 * (g + 1), CH * c:CH * (c + 1)]))
            pl.ld(C("dma_start", out=scr_b[:, :],
                    in_=xwork[128 * g:128 * (g + 1), CH * c:CH * (c + 1)]))
            pl.v(C("tensor_add", B["xsum"][g][:, :], scr_a[:, :], scr_b[:, :]))
        def wrt(g, src, c=c):
            return [("st", C("dma_start",
                             out=xout[128 * g:128 * (g + 1), CH * c:CH * (c + 1)],
                             in_=src[:, :]))]
        ln_chunk(pl, B, wrt)
    emit_plan(nc, chain, state, pl)


def emit_l1_attn(nc, chain, state, B, dr, aux, xin, xout):
    es = ExitStack()
    sb = lambda name, shape, dt: es.enter_context(nc.sbuf_tensor(name, shape, dt))
    xTb = [sb(f"b_xtb{g}", [128, L], BF16) for g in range(8)]
    qb = [sb(f"b_qb{g}", [128, L], BF16) for g in range(8)]
    kb = [sb(f"b_kb{g}", [128, L], BF16) for g in range(8)]
    vtok = [sb(f"b_vt{t}", [128, 1024], BF16) for t in range(16)]
    Pb = sb("b_Pb", [128, 16 * CH], BF16)
    wt32 = sb("b_wt32", [128, 1024], F32)
    wtb = sb("b_wtb", [128, 1024], BF16)
    dtp = sb("b_dtp", [128, CH], F32)
    dtm = [sb(f"b_dtm{v}", [128, CH], F32) for v in range(4)]
    nrow = sb("b_nrow", [1, L], F32)
    mk2 = sb("b_mk2", [1, 16], F32)
    oT = xTb

    P = B["p_main"]
    scr_a, scr_b, scr_c = B["scr_a"], B["scr_b"], B["scr_c"]
    slopes = [2.0 ** (-0.5 * (i + 1)) for i in range(16)]

    pl = PL()
    pl.ld(C("dma_start", out=dtp[:, :], in_=aux["dt1"][:, :]))
    pl.ld(*[C("dma_start", out=dtm[v][:, :], in_=aux["dtm1"][v, :, :])
            for v in range(4)])
    for g in range(8):
        for c in range(NCH):
            pl.ld(C("dma_start", out=scr_a[:, :],
                    in_=xin[128 * g:128 * (g + 1), CH * c:CH * (c + 1)]))
            pl.v(C("tensor_copy", xTb[g][:, CH * c:CH * (c + 1)], scr_a[:, :]))
    wqk = dr["L1_attn_W_qkv"]
    for part, dst, fold in ((0, qb, 0.125), (1, kb, 1.0)):
        for m in range(8):
            for k in range(8):
                pl.ld(C("dma_start", out=wt32[:, 0:128],
                        in_=wqk[128 * k:128 * (k + 1),
                                1024 * part + 128 * m:1024 * part + 128 * (m + 1)]))
                pl.v(C("tensor_copy", wtb[:, 0:128], wt32[:, 0:128]))
                pl.t(*[C("matmul", P[c][:, :], wtb[:, 0:128],
                         xTb[k][:, CH * c:CH * (c + 1)],
                         start=(k == 0), stop=(k == 7)) for c in range(NCH)])
            for c in range(NCH):
                pl.v(C("tensor_scalar", dst[m][:, CH * c:CH * (c + 1)],
                      P[c][:, :], fold, None, OP.mult))
    for tg in range(4):
        for hf in range(2):
            for k in range(8):
                pl.ld(C("dma_start", out=wt32[:, 0:512],
                        in_=wqk[128 * k:128 * (k + 1),
                                2048 + 512 * hf:2048 + 512 * (hf + 1)]))
                pl.v(C("tensor_copy", wtb[:, 0:512], wt32[:, 0:512]))
                pl.t(*[C("matmul", P[t % 4][:, :],
                         xTb[k][:, 128 * t:128 * (t + 1)],
                         wtb[:, 0:512],
                         start=(k == 0), stop=(k == 7))
                       for t in range(4 * tg, 4 * tg + 4)])
            pl.v(*[C("tensor_copy", vtok[t][:, 512 * hf:512 * (hf + 1)],
                     P[t % 4][:, :]) for t in range(4 * tg, 4 * tg + 4)])
    emit_plan(nc, chain, state, pl)
    pl = PL()

    for h in range(16):
        g, r0 = h // 2, 64 * (h % 2)
        for c in range(NCH):
            pl.s(C("activation", B["scr_bf"][0:64, :],
                  kb[g][r0:r0 + 64, CH * c:CH * (c + 1)], AF.Square))
            pl.t(C("matmul", B["p_row"][0:1, :], B["ones_col_b"][0:64, :],
                  B["scr_bf"][0:64, :]))
            pl.v(C("tensor_copy", nrow[0:1, CH * c:CH * (c + 1)],
                  B["p_row"][0:1, :]))
        pl.v(C("reduce_max", mk2[0:1, h:h + 1], nrow[0:1, :], AX.X))

    for h in range(16):
        g, r0 = h // 2, 64 * (h % 2)
        sl = slopes[h]
        for ic in range(NCH):
            c0 = CH * ic
            jts = list(range(4 * ic, 16))
            pl.s(C("activation", B["scr_bf"][0:64, :],
                  qb[g][r0:r0 + 64, c0:c0 + CH], AF.Square))
            pl.t(C("matmul", B["p_row"][0:1, :], B["ones_col_b"][0:64, :],
                  B["scr_bf"][0:64, :]))
            pl.v(C("tensor_scalar", scr_b[0:1, :], B["p_row"][0:1, :],
                  mk2[0:1, h:h + 1], None, OP.mult))
            pl.s(C("activation", scr_b[0:1, :], scr_b[0:1, :], AF.Sqrt))
            pl.v(C("tensor_scalar_mul", scr_b[0:1, :], scr_b[0:1, :], -1.0))
            pl.v(C("tensor_copy", B["row_bf_a"][:, :], scr_b[0:1, :]))
            pl.t(C("matmul", B["p_bc2"][:, :], B["ones_row_b"][:, :],
                  B["row_bf_a"][:, :]))
            for jt in jts:
                v = jt - 4 * ic
                pl.t(C("matmul", P[jt % 4][:, :],
                      kb[g][r0:r0 + 64, 128 * jt:128 * (jt + 1)],
                      qb[g][r0:r0 + 64, c0:c0 + CH]))
                if v < 4:
                    pl.v(C("tensor_scalar", scr_a[:, :], dtm[v][:, :], -sl,
                          None, OP.mult))
                else:
                    pl.v(C("tensor_scalar", scr_a[:, :], dtp[:, :],
                          float(128 * jt - 512 * ic), -sl, OP.add, OP.mult))
                pl.v(C("tensor_add", scr_a[:, :], scr_a[:, :], B["p_bc2"][:, :]))
                pl.v(C("tensor_add", scr_a[:, :], scr_a[:, :], P[jt % 4][:, :]))
                pl.s(C("activation", Pb[:, CH * jt:CH * (jt + 1)], scr_a[:, :],
                      AF.Exp))
            pl.t(*[C("matmul", B["p_row"][0:1, :], B["ones_col_b"][:, :],
                     Pb[:, CH * jt:CH * (jt + 1)],
                     start=(i == 0), stop=(i == len(jts) - 1))
                   for i, jt in enumerate(jts)])
            pl.v(C("reciprocal", scr_c[0:1, :], B["p_row"][0:1, :]))
            pl.v(C("tensor_copy", B["row_bf_b"][:, :], scr_c[0:1, :]))
            pl.t(*[C("matmul", B["p_bc"][0:64, :],
                     vtok[jt][:, 64 * h:64 * (h + 1)],
                     Pb[:, CH * jt:CH * (jt + 1)],
                     start=(i == 0), stop=(i == len(jts) - 1))
                   for i, jt in enumerate(jts)])
            pl.t(C("matmul", B["p_bc2"][0:64, :], B["ones_row_b"][:, 0:64],
                  B["row_bf_b"][:, :]))
            pl.v(C("tensor_copy", scr_a[0:64, :], B["p_bc"][0:64, :]))
            pl.v(C("tensor_mul", oT[g][r0:r0 + 64, c0:c0 + CH],
                  scr_a[0:64, :], B["p_bc2"][0:64, :]))

    wo = dr["L1_attn_W_out"]
    for c in range(NCH):
        for g in range(8):
            pl.ld(*[C("dma_start", out=wt32[:, 128 * k:128 * (k + 1)],
                      in_=wo[128 * k:128 * (k + 1), 128 * g:128 * (g + 1)])
                    for k in range(8)])
            pl.v(C("tensor_copy", wtb[:, :], wt32[:, :]))
            pl.t(*[C("matmul", P[g % 4][:, :], wtb[:, 128 * k:128 * (k + 1)],
                     oT[k][:, CH * c:CH * (c + 1)],
                     start=(k == 0), stop=(k == 7)) for k in range(8)])
            pl.ld(C("dma_start", out=scr_a[:, :],
                    in_=xin[128 * g:128 * (g + 1), CH * c:CH * (c + 1)]))
            pl.v(C("tensor_add", B["xsum"][g][:, :], scr_a[:, :], P[g % 4][:, :]))
        def wrt(g, src, c=c):
            return [("st", C("dma_start",
                             out=xout[128 * g:128 * (g + 1), CH * c:CH * (c + 1)],
                             in_=src[:, :]))]
        ln_chunk(pl, B, wrt)
    emit_plan(nc, chain, state, pl)
    es.close()


def emit_l1_ffn(nc, chain, state, B, dr, aux, xin, xout, xwork):
    es = ExitStack()
    sb = lambda name, shape, dt: es.enter_context(nc.sbuf_tensor(name, shape, dt))
    xTb = [sb(f"c_xtb{g}", [128, L], BF16) for g in range(8)]
    fb = [sb(f"c_fb{m}", [128, L], BF16) for m in range(32)]
    wt32 = sb("c_wt32", [128, 128], F32)
    absb = sb("c_absb", [128, 128], BF16)
    sg = [sb(f"c_sg{k}", [128, 128], BF16) for k in range(8)]
    al_row = sb("c_alrow", [1, 128], F32)
    al_col = sb("c_alcol", [128, 1], F32)
    b_col = sb("c_bcol", [128, 1], F32)
    xstage = sb("c_xstage", [128, L], F32)

    P = B["p_main"]
    scr_a, scr_b = B["scr_a"], B["scr_b"]
    pl = PL()
    for g in range(8):
        pl.ld(C("dma_start", out=xstage[:, :], in_=xin[128 * g:128 * (g + 1), :]))
        pl.v(C("tensor_copy", xTb[g][:, :], xstage[:, :]))

    w1t = dr["L1_ffn_w1T"]
    for m in range(32):
        for k in range(8):
            pl.ld(C("dma_start", out=wt32[:, :],
                    in_=w1t[128 * k:128 * (k + 1), 128 * m:128 * (m + 1)]))
            pl.s(C("activation", absb[:, :], wt32[:, :], AF.Abs))
            pl.t(C("matmul", B["p_row"][0:1, 0:128], B["ones_col_b"][:, :],
                  absb[:, :], start=(k == 0), stop=(k == 7)))
            pl.s(C("activation", sg[k][:, :], wt32[:, :], AF.Sign))
        pl.v(C("tensor_scalar", al_row[:, :], B["p_row"][0:1, 0:128],
              1.0 / 1024.0, None, OP.mult))
        pl.st(C("dma_start", out=al_col[:, 0:1], in_=al_row[0:1, :]))
        pl.ld(C("dma_start", out=b_col[:, 0:1],
                in_=dr["L1_ffn_b1"][128 * m:128 * (m + 1), :]))
        for k in range(8):
            pl.t(*[C("matmul", P[c][:, :], sg[k][:, :],
                     xTb[k][:, CH * c:CH * (c + 1)],
                     start=(k == 0), stop=(k == 7)) for c in range(NCH)])
        for c in range(NCH):
            pl.v(C("tensor_scalar", scr_a[:, :], P[c][:, :], al_col[:, 0:1],
                  b_col[:, 0:1], OP.mult, OP.add))
            pl.s(C("activation", fb[m][:, CH * c:CH * (c + 1)], scr_a[:, :],
                  AF.Gelu))

    w2t = dr["L1_ffn_w2T"]
    for dt_ in range(8):
        for m in range(32):
            pl.ld(C("dma_start", out=wt32[:, :],
                    in_=w2t[128 * m:128 * (m + 1), 128 * dt_:128 * (dt_ + 1)]))
            pl.s(C("activation", absb[:, :], wt32[:, :], AF.Abs))
            pl.t(C("matmul", B["p_row"][0:1, 0:128], B["ones_col_b"][:, :],
                  absb[:, :], start=(m == 0), stop=(m == 31)))
            pl.s(C("activation", absb[:, :], wt32[:, :], AF.Sign))
            pl.t(*[C("matmul", P[c][:, :], absb[:, :],
                     fb[m][:, CH * c:CH * (c + 1)],
                     start=(m == 0), stop=(m == 31)) for c in range(NCH)])
        pl.v(C("tensor_scalar", al_row[:, :], B["p_row"][0:1, 0:128],
              1.0 / 4096.0, None, OP.mult))
        pl.st(C("dma_start", out=al_col[:, 0:1], in_=al_row[0:1, :]))
        pl.ld(C("dma_start", out=b_col[:, 0:1],
                in_=dr["L1_ffn_b2"][128 * dt_:128 * (dt_ + 1), :]))
        for c in range(NCH):
            pl.v(C("tensor_scalar", scr_a[:, :], P[c][:, :], al_col[:, 0:1],
                  b_col[:, 0:1], OP.mult, OP.add))
            pl.st(C("dma_start",
                    out=xwork[128 * dt_:128 * (dt_ + 1), CH * c:CH * (c + 1)],
                    in_=scr_a[:, :]))
    for c in range(NCH):
        for g in range(8):
            pl.ld(C("dma_start", out=scr_a[:, :],
                    in_=xin[128 * g:128 * (g + 1), CH * c:CH * (c + 1)]))
            pl.ld(C("dma_start", out=scr_b[:, :],
                    in_=xwork[128 * g:128 * (g + 1), CH * c:CH * (c + 1)]))
            pl.v(C("tensor_add", B["xsum"][g][:, :], scr_a[:, :], scr_b[:, :]))
        def wrt(g, src, c=c):
            return [("st", C("dma_start",
                             out=xout[128 * g:128 * (g + 1), CH * c:CH * (c + 1)],
                             in_=src[:, :]))]
        ln_chunk(pl, B, wrt)
    emit_plan(nc, chain, state, pl)
    es.close()


def build_aux_arrays():
    A = {}
    t = np.arange(L, dtype=np.float64)
    dist = np.empty((32, L), np.float32)
    dist[:28] = (L - 1.0) - t
    dist[28:] = t
    A["dist"] = dist

    selpw = np.zeros((32, 1024), np.float32)
    for g in range(8):
        for col in range(128):
            selpw[4 * g + col // 32, 128 * g + col] = 1.0
    A["selpw"] = selpw.astype(BF)

    selT = np.zeros((128, 256), np.float32)
    for g in range(8):
        for p in range(128):
            selT[p, 32 * g + 4 * g + p // 32] = 1.0
    A["selT"] = selT.astype(BF)

    selH = np.zeros((128, 2), np.float32)
    selH[0:64, 0] = 1.0
    selH[64:128, 1] = 1.0
    A["selH"] = selH.astype(BF)
    selB = np.zeros((2, 128), np.float32)
    selB[0, 0:64] = 1.0
    selB[1, 64:128] = 1.0
    A["selB"] = selB.astype(BF)

    prot = np.zeros((128, 128), np.float32)
    for blk in range(2):
        o = 64 * blk
        for m in range(64):
            if m < 32:
                prot[o + m + 32, o + m] = -1.0
            else:
                prot[o + m - 32, o + m] = 1.0
    A["prot"] = prot.astype(BF)

    DIM_HEAD = 64
    WINDOW = 512
    half = np.arange(0, DIM_HEAD, 2, dtype=np.float64)
    inv_freq = 1.0 / (10000.0 ** (half / DIM_HEAD))
    base = (half + 0.4 * DIM_HEAD) / (1.4 * DIM_HEAD)
    dh = np.arange(DIM_HEAD)
    invf_d = inv_freq[dh % 32]
    base_d = base[dh % 32]
    i = np.arange(L, dtype=np.float64)
    freqs = i[None, :] * invf_d[:, None]
    cw = 512.0 * (i // 512)
    srel = base_d[:, None] ** ((i[None, :] - cw[None, :]) / 256.0)
    qcos = np.cos(freqs) * srel * 8.0
    qsin = np.sin(freqs) * srel * 8.0
    A["qcos"] = np.tile(qcos, (2, 1)).astype(BF)
    A["qsin"] = np.tile(qsin, (2, 1)).astype(BF)
    kcos = np.zeros((DIM_HEAD, 4096), np.float64)
    ksin = np.zeros((DIM_HEAD, 4096), np.float64)
    for w in range(4):
        for jcol in range(1024):
            j = 512 * (w - 1) + jcol
            if j < 0:
                continue
            fr = j * invf_d
            sr = base_d ** (-(j - 512.0 * w) / 256.0)
            kcos[:, 1024 * w + jcol] = np.cos(fr) * sr
            ksin[:, 1024 * w + jcol] = np.sin(fr) * sr
    A["kcos"] = np.tile(kcos, (2, 1)).astype(BF)
    A["ksin"] = np.tile(ksin, (2, 1)).astype(BF)

    fm = np.zeros((8, 128, CH), np.float32)
    r = np.arange(128)[:, None]
    cc = np.arange(CH)[None, :]
    for jc in range(8):
        d = 512 + cc - 128 * jc - r
        fm[jc] = np.where((d >= 0) & (d <= 512), 0.0, NEG)
    A["fmask"] = fm

    A["dt1"] = (r - cc).astype(np.float32) * np.ones((128, CH), np.float32)
    dtm = np.zeros((4, 128, CH), np.float32)
    for v in range(4):
        val = r - cc + 128 * v
        dtm[v] = np.where(val >= 0, val, 1.0e33).astype(np.float32)
    A["dtm1"] = dtm
    return A


AUX_SPECS = {
    "dist": ([32, L], F32), "selpw": ([32, 1024], BF16),
    "selT": ([128, 256], BF16), "selH": ([128, 2], BF16),
    "selB": ([2, 128], BF16), "prot": ([128, 128], BF16),
    "qcos": ([128, 4 * CH], BF16), "qsin": ([128, 4 * CH], BF16),
    "kcos": ([128, 4096], BF16), "ksin": ([128, 4096], BF16),
    "fmask": ([8, 128, CH], F32), "dt1": ([128, CH], F32),
    "dtm1": ([4, 128, CH], F32),
}

W_SPECS = {
    "L0_attn_W_in": [1024, 2080],
    "L0_attn_conv_kernel": [4, 2080, 1],
    "L0_attn_theta": [1024, 1],
    "L0_attn_decay_slopes": [28, 1],
    "L0_attn_anchor_slopes": [4, 1],
    "L0_attn_score_scale": [32, 1],
    "L0_attn_W_re": [32, 32],
    "L0_attn_W_im": [32, 32],
    "L0_attn_norm_scale": [64, 1],
    "L0_attn_W_out": [1024, 1024],
    "L0_ffn_wqkv": [2, 1024, 1536],
    "L0_ffn_q_scale": [2, 64, 1],
    "L0_ffn_k_scale": [2, 64, 1],
    "L0_ffn_wo": [2, 512, 1024],
    "L0_ffn_ln_g": [2, 1024, 1],
    "L0_ffn_ln_b": [2, 1024, 1],
    "L0_ffn_w1": [2, 1024, 5460],
    "L0_ffn_b1": [2, 5460, 1],
    "L0_ffn_w2": [2, 2730, 1024],
    "L0_ffn_b2": [2, 1024, 1],
    "L1_attn_W_qkv": [1024, 3072],
    "L1_attn_W_out": [1024, 1024],
    "L1_ffn_w1T": [1024, 4096],
    "L1_ffn_b1": [4096, 1],
    "L1_ffn_w2T": [4096, 1024],
    "L1_ffn_b2": [1024, 1],
}

MOD_WEIGHTS = {
    "a0": [n for n in W_SPECS if n.startswith("L0_attn")],
    "f0": [n for n in W_SPECS if n.startswith("L0_ffn")],
    "a1": ["L1_attn_W_qkv", "L1_attn_W_out"],
    "f1": ["L1_ffn_w1T", "L1_ffn_b1", "L1_ffn_w2T", "L1_ffn_b2"],
}
MOD_AUX = {
    "a0": ["dist", "selpw", "selT"],
    "f0": ["selB", "selH", "prot", "qcos", "qsin", "kcos", "ksin", "fmask"],
    "a1": ["dt1", "dtm1"],
    "f1": [],
}


def build_mod_nc(mod):
    nc = bass.Bass("TRN2", target_bir_lowering=False)
    dr = {}
    for n in MOD_WEIGHTS[mod]:
        dr[n] = nc.dram_tensor(n, W_SPECS[n], F32, kind="ExternalInput").ap()
    aux = {}
    for n in MOD_AUX[mod]:
        shape, dt = AUX_SPECS[n]
        aux[n] = nc.dram_tensor("aux_" + n, shape, dt, kind="ExternalInput").ap()
    xin = nc.dram_tensor("xin", [D, L], F32, kind="ExternalInput").ap()
    xout = nc.dram_tensor("xout", [D, L], F32, kind="ExternalOutput").ap()
    xwork = None
    if mod in ("f0", "f1"):
        xwork = nc.dram_tensor("xwork", [D, L], F32, kind="Internal").ap()
    es = ExitStack()
    chain = es.enter_context(nc.semaphore("chain"))
    state = {"val": 0}
    B = build_common(nc, es, chain, state)
    if mod == "a0":
        emit_l0_attn(nc, chain, state, B, dr, aux, xin, xout)
    elif mod == "f0":
        emit_l0_ffn(nc, chain, state, B, dr, aux, xin, xout, xwork)
    elif mod == "a1":
        emit_l1_attn(nc, chain, state, B, dr, aux, xin, xout)
    elif mod == "f1":
        emit_l1_ffn(nc, chain, state, B, dr, aux, xin, xout, xwork)
    tot = state["val"]
    with nc.Block() as blk:
        blk.tensor(lambda e: e.wait_ge(chain, tot))
        blk.vector(lambda e: e.wait_ge(chain, tot))
        blk.scalar(lambda e: e.wait_ge(chain, tot))
        blk.sync(lambda e: e.wait_ge(chain, tot))
        blk.gpsimd(lambda e: e.wait_ge(chain, tot))
    es.close()
    return nc


def kernel(**inputs):
    inp = {k: np.asarray(v) for k, v in inputs.items()}
    arrs = {}
    arrs["L0_attn_W_in"] = inp["L0_attn_W_in"]
    arrs["L0_attn_conv_kernel"] = np.ascontiguousarray(
        inp["L0_attn_conv_kernel"].reshape(4, 2080, 1))
    arrs["L0_attn_theta"] = np.ascontiguousarray(
        inp["L0_attn_theta"].reshape(1024, 1))
    arrs["L0_attn_decay_slopes"] = inp["L0_attn_decay_slopes"].reshape(28, 1)
    arrs["L0_attn_anchor_slopes"] = inp["L0_attn_anchor_slopes"].reshape(4, 1)
    arrs["L0_attn_score_scale"] = inp["L0_attn_score_scale"].reshape(32, 1)
    arrs["L0_attn_W_re"] = inp["L0_attn_W_re"]
    arrs["L0_attn_W_im"] = inp["L0_attn_W_im"]
    arrs["L0_attn_norm_scale"] = inp["L0_attn_norm_scale"].reshape(64, 1)
    arrs["L0_attn_W_out"] = inp["L0_attn_W_out"]
    arrs["L0_ffn_wqkv"] = inp["L0_ffn_wqkv"]
    arrs["L0_ffn_q_scale"] = inp["L0_ffn_q_scale"].reshape(2, 64, 1)
    arrs["L0_ffn_k_scale"] = inp["L0_ffn_k_scale"].reshape(2, 64, 1)
    arrs["L0_ffn_wo"] = inp["L0_ffn_wo"]
    arrs["L0_ffn_ln_g"] = inp["L0_ffn_ln_g"].reshape(2, 1024, 1)
    arrs["L0_ffn_ln_b"] = inp["L0_ffn_ln_b"].reshape(2, 1024, 1)
    arrs["L0_ffn_w1"] = inp["L0_ffn_w1"]
    arrs["L0_ffn_b1"] = inp["L0_ffn_b1"].reshape(2, 5460, 1)
    arrs["L0_ffn_w2"] = inp["L0_ffn_w2"]
    arrs["L0_ffn_b2"] = inp["L0_ffn_b2"].reshape(2, 1024, 1)
    arrs["L1_attn_W_qkv"] = inp["L1_attn_W_qkv"]
    arrs["L1_attn_W_out"] = inp["L1_attn_W_out"]
    arrs["L1_ffn_w1T"] = np.ascontiguousarray(inp["L1_ffn_w1"].T)
    arrs["L1_ffn_b1"] = inp["L1_ffn_b1"].reshape(4096, 1)
    arrs["L1_ffn_w2T"] = np.ascontiguousarray(inp["L1_ffn_w2"].T)
    arrs["L1_ffn_b2"] = inp["L1_ffn_b2"].reshape(1024, 1)
    arrs = {k: np.ascontiguousarray(v.astype(np.float32)) for k, v in arrs.items()}
    auxarr = build_aux_arrays()

    X = np.ascontiguousarray(
        inp["L0_attn_x"].astype(np.float32).transpose(0, 2, 1))

    mods = ["a0", "f0", "a1", "f1"]
    ncs = {m: build_mod_nc(m) for m in mods}
    cur = X
    for mod in mods:
        nxt = np.empty_like(cur)
        const = {n: arrs[n] for n in MOD_WEIGHTS[mod]}
        const.update({"aux_" + n: auxarr[n] for n in MOD_AUX[mod]})
        for s in range(4):
            in_maps = []
            for cid in range(8):
                d = dict(const)
                d["xin"] = np.ascontiguousarray(cur[cid * 4 + s])
                in_maps.append(d)
            res = run_bass_kernel_spmd(ncs[mod], in_maps, core_ids=list(range(8)))
            for cid in range(8):
                nxt[cid * 4 + s] = res.results[cid]["xout"]
        cur = nxt
    out = np.ascontiguousarray(cur.transpose(0, 2, 1)).astype(np.float32)
    return out


# version 02
# speedup vs baseline: 1.0061x; 1.0061x over previous
import math
import numpy as np
import ml_dtypes
from contextlib import ExitStack

import concourse.bass as bass
import concourse.mybir as mybir
from concourse.bass_utils import run_bass_kernel_spmd

F32 = mybir.dt.float32
BF16 = mybir.dt.bfloat16
AF = mybir.ActivationFunctionType
OP = mybir.AluOpType
AX = mybir.AxisListType
BF = ml_dtypes.bfloat16

L = 2048
D = 1024
CH = 512
NCH = 4
NEG = -1.0e30
TWO_PI = 2.0 * math.pi
INV_2PI = 1.0 / TWO_PI
MAGIC = 12582912.0


def C(method, *args, **kw):
    return (method, args, kw)


class PL:
    def __init__(self):
        self.steps = []

    def t(self, *calls):
        self.steps.append(("tensor", calls, 1))

    def v(self, *calls):
        self.steps.append(("vector", calls, 1))

    def s(self, *calls):
        self.steps.append(("scalar", calls, 1))

    def ld(self, *calls):
        self.steps.append(("sync", calls, 16 * len(calls)))

    def st(self, *calls):
        self.steps.append(("gpsimd", calls, 16 * len(calls)))


def emit_plan(nc, chain, state, pl):
    pres = []
    tot = state["val"]
    for eng, calls, inc in pl.steps:
        pres.append(tot)
        tot += inc
    state["val"] = tot
    with nc.Block() as blk:
        def mk(engname):
            def body(e):
                for i, (eng, calls, inc) in enumerate(pl.steps):
                    if eng != engname:
                        continue
                    if pres[i] > 0:
                        e.wait_ge(chain, pres[i])
                    insts = [getattr(e, m)(*a, **k) for (m, a, k) in calls]
                    if inc >= 16:
                        for ins in insts:
                            ins.then_inc(chain, 16)
                    else:
                        insts[-1].then_inc(chain, inc)
            return body
        blk.tensor(mk("tensor"))
        blk.vector(mk("vector"))
        blk.scalar(mk("scalar"))
        blk.sync(mk("sync"))
        blk.gpsimd(mk("gpsimd"))


def build_common(nc, es, chain, state):
    B = {}
    sb = lambda name, shape, dt: es.enter_context(nc.sbuf_tensor(name, shape, dt))
    ps = lambda name, shape: es.enter_context(nc.psum_tensor(name, shape, F32))
    B["ones_col_b"] = sb("g_ones_col", [128, 1], BF16)
    B["ones_row_b"] = sb("g_ones_row", [1, 128], BF16)
    B["zeros"] = sb("g_zeros", [128, CH], F32)
    B["scr_a"] = sb("g_scr_a", [128, CH], F32)
    B["scr_b"] = sb("g_scr_b", [128, CH], F32)
    B["scr_c"] = sb("g_scr_c", [128, CH], F32)
    B["scr_bf"] = sb("g_scr_bf", [128, CH], BF16)
    B["row_a"] = sb("g_row_a", [1, L], F32)
    B["row_bf_a"] = sb("g_row_bf_a", [1, CH], BF16)
    B["row_bf_b"] = sb("g_row_bf_b", [1, CH], BF16)
    B["xsum"] = [sb(f"g_xsum{g}", [128, CH], F32) for g in range(8)]
    B["p_main"] = [ps(f"g_pm{i}", [128, CH]) for i in range(4)]
    B["p_bc"] = ps("g_pbc", [128, CH])
    B["p_bc2"] = ps("g_pbc2", [128, CH])
    B["p_row"] = ps("g_prow", [32, CH])
    B["p_row2"] = ps("g_prow2", [32, CH])
    consts = {0.0, 1.0, -1.0, 0.125, 1e-4, 1e-5, 1e-12, 1.0 / 64.0,
              1.0 / 1024.0, -1.0 / 1024.0, 1.0 / 4096.0, INV_2PI, MAGIC,
              MAGIC + 0.25, -TWO_PI, math.pi / 2.0, NEG}
    for i in range(16):
        consts.add(-(2.0 ** (-0.5 * (i + 1))))
    for d_ in range(1, 16):
        consts.add(float(128 * d_))
    consts = sorted(consts)
    ctile = sb("g_consts", [128, len(consts)], F32)
    ctileb = sb("g_constsb", [128, len(consts)], BF16)
    pl = PL()
    pl.v(C("memset", B["ones_col_b"][:, :], 1.0))
    pl.v(C("memset", B["ones_row_b"][:, :], 1.0))
    pl.v(C("memset", B["zeros"][:, :], 0.0))
    for i, vv in enumerate(consts):
        pl.v(C("memset", ctile[:, i:i + 1], float(vv)))
        pl.v(C("memset", ctileb[:, i:i + 1], float(vv)))
        nc.const_aps.register(F32, float(vv), ctile[:, i:i + 1])
        nc.const_aps.register(BF16, float(vv), ctileb[:, i:i + 1])
    emit_plan(nc, chain, state, pl)
    return B


def ln_chunk(pl, B, write_calls_fn):
    xb = B["scr_bf"]
    sums = B["p_row"][0:1, :]
    sqs = B["p_row2"][0:1, :]
    mb = B["p_bc"]
    rb = B["p_bc2"]
    t2 = B["scr_b"]
    ra = B["row_a"]
    for g in range(8):
        xg = B["xsum"][g][:, :]
        pl.v(C("tensor_copy", xb[:, :], xg))
        pl.t(C("matmul", sums, B["ones_col_b"][:, :], xb[:, :],
              start=(g == 0), stop=(g == 7)))
        pl.s(C("activation", xb[:, :], xb[:, :], AF.Square))
        pl.t(C("matmul", sqs, B["ones_col_b"][:, :], xb[:, :],
              start=(g == 0), stop=(g == 7)))
    pl.v(C("tensor_scalar", ra[0:1, 0:CH], sums, -1.0 / 1024.0, None, OP.mult))
    pl.v(C("tensor_scalar", ra[0:1, CH:2 * CH], sqs, 1.0 / 1024.0, None, OP.mult))
    pl.s(C("activation", ra[0:1, 2 * CH:3 * CH], ra[0:1, 0:CH], AF.Square))
    pl.v(C("tensor_sub", ra[0:1, CH:2 * CH], ra[0:1, CH:2 * CH],
          ra[0:1, 2 * CH:3 * CH]))
    pl.s(C("activation", ra[0:1, CH:2 * CH], ra[0:1, CH:2 * CH], AF.Sqrt,
          bias=1e-5))
    pl.v(C("reciprocal", ra[0:1, CH:2 * CH], ra[0:1, CH:2 * CH]))
    pl.v(C("tensor_copy", B["row_bf_a"][:, :], ra[0:1, 0:CH]))
    pl.v(C("tensor_copy", B["row_bf_b"][:, :], ra[0:1, CH:2 * CH]))
    pl.t(C("matmul", mb[:, :], B["ones_row_b"][:, :], B["row_bf_a"][:, :]))
    pl.t(C("matmul", rb[:, :], B["ones_row_b"][:, :], B["row_bf_b"][:, :]))
    for g in range(8):
        pl.v(C("tensor_add", t2[:, :], B["xsum"][g][:, :], mb[:, :]))
        pl.v(C("tensor_mul", t2[:, :], t2[:, :], rb[:, :]))
        for eng, call in write_calls_fn(g, t2):
            getattr(pl, eng)(call)


def emit_l0_attn(nc, chain, state, B, dr, aux, xin, xout):
    ZC = 2080
    es = ExitStack()
    sb = lambda name, shape, dt: es.enter_context(nc.sbuf_tensor(name, shape, dt))
    zb = [sb(f"a_zb{g}", [128, L + 3], BF16) for g in range(16)]
    z16 = sb("a_z16", [32, L + 3], F32)
    gate = [sb(f"a_gate{g}", [128, CH], BF16) for g in range(8)]
    nre = [sb(f"a_nre{g}", [128, CH], F32) for g in range(8)]
    nim = [sb(f"a_nim{g}", [128, CH], F32) for g in range(8)]
    car_re = sb("a_car_re", [128, 8], F32)
    car_im = sb("a_car_im", [128, 8], F32)
    car_dn = sb("a_car_dn", [32, 1], F32)
    pw = sb("a_pw", [32, CH], F32)
    pwb = sb("a_pwb", [32, CH], BF16)
    den = sb("a_den", [32, CH], F32)
    cscb = sb("a_cscb", [32, CH], BF16)
    rates = sb("a_rates", [32, 1], F32)
    ssc_col = sb("a_ssc", [32, 1], F32)
    theta_col = sb("a_th", [128, 8], F32)
    nsre_col = sb("a_nsre", [128, 1], F32)
    nsim_col = sb("a_nsim", [128, 1], F32)
    ck_cols = sb("a_ck", [128, 68], F32)
    blk_re = sb("a_blkre", [128, 128], BF16)
    blk_im = sb("a_blkim", [128, 128], BF16)
    blk32 = sb("a_blk32", [128, 128], F32)
    wt32 = sb("a_wt32", [128, 1024], F32)
    wtb = sb("a_wtb", [128, 1024], BF16)
    sdist = sb("a_sdist", [32, L], F32)
    selpw = sb("a_selpw", [32, 1024], BF16)
    selT = sb("a_selT", [128, 256], BF16)
    tsc = sb("a_tsc", [32, 4], F32)
    es_x = ExitStack()
    sbx = lambda name, shape, dt: es_x.enter_context(nc.sbuf_tensor(name, shape, dt))
    xTb = [sbx(f"a_xtb{g}", [128, L], BF16) for g in range(8)]

    P = B["p_main"]
    scr_a, scr_b, scr_c = B["scr_a"], B["scr_b"], B["scr_c"]
    sqb = B["scr_bf"]
    pl = PL()

    pl.ld(C("dma_start", out=sdist[:, :], in_=aux["dist"][:, :]),
          C("dma_start", out=selpw[:, :], in_=aux["selpw"][:, :]),
          C("dma_start", out=selT[:, :], in_=aux["selT"][:, :]),
          C("dma_start", out=rates[0:28, 0:1], in_=dr["L0_attn_decay_slopes"][:, :]),
          C("dma_start", out=rates[28:32, 0:1], in_=dr["L0_attn_anchor_slopes"][:, :]),
          C("dma_start", out=ssc_col[:, 0:1], in_=dr["L0_attn_score_scale"][:, :]))
    pl.ld(*[C("dma_start", out=theta_col[:, g:g + 1],
              in_=dr["L0_attn_theta"][128 * g:128 * (g + 1), :]) for g in range(8)])
    pl.ld(*([C("dma_start", out=nsre_col[32 * r:32 * (r + 1), 0:1],
               in_=dr["L0_attn_norm_scale"][0:32, :]) for r in range(4)] +
            [C("dma_start", out=nsim_col[32 * r:32 * (r + 1), 0:1],
               in_=dr["L0_attn_norm_scale"][32:64, :]) for r in range(4)]))
    ckcalls = []
    for j in range(4):
        for g in range(17):
            n = min(128, ZC - 128 * g)
            ckcalls.append(C("dma_start", out=ck_cols[0:n, j * 17 + g:j * 17 + g + 1],
                             in_=dr["L0_attn_conv_kernel"][j, 128 * g:128 * g + n, :]))
    pl.ld(*ckcalls)
    pl.v(C("memset", blk32[:, :], 0.0))
    pl.ld(*[C("dma_start", out=blk32[32 * r:32 * (r + 1), 32 * r:32 * (r + 1)],
              in_=dr["L0_attn_W_re"][:, :]) for r in range(4)])
    pl.v(C("tensor_copy", blk_re[:, :], blk32[:, :]))
    pl.v(C("memset", blk32[:, :], 0.0))
    pl.ld(*[C("dma_start", out=blk32[32 * r:32 * (r + 1), 32 * r:32 * (r + 1)],
              in_=dr["L0_attn_W_im"][:, :]) for r in range(4)])
    pl.v(C("tensor_copy", blk_im[:, :], blk32[:, :]))
    pl.s(C("activation", tsc[:, 0:1], rates[:, 0:1], AF.Abs))
    pl.s(C("activation", tsc[:, 1:2], tsc[:, 0:1], AF.Exp, scale=-1.0))
    pl.s(C("activation", tsc[:, 2:3], tsc[:, 1:2], AF.Ln, bias=1.0))
    pl.v(C("tensor_scalar_max", tsc[:, 3:4], rates[:, 0:1], 0.0))
    pl.v(C("tensor_add", rates[:, 0:1], tsc[:, 2:3], tsc[:, 3:4]))
    pl.v(C("tensor_scalar_mul", rates[:, 0:1], rates[:, 0:1], -1.0))
    pl.v(*([C("memset", zb[g][:, 0:3], 0.0) for g in range(16)] +
           [C("memset", z16[:, 0:3], 0.0),
            C("memset", car_re[:, :], 0.0),
            C("memset", car_im[:, :], 0.0),
            C("memset", car_dn[:, :], 0.0)]))
    for g in range(8):
        for c in range(NCH):
            pl.ld(C("dma_start", out=B["scr_a"][:, :],
                    in_=xin[128 * g:128 * (g + 1), CH * c:CH * (c + 1)]))
            pl.v(C("tensor_copy", xTb[g][:, CH * c:CH * (c + 1)],
                  B["scr_a"][:, :]))

    win = dr["L0_attn_W_in"]
    for m in range(17):
        mm = min(128, ZC - 128 * m)
        for k in range(8):
            pl.ld(C("dma_start", out=wt32[:, 0:mm],
                    in_=win[128 * k:128 * (k + 1), 128 * m:128 * m + mm]))
            pl.v(C("tensor_copy", wtb[:, 0:mm], wt32[:, 0:mm]))
            pl.t(*[C("matmul", P[c][0:mm, :], wtb[0:128, 0:mm],
                     xTb[k][:, CH * c:CH * (c + 1)],
                     start=(k == 0), stop=(k == 7)) for c in range(NCH)])
        if m < 16:
            pl.v(*[C("tensor_copy", zb[m][0:mm, 3 + CH * c:3 + CH * (c + 1)],
                     P[c][0:mm, :]) for c in range(NCH)])
        else:
            pl.v(*[C("tensor_copy", z16[0:mm, 3 + CH * c:3 + CH * (c + 1)],
                     P[c][0:mm, :]) for c in range(NCH)])
    emit_plan(nc, chain, state, pl)
    es_x.close()
    gy = [sb(f"a_gy{g}", [128, L], BF16) for g in range(8)]
    pl = PL()

    sq_ps = B["p_row"]
    y_ps = B["p_bc2"]
    for c in range(NCH):
        t0 = CH * c
        pl.v(C("tensor_scalar", scr_a[0:32, :], z16[:, t0:t0 + CH],
              ck_cols[0:32, 16:17], None, OP.mult))
        for j in range(1, 4):
            pl.v(C("scalar_tensor_tensor", scr_a[0:32, :], z16[:, t0 + j:t0 + j + CH],
                  ck_cols[0:32, j * 17 + 16:j * 17 + 17], scr_a[0:32, :],
                  OP.mult, OP.add))
        pl.v(C("tensor_scalar", scr_b[0:32, :], sdist[:, t0:t0 + CH],
              rates[:, 0:1], None, OP.mult))
        pl.v(C("scalar_tensor_tensor", scr_c[0:32, :], scr_a[0:32, :],
              ssc_col[:, 0:1], scr_b[0:32, :], OP.mult, OP.add))
        pl.s(C("activation", pw[:, :], scr_c[0:32, :], AF.Exp))
        pl.v(C("tensor_copy", pwb[:, :], pw[:, :]))
        pl.v(C("tensor_tensor_scan", den[:, :], pw[:, :], B["zeros"][0:32, :],
              car_dn[:, 0:1] if c > 0 else 0.0, OP.add, OP.add))
        pl.v(C("tensor_copy", car_dn[:, 0:1], den[:, CH - 1:CH]))
        for g in range(8):
            zt = zb[8 + g]
            pl.v(C("tensor_scalar", scr_a[:, :], zt[:, t0:t0 + CH],
                  ck_cols[:, 8 + g:9 + g], None, OP.mult))
            for j in range(1, 4):
                pl.v(C("scalar_tensor_tensor", scr_a[:, :], zt[:, t0 + j:t0 + j + CH],
                      ck_cols[:, j * 17 + 8 + g:j * 17 + 9 + g], scr_a[:, :],
                      OP.mult, OP.add))
            pl.s(C("activation", gate[g][:, :], scr_a[:, :], AF.Silu))
        for g in range(8):
            zt = zb[g]
            pl.v(C("tensor_scalar", scr_a[:, :], zt[:, t0:t0 + CH],
                  ck_cols[:, g:g + 1], None, OP.mult))
            for j in range(1, 4):
                pl.v(C("scalar_tensor_tensor", scr_a[:, :], zt[:, t0 + j:t0 + j + CH],
                      ck_cols[:, j * 17 + g:j * 17 + 1 + g], scr_a[:, :],
                      OP.mult, OP.add))
            pl.v(C("tensor_scalar", scr_b[:, :], scr_a[:, :],
                  theta_col[:, g:g + 1], None, OP.mult))
            pl.v(C("tensor_scalar", scr_a[:, :], scr_b[:, :], INV_2PI, MAGIC,
                  OP.mult, OP.add))
            pl.v(C("tensor_scalar", scr_a[:, :], scr_a[:, :], MAGIC, None,
                  OP.subtract))
            pl.v(C("scalar_tensor_tensor", scr_c[:, :], scr_a[:, :], -TWO_PI,
                  scr_b[:, :], OP.mult, OP.add))
            pl.s(C("activation", scr_c[:, :], scr_c[:, :], AF.Sin))
            pl.v(C("tensor_scalar", scr_a[:, :], scr_b[:, :], INV_2PI,
                  MAGIC + 0.25, OP.mult, OP.add))
            pl.v(C("tensor_scalar", scr_a[:, :], scr_a[:, :], MAGIC, None,
                  OP.subtract))
            pl.v(C("scalar_tensor_tensor", scr_a[:, :], scr_a[:, :], -TWO_PI,
                  scr_b[:, :], OP.mult, OP.add))
            pl.v(C("tensor_scalar", scr_a[:, :], scr_a[:, :], math.pi / 2.0,
                  None, OP.add))
            pl.s(C("activation", scr_a[:, :], scr_a[:, :], AF.Sin))
            pl.t(C("matmul", B["p_bc"][:, :], selpw[:, 128 * g:128 * (g + 1)],
                  pwb[:, :]))
            pl.v(C("tensor_mul", scr_a[:, :], scr_a[:, :], B["p_bc"][:, :]))
            pl.v(C("tensor_mul", scr_c[:, :], scr_c[:, :], B["p_bc"][:, :]))
            pl.v(C("tensor_tensor_scan", nre[g][:, :], scr_a[:, :],
                  B["zeros"][:, :], car_re[:, g:g + 1] if c > 0 else 0.0,
                  OP.add, OP.add))
            pl.v(C("tensor_tensor_scan", nim[g][:, :], scr_c[:, :],
                  B["zeros"][:, :], car_im[:, g:g + 1] if c > 0 else 0.0,
                  OP.add, OP.add))
            pl.v(C("tensor_copy", car_re[:, g:g + 1], nre[g][:, CH - 1:CH]),
                 C("tensor_copy", car_im[:, g:g + 1], nim[g][:, CH - 1:CH]))
            pl.s(C("activation", sqb[:, :], nre[g][:, :], AF.Square))
            pl.t(C("matmul", sq_ps[:, :], selT[:, 32 * g:32 * (g + 1)], sqb[:, :],
                  start=(g == 0), stop=False))
            pl.s(C("activation", sqb[:, :], nim[g][:, :], AF.Square))
            pl.t(C("matmul", sq_ps[:, :], selT[:, 32 * g:32 * (g + 1)], sqb[:, :],
                  start=False, stop=(g == 7)))
        pl.v(C("tensor_scalar_max", scr_a[0:32, :], den[:, :], 1e-4))
        pl.v(C("reciprocal", scr_a[0:32, :], scr_a[0:32, :]))
        pl.s(C("activation", scr_b[0:32, :], scr_a[0:32, :], AF.Square))
        pl.v(C("tensor_mul", scr_b[0:32, :], scr_b[0:32, :], sq_ps[:, :]))
        pl.v(C("tensor_scalar", scr_b[0:32, :], scr_b[0:32, :], 1.0 / 64.0,
              1e-5, OP.mult, OP.add))
        pl.s(C("activation", scr_b[0:32, :], scr_b[0:32, :], AF.Sqrt))
        pl.v(C("reciprocal", scr_b[0:32, :], scr_b[0:32, :]))
        pl.v(C("tensor_mul", scr_b[0:32, :], scr_a[0:32, :], scr_b[0:32, :]))
        pl.v(C("tensor_copy", cscb[:, :], scr_b[0:32, :]))
        for g in range(8):
            pl.t(C("matmul", B["p_bc"][:, :], selpw[:, 128 * g:128 * (g + 1)],
                  cscb[:, :]))
            pl.v(C("scalar_tensor_tensor", sqb[:, :], nre[g][:, :],
                  nsre_col[:, 0:1], B["p_bc"][:, :], OP.mult, OP.mult))
            pl.t(C("matmul", y_ps[:, :], blk_re[:, :], sqb[:, :],
                  start=True, stop=False))
            pl.v(C("scalar_tensor_tensor", sqb[:, :], nim[g][:, :],
                  nsim_col[:, 0:1], B["p_bc"][:, :], OP.mult, OP.mult))
            pl.t(C("matmul", y_ps[:, :], blk_im[:, :], sqb[:, :],
                  start=False, stop=True))
            pl.v(C("tensor_mul", gy[g][:, t0:t0 + CH], gate[g][:, :],
                  y_ps[:, :]))

    wo = dr["L0_attn_W_out"]
    for c in range(NCH):
        for g in range(8):
            pl.ld(*[C("dma_start", out=wt32[:, 128 * k:128 * (k + 1)],
                      in_=wo[128 * k:128 * (k + 1), 128 * g:128 * (g + 1)])
                    for k in range(8)])
            pl.v(C("tensor_copy", wtb[:, :], wt32[:, :]))
            pl.t(*[C("matmul", P[g % 4][:, :], wtb[:, 128 * k:128 * (k + 1)],
                     gy[k][:, CH * c:CH * (c + 1)],
                     start=(k == 0), stop=(k == 7)) for k in range(8)])
            pl.ld(C("dma_start", out=scr_a[:, :],
                    in_=xin[128 * g:128 * (g + 1), CH * c:CH * (c + 1)]))
            pl.v(C("tensor_add", B["xsum"][g][:, :], scr_a[:, :], P[g % 4][:, :]))
        def wrt(g, src, c=c):
            return [("st", C("dma_start",
                             out=xout[128 * g:128 * (g + 1), CH * c:CH * (c + 1)],
                             in_=src[:, :]))]
        ln_chunk(pl, B, wrt)

    emit_plan(nc, chain, state, pl)
    es.close()


def emit_l0_ffn(nc, chain, state, B, dr, aux, xin, xout, xwork):
    HID = 2730
    P = B["p_main"]
    scr_a, scr_b, scr_c = B["scr_a"], B["scr_b"], B["scr_c"]

    pl = PL()
    for g in range(8):
        for c in range(NCH):
            pl.ld(C("dma_start", out=scr_a[:, :],
                    in_=xin[128 * g:128 * (g + 1), CH * c:CH * (c + 1)]))
            pl.st(C("dma_start", out=xwork[128 * g:128 * (g + 1), CH * c:CH * (c + 1)],
                    in_=scr_a[:, :]))
    emit_plan(nc, chain, state, pl)

    for lyr in range(2):
        es1 = ExitStack()
        sb1 = lambda name, shape, dt: es1.enter_context(nc.sbuf_tensor(f"{name}_{lyr}", shape, dt))
        vtok = [sb1(f"f_vt{t}", [128, 512], BF16) for t in range(16)]
        q2b = [sb1(f"f_q2b{g}", [128, L], BF16) for g in range(4)]
        k2b = [sb1(f"f_k2b{g}", [128, 4096], BF16) for g in range(4)]
        wt32 = sb1("f_wt32", [128, 512], F32)
        wtb = sb1("f_wtb", [128, 512], BF16)
        qs_col = sb1("f_qs", [128, 1], F32)
        qs_rot = sb1("f_qsr", [128, 1], F32)
        ks_col = sb1("f_ks", [128, 1], F32)
        ks_rot = sb1("f_ksr", [128, 1], F32)
        selB = sb1("f_selB", [2, 128], BF16)
        selH = sb1("f_selH", [128, 2], BF16)
        es1a = ExitStack()
        sb1a = lambda name, shape, dt: es1a.enter_context(nc.sbuf_tensor(f"{name}_{lyr}", shape, dt))
        qkb = [sb1a(f"f_qkb{g}", [128, L], BF16) for g in range(8)]
        rin = [sb1a(f"f_rin{i}", [2, L], BF16) for i in range(8)]
        es1a1 = ExitStack()
        shb = [es1a1.enter_context(nc.sbuf_tensor(f"f_shb{g}_{lyr}", [128, L], BF16))
               for g in range(8)]

        pl = PL()
        pl.ld(C("dma_start", out=selB[:, :], in_=aux["selB"][:, :]),
              C("dma_start", out=selH[:, :], in_=aux["selH"][:, :]))
        qsd = dr["L0_ffn_q_scale"]
        ksd = dr["L0_ffn_k_scale"]
        sc_calls = []
        for hl in range(2):
            sc_calls.append(C("dma_start", out=qs_col[64 * hl:64 * hl + 64, 0:1],
                              in_=qsd[lyr, :, :]))
            sc_calls.append(C("dma_start", out=ks_col[64 * hl:64 * hl + 64, 0:1],
                              in_=ksd[lyr, :, :]))
            for hf in range(2):
                sc_calls.append(C("dma_start",
                                  out=qs_rot[64 * hl + 32 * hf:64 * hl + 32 * hf + 32, 0:1],
                                  in_=qsd[lyr, 32 - 32 * hf:64 - 32 * hf, :]))
                sc_calls.append(C("dma_start",
                                  out=ks_rot[64 * hl + 32 * hf:64 * hl + 32 * hf + 32, 0:1],
                                  in_=ksd[lyr, 32 - 32 * hf:64 - 32 * hf, :]))
        pl.ld(*sc_calls)

        for g in range(8):
            for c in range(NCH):
                pl.ld(C("dma_start", out=scr_a[:, :],
                        in_=xwork[128 * g:128 * (g + 1), CH * c:CH * (c + 1)]))
                if g >= 4:
                    if c == 0:
                        pl.v(C("memset", shb[g][:, 0:1], 0.0))
                    else:
                        pl.ld(C("dma_start", out=scr_b[:, 0:1],
                                in_=xwork[128 * g:128 * (g + 1), CH * c - 1:CH * c],
                                allow_slow_non_contiguous=True))
                        pl.v(C("tensor_copy", shb[g][:, CH * c:CH * c + 1],
                              scr_b[:, 0:1]))
                    pl.v(C("tensor_copy", shb[g][:, CH * c + 1:CH * (c + 1)],
                          scr_a[:, 0:CH - 1]))
                else:
                    pl.v(C("tensor_copy", shb[g][:, CH * c:CH * (c + 1)],
                          scr_a[:, :]))

        wq = dr["L0_ffn_wqkv"]
        for m in range(8):
            for k in range(8):
                pl.ld(C("dma_start", out=wt32[:, 0:128],
                        in_=wq[lyr, 128 * k:128 * (k + 1), 128 * m:128 * (m + 1)]))
                pl.v(C("tensor_copy", wtb[:, 0:128], wt32[:, 0:128]))
                pl.t(*[C("matmul", P[c][:, :], wtb[:, 0:128],
                         shb[k][:, CH * c:CH * (c + 1)],
                         start=(k == 0), stop=(k == 7)) for c in range(NCH)])
            for c in range(NCH):
                pl.v(C("tensor_copy", qkb[m][:, CH * c:CH * (c + 1)], P[c][:, :]))
                pl.s(C("activation", B["scr_bf"][:, :], P[c][:, :], AF.Square))
                pl.t(C("matmul", B["p_row2"][0:2, :], selH[:, :], B["scr_bf"][:, :]))
                pl.s(C("activation", scr_a[0:2, :], B["p_row2"][0:2, :], AF.Sqrt))
                pl.v(C("tensor_scalar_max", scr_a[0:2, :], scr_a[0:2, :], 1e-12))
                pl.v(C("reciprocal", scr_a[0:2, :], scr_a[0:2, :]))
                pl.v(C("tensor_copy", rin[m][:, CH * c:CH * (c + 1)],
                      scr_a[0:2, :]))
        for tg in range(4):
            for k in range(8):
                pl.ld(C("dma_start", out=wt32[:, 0:512],
                        in_=wq[lyr, 128 * k:128 * (k + 1), 1024:1536]))
                pl.v(C("tensor_copy", wtb[:, 0:512], wt32[:, 0:512]))
                pl.t(*[C("matmul", P[t % 4][:, :],
                         shb[k][:, 128 * t:128 * (t + 1)], wtb[:, 0:512],
                         start=(k == 0), stop=(k == 7))
                       for t in range(4 * tg, 4 * tg + 4)])
            pl.v(*[C("tensor_copy", vtok[t][:, :], P[t % 4][:, :])
                   for t in range(4 * tg, 4 * tg + 4)])
        emit_plan(nc, chain, state, pl)
        es1a1.close()
        es1a2 = ExitStack()
        sb1a2 = lambda name, shape, dt: es1a2.enter_context(nc.sbuf_tensor(f"{name}_{lyr}", shape, dt))
        prot = sb1a2("f_prot", [128, 128], BF16)
        qcosb = sb1a2("f_qcos", [128, 4 * CH], BF16)
        qsinb = sb1a2("f_qsin", [128, 4 * CH], BF16)
        kcosb = sb1a2("f_kcos", [128, 4096], BF16)
        ksinb = sb1a2("f_ksin", [128, 4096], BF16)
        pl = PL()
        pl.ld(C("dma_start", out=prot[:, :], in_=aux["prot"][:, :]),
              C("dma_start", out=qcosb[:, :], in_=aux["qcos"][:, :]),
              C("dma_start", out=qsinb[:, :], in_=aux["qsin"][:, :]),
              C("dma_start", out=kcosb[:, :], in_=aux["kcos"][:, :]),
              C("dma_start", out=ksinb[:, :], in_=aux["ksin"][:, :]))
        for g in range(4):
            for w in range(4):
                cw = CH * w
                pl.t(C("matmul", B["p_bc"][:, :], prot[:, :], qkb[g][:, cw:cw + CH]))
                pl.t(C("matmul", B["p_bc2"][:, :], selB[:, :],
                      rin[g][:, cw:cw + CH]))
                pl.v(C("scalar_tensor_tensor", scr_a[:, :], qkb[g][:, cw:cw + CH],
                      qs_col[:, 0:1], qcosb[:, cw:cw + CH], OP.mult, OP.mult))
                pl.v(C("scalar_tensor_tensor", scr_b[:, :], B["p_bc"][:, :],
                      qs_rot[:, 0:1], qsinb[:, cw:cw + CH], OP.mult, OP.mult))
                pl.v(C("tensor_add", scr_a[:, :], scr_a[:, :], scr_b[:, :]))
                pl.v(C("tensor_mul", q2b[g][:, cw:cw + CH], scr_a[:, :],
                      B["p_bc2"][:, :]))
                for hf in range(2):
                    j0 = cw - CH + hf * CH
                    if j0 < 0:
                        continue
                    kc = 1024 * w + CH * hf
                    pl.t(C("matmul", B["p_bc"][:, :], prot[:, :],
                          qkb[4 + g][:, j0:j0 + CH]))
                    pl.t(C("matmul", B["p_bc2"][:, :], selB[:, :],
                          rin[4 + g][:, j0:j0 + CH]))
                    pl.v(C("scalar_tensor_tensor", scr_a[:, :],
                          qkb[4 + g][:, j0:j0 + CH], ks_col[:, 0:1],
                          kcosb[:, kc:kc + CH], OP.mult, OP.mult))
                    pl.v(C("scalar_tensor_tensor", scr_b[:, :], B["p_bc"][:, :],
                          ks_rot[:, 0:1], ksinb[:, kc:kc + CH], OP.mult, OP.mult))
                    pl.v(C("tensor_add", scr_a[:, :], scr_a[:, :], scr_b[:, :]))
                    pl.v(C("tensor_mul", k2b[g][:, kc:kc + CH], scr_a[:, :],
                          B["p_bc2"][:, :]))
        emit_plan(nc, chain, state, pl)
        es1a2.close()
        es1a.close()
        es1b = ExitStack()
        sb1b = lambda name, shape, dt: es1b.enter_context(nc.sbuf_tensor(f"{name}_{lyr}", shape, dt))
        Pb = sb1b("f_Pb", [128, 8 * CH], BF16)
        maskm = [sb1b(f"f_mk{j}", [128, CH], F32) for j in range(8)]
        nrow = sb1b("f_nrow", [1, 1024], F32)
        oT = [sb1b(f"f_oT{g}", [128, L], BF16) for g in range(4)]
        pl = PL()
        pl.ld(*[C("dma_start", out=maskm[j][:, :], in_=aux["fmask"][j, :, :])
                for j in range(8)])
        for h in range(8):
            g = h // 2
            r0 = 64 * (h % 2)
            for w in range(4):
                cw = CH * w
                jcs = list(range(8)) if w > 0 else list(range(4, 8))
                pl.s(C("activation", B["scr_bf"][0:64, :],
                      q2b[g][r0:r0 + 64, cw:cw + CH], AF.Square))
                pl.t(C("matmul", B["p_row"][0:1, :], B["ones_col_b"][0:64, :],
                      B["scr_bf"][0:64, :]))
                for idx, jc in enumerate(jcs):
                    pl.s(C("activation", B["scr_bf"][0:64, 0:128],
                          k2b[g][r0:r0 + 64, 1024 * w + 128 * jc:1024 * w + 128 * jc + 128],
                          AF.Square))
                    pl.t(C("matmul", B["p_row2"][0:1, 0:128],
                          B["ones_col_b"][0:64, :], B["scr_bf"][0:64, 0:128]))
                    pl.v(C("tensor_copy", nrow[0:1, 128 * idx:128 * (idx + 1)],
                          B["p_row2"][0:1, 0:128]))
                nj = len(jcs) * 128
                pl.v(C("reduce_max", scr_c[0:1, 0:1], nrow[0:1, 0:nj], AX.X))
                pl.v(C("tensor_scalar", scr_b[0:1, :], B["p_row"][0:1, :],
                      scr_c[0:1, 0:1], None, OP.mult))
                pl.s(C("activation", scr_b[0:1, :], scr_b[0:1, :], AF.Sqrt))
                pl.v(C("tensor_scalar_mul", scr_b[0:1, :], scr_b[0:1, :], -1.0))
                pl.v(C("tensor_copy", B["row_bf_a"][:, :], scr_b[0:1, :]))
                pl.t(C("matmul", B["p_bc2"][:, :], B["ones_row_b"][:, :],
                      B["row_bf_a"][:, :]))
                for jc in jcs:
                    pl.t(C("matmul", P[jc % 4][:, :],
                          k2b[g][r0:r0 + 64, 1024 * w + 128 * jc:1024 * w + 128 * jc + 128],
                          q2b[g][r0:r0 + 64, cw:cw + CH]))
                    pl.v(C("tensor_add", scr_a[:, :], maskm[jc][:, :],
                          B["p_bc2"][:, :]))
                    pl.v(C("tensor_add", scr_a[:, :], scr_a[:, :],
                          P[jc % 4][:, :]))
                    pl.s(C("activation", Pb[:, CH * jc:CH * (jc + 1)],
                          scr_a[:, :], AF.Exp))
                pl.t(*[C("matmul", B["p_row"][0:1, :], B["ones_col_b"][:, :],
                         Pb[:, CH * jc:CH * (jc + 1)],
                         start=(i == 0), stop=(i == len(jcs) - 1))
                       for i, jc in enumerate(jcs)])
                pl.v(C("reciprocal", scr_c[0:1, :], B["p_row"][0:1, :]))
                pl.v(C("tensor_copy", B["row_bf_b"][:, :], scr_c[0:1, :]))
                pl.t(*[C("matmul", B["p_bc"][0:64, :],
                         vtok[4 * (w - 1) + jc][:, 64 * h:64 * (h + 1)],
                         Pb[:, CH * jc:CH * (jc + 1)],
                         start=(i == 0), stop=(i == len(jcs) - 1))
                       for i, jc in enumerate(jcs)])
                pl.t(C("matmul", B["p_bc2"][0:64, :], B["ones_row_b"][:, 0:64],
                      B["row_bf_b"][:, :]))
                pl.v(C("tensor_copy", scr_a[0:64, :], B["p_bc"][0:64, :]))
                pl.v(C("tensor_mul", oT[g][r0:r0 + 64, cw:cw + CH],
                      scr_a[0:64, :], B["p_bc2"][0:64, :]))
        wod = dr["L0_ffn_wo"]
        for m in range(8):
            for k in range(4):
                pl.ld(C("dma_start", out=wt32[:, 0:128],
                        in_=wod[lyr, 128 * k:128 * (k + 1), 128 * m:128 * (m + 1)]))
                pl.v(C("tensor_copy", wtb[:, 0:128], wt32[:, 0:128]))
                pl.t(*[C("matmul", P[c][:, :], wtb[:, 0:128],
                         oT[k][:, CH * c:CH * (c + 1)],
                         start=(k == 0), stop=(k == 3)) for c in range(NCH)])
            for c in range(NCH):
                pl.ld(C("dma_start", out=scr_a[:, :],
                        in_=xwork[128 * m:128 * (m + 1), CH * c:CH * (c + 1)]))
                pl.v(C("tensor_add", scr_b[:, :], scr_a[:, :], P[c][:, :]))
                pl.st(C("dma_start",
                        out=xwork[128 * m:128 * (m + 1), CH * c:CH * (c + 1)],
                        in_=scr_b[:, :]))
        emit_plan(nc, chain, state, pl)
        es1b.close()
        es1.close()

        es2 = ExitStack()
        sb2 = lambda name, shape, dt: es2.enter_context(nc.sbuf_tensor(f"{name}_{lyr}", shape, dt))
        hnb = [sb2(f"f_hn{g}", [128, L], BF16) for g in range(8)]
        fb = [sb2(f"f_fb{m}", [128, L], BF16) for m in range(22)]
        wt32 = sb2("f_wt32b", [128, 128], F32)
        wtb = sb2("f_wtbb", [128, 128], BF16)
        lg_col = sb2("f_lg", [128, 8], F32)
        lb_col = sb2("f_lb", [128, 8], F32)
        b2_col = sb2("f_b2", [128, 8], F32)
        b1a_col = sb2("f_b1a", [128, 22], F32)
        b1g_col = sb2("f_b1g", [128, 22], F32)
        pl = PL()
        pl.ld(*([C("dma_start", out=lg_col[:, g:g + 1],
                   in_=dr["L0_ffn_ln_g"][lyr, 128 * g:128 * (g + 1), :])
                 for g in range(8)] +
                [C("dma_start", out=lb_col[:, g:g + 1],
                   in_=dr["L0_ffn_ln_b"][lyr, 128 * g:128 * (g + 1), :])
                 for g in range(8)] +
                [C("dma_start", out=b2_col[:, g:g + 1],
                   in_=dr["L0_ffn_b2"][lyr, 128 * g:128 * (g + 1), :])
                 for g in range(8)]))
        MT = [(m, min(128, HID - 128 * m)) for m in range(22)]
        pl.ld(*([C("dma_start", out=b1a_col[0:mm, m:m + 1],
                   in_=dr["L0_ffn_b1"][lyr, 128 * m:128 * m + mm, :])
                 for m, mm in MT] +
                [C("dma_start", out=b1g_col[0:mm, m:m + 1],
                   in_=dr["L0_ffn_b1"][lyr, HID + 128 * m:HID + 128 * m + mm, :])
                 for m, mm in MT]))

        for c in range(NCH):
            for g in range(8):
                pl.ld(C("dma_start", out=scr_a[:, :],
                        in_=xwork[128 * g:128 * (g + 1), CH * c:CH * (c + 1)]))
                if g >= 4:
                    if c == 0:
                        pl.v(C("memset", B["xsum"][g][:, 0:1], 0.0))
                    else:
                        pl.ld(C("dma_start", out=scr_b[:, 0:1],
                                in_=xwork[128 * g:128 * (g + 1), CH * c - 1:CH * c],
                                allow_slow_non_contiguous=True))
                        pl.v(C("tensor_copy", B["xsum"][g][:, 0:1], scr_b[:, 0:1]))
                    pl.v(C("tensor_copy", B["xsum"][g][:, 1:CH], scr_a[:, 0:CH - 1]))
                else:
                    pl.v(C("tensor_copy", B["xsum"][g][:, :], scr_a[:, :]))
            def wr_hn(g, src, c=c):
                return [("v", C("tensor_scalar", hnb[g][:, CH * c:CH * (c + 1)],
                                src[:, :], lg_col[:, g:g + 1], lb_col[:, g:g + 1],
                                OP.mult, OP.add))]
            ln_chunk(pl, B, wr_hn)

        w1d = dr["L0_ffn_w1"]
        for m, mm in MT:
            for k in range(8):
                pl.ld(C("dma_start", out=wt32[0:128, 0:mm] if mm < 128 else wt32[:, :],
                        in_=w1d[lyr, 128 * k:128 * (k + 1),
                                HID + 128 * m:HID + 128 * m + mm]))
                pl.v(C("tensor_copy", wtb[:, 0:mm], wt32[:, 0:mm]))
                pl.t(*[C("matmul", P[c][0:mm, :], wtb[:, 0:mm],
                         hnb[k][:, CH * c:CH * (c + 1)],
                         start=(k == 0), stop=(k == 7)) for c in range(NCH)])
            for c in range(NCH):
                pl.v(C("tensor_scalar", scr_a[0:mm, :], P[c][0:mm, :], 1.0,
                      b1g_col[0:mm, m:m + 1], OP.mult, OP.add))
                pl.s(C("activation", scr_a[0:mm, :], scr_a[0:mm, :], AF.Gelu))
                pl.v(C("tensor_copy", fb[m][0:mm, CH * c:CH * (c + 1)],
                      scr_a[0:mm, :]))
        for m, mm in MT:
            for k in range(8):
                pl.ld(C("dma_start", out=wt32[0:128, 0:mm] if mm < 128 else wt32[:, :],
                        in_=w1d[lyr, 128 * k:128 * (k + 1), 128 * m:128 * m + mm]))
                pl.v(C("tensor_copy", wtb[:, 0:mm], wt32[:, 0:mm]))
                pl.t(*[C("matmul", P[c][0:mm, :], wtb[:, 0:mm],
                         hnb[k][:, CH * c:CH * (c + 1)],
                         start=(k == 0), stop=(k == 7)) for c in range(NCH)])
            for c in range(NCH):
                pl.v(C("tensor_scalar", scr_a[0:mm, :], P[c][0:mm, :], 1.0,
                      b1a_col[0:mm, m:m + 1], OP.mult, OP.add))
                pl.v(C("tensor_mul", scr_b[0:mm, :], scr_a[0:mm, :],
                      fb[m][0:mm, CH * c:CH * (c + 1)]))
                pl.v(C("tensor_copy", fb[m][0:mm, CH * c:CH * (c + 1)],
                      scr_b[0:mm, :]))
        w2d = dr["L0_ffn_w2"]
        for dt_ in range(8):
            for ki, (m, mm) in enumerate(MT):
                pl.ld(C("dma_start", out=wt32[0:mm, 0:128],
                        in_=w2d[lyr, 128 * m:128 * m + mm,
                                128 * dt_:128 * (dt_ + 1)]))
                pl.v(C("tensor_copy", wtb[0:mm, 0:128], wt32[0:mm, 0:128]))
                pl.t(*[C("matmul", P[c][:, :], wtb[0:mm, 0:128],
                         fb[m][0:mm, CH * c:CH * (c + 1)],
                         start=(ki == 0), stop=(ki == 21)) for c in range(NCH)])
            for c in range(NCH):
                pl.ld(C("dma_start", out=scr_a[:, :],
                        in_=xwork[128 * dt_:128 * (dt_ + 1), CH * c:CH * (c + 1)]))
                pl.v(C("tensor_scalar", scr_b[:, :], P[c][:, :], 1.0,
                      b2_col[:, dt_:dt_ + 1], OP.mult, OP.add))
                pl.v(C("tensor_add", scr_b[:, :], scr_b[:, :], scr_a[:, :]))
                pl.st(C("dma_start",
                        out=xwork[128 * dt_:128 * (dt_ + 1), CH * c:CH * (c + 1)],
                        in_=scr_b[:, :]))
        emit_plan(nc, chain, state, pl)
        es2.close()

    pl = PL()
    for c in range(NCH):
        for g in range(8):
            pl.ld(C("dma_start", out=scr_a[:, :],
                    in_=xin[128 * g:128 * (g + 1), CH * c:CH * (c + 1)]))
            pl.ld(C("dma_start", out=scr_b[:, :],
                    in_=xwork[128 * g:128 * (g + 1), CH * c:CH * (c + 1)]))
            pl.v(C("tensor_add", B["xsum"][g][:, :], scr_a[:, :], scr_b[:, :]))
        def wrt(g, src, c=c):
            return [("st", C("dma_start",
                             out=xout[128 * g:128 * (g + 1), CH * c:CH * (c + 1)],
                             in_=src[:, :]))]
        ln_chunk(pl, B, wrt)
    emit_plan(nc, chain, state, pl)


def emit_l1_attn(nc, chain, state, B, dr, aux, xin, xout):
    es = ExitStack()
    sb = lambda name, shape, dt: es.enter_context(nc.sbuf_tensor(name, shape, dt))
    xTb = [sb(f"b_xtb{g}", [128, L], BF16) for g in range(8)]
    qb = [sb(f"b_qb{g}", [128, L], BF16) for g in range(8)]
    kb = [sb(f"b_kb{g}", [128, L], BF16) for g in range(8)]
    vtok = [sb(f"b_vt{t}", [128, 1024], BF16) for t in range(16)]
    Pb = sb("b_Pb", [128, 16 * CH], BF16)
    wt32 = sb("b_wt32", [128, 1024], F32)
    wtb = sb("b_wtb", [128, 1024], BF16)
    dtp = sb("b_dtp", [128, CH], F32)
    dtm = [sb(f"b_dtm{v}", [128, CH], F32) for v in range(4)]
    nrow = sb("b_nrow", [1, L], F32)
    mk2 = sb("b_mk2", [1, 16], F32)
    oT = xTb

    P = B["p_main"]
    scr_a, scr_b, scr_c = B["scr_a"], B["scr_b"], B["scr_c"]
    slopes = [2.0 ** (-0.5 * (i + 1)) for i in range(16)]

    pl = PL()
    pl.ld(C("dma_start", out=dtp[:, :], in_=aux["dt1"][:, :]))
    pl.ld(*[C("dma_start", out=dtm[v][:, :], in_=aux["dtm1"][v, :, :])
            for v in range(4)])
    for g in range(8):
        for c in range(NCH):
            pl.ld(C("dma_start", out=scr_a[:, :],
                    in_=xin[128 * g:128 * (g + 1), CH * c:CH * (c + 1)]))
            pl.v(C("tensor_copy", xTb[g][:, CH * c:CH * (c + 1)], scr_a[:, :]))
    wqk = dr["L1_attn_W_qkv"]
    for part, dst, fold in ((0, qb, 0.125), (1, kb, 1.0)):
        for m in range(8):
            for k in range(8):
                pl.ld(C("dma_start", out=wt32[:, 0:128],
                        in_=wqk[128 * k:128 * (k + 1),
                                1024 * part + 128 * m:1024 * part + 128 * (m + 1)]))
                pl.v(C("tensor_copy", wtb[:, 0:128], wt32[:, 0:128]))
                pl.t(*[C("matmul", P[c][:, :], wtb[:, 0:128],
                         xTb[k][:, CH * c:CH * (c + 1)],
                         start=(k == 0), stop=(k == 7)) for c in range(NCH)])
            for c in range(NCH):
                pl.v(C("tensor_scalar", dst[m][:, CH * c:CH * (c + 1)],
                      P[c][:, :], fold, None, OP.mult))
    for tg in range(4):
        for hf in range(2):
            for k in range(8):
                pl.ld(C("dma_start", out=wt32[:, 0:512],
                        in_=wqk[128 * k:128 * (k + 1),
                                2048 + 512 * hf:2048 + 512 * (hf + 1)]))
                pl.v(C("tensor_copy", wtb[:, 0:512], wt32[:, 0:512]))
                pl.t(*[C("matmul", P[t % 4][:, :],
                         xTb[k][:, 128 * t:128 * (t + 1)],
                         wtb[:, 0:512],
                         start=(k == 0), stop=(k == 7))
                       for t in range(4 * tg, 4 * tg + 4)])
            pl.v(*[C("tensor_copy", vtok[t][:, 512 * hf:512 * (hf + 1)],
                     P[t % 4][:, :]) for t in range(4 * tg, 4 * tg + 4)])
    emit_plan(nc, chain, state, pl)
    pl = PL()

    for h in range(16):
        g, r0 = h // 2, 64 * (h % 2)
        for c in range(NCH):
            pl.s(C("activation", B["scr_bf"][0:64, :],
                  kb[g][r0:r0 + 64, CH * c:CH * (c + 1)], AF.Square))
            pl.t(C("matmul", B["p_row"][0:1, :], B["ones_col_b"][0:64, :],
                  B["scr_bf"][0:64, :]))
            pl.v(C("tensor_copy", nrow[0:1, CH * c:CH * (c + 1)],
                  B["p_row"][0:1, :]))
        pl.v(C("reduce_max", mk2[0:1, h:h + 1], nrow[0:1, :], AX.X))

    for h in range(16):
        g, r0 = h // 2, 64 * (h % 2)
        sl = slopes[h]
        for ic in range(NCH):
            c0 = CH * ic
            jts = list(range(4 * ic, 16))
            pl.s(C("activation", B["scr_bf"][0:64, :],
                  qb[g][r0:r0 + 64, c0:c0 + CH], AF.Square))
            pl.t(C("matmul", B["p_row"][0:1, :], B["ones_col_b"][0:64, :],
                  B["scr_bf"][0:64, :]))
            pl.v(C("tensor_scalar", scr_b[0:1, :], B["p_row"][0:1, :],
                  mk2[0:1, h:h + 1], None, OP.mult))
            pl.s(C("activation", scr_b[0:1, :], scr_b[0:1, :], AF.Sqrt))
            pl.v(C("tensor_scalar_mul", scr_b[0:1, :], scr_b[0:1, :], -1.0))
            pl.v(C("tensor_copy", B["row_bf_a"][:, :], scr_b[0:1, :]))
            pl.t(C("matmul", B["p_bc2"][:, :], B["ones_row_b"][:, :],
                  B["row_bf_a"][:, :]))
            for jt in jts:
                v = jt - 4 * ic
                pl.t(C("matmul", P[jt % 4][:, :],
                      kb[g][r0:r0 + 64, 128 * jt:128 * (jt + 1)],
                      qb[g][r0:r0 + 64, c0:c0 + CH]))
                if v < 4:
                    pl.v(C("tensor_scalar", scr_a[:, :], dtm[v][:, :], -sl,
                          None, OP.mult))
                else:
                    pl.v(C("tensor_scalar", scr_a[:, :], dtp[:, :],
                          float(128 * jt - 512 * ic), -sl, OP.add, OP.mult))
                pl.v(C("tensor_add", scr_a[:, :], scr_a[:, :], B["p_bc2"][:, :]))
                pl.v(C("tensor_add", scr_a[:, :], scr_a[:, :], P[jt % 4][:, :]))
                pl.s(C("activation", Pb[:, CH * jt:CH * (jt + 1)], scr_a[:, :],
                      AF.Exp))
            pl.t(*[C("matmul", B["p_row"][0:1, :], B["ones_col_b"][:, :],
                     Pb[:, CH * jt:CH * (jt + 1)],
                     start=(i == 0), stop=(i == len(jts) - 1))
                   for i, jt in enumerate(jts)])
            pl.v(C("reciprocal", scr_c[0:1, :], B["p_row"][0:1, :]))
            pl.v(C("tensor_copy", B["row_bf_b"][:, :], scr_c[0:1, :]))
            pl.t(*[C("matmul", B["p_bc"][0:64, :],
                     vtok[jt][:, 64 * h:64 * (h + 1)],
                     Pb[:, CH * jt:CH * (jt + 1)],
                     start=(i == 0), stop=(i == len(jts) - 1))
                   for i, jt in enumerate(jts)])
            pl.t(C("matmul", B["p_bc2"][0:64, :], B["ones_row_b"][:, 0:64],
                  B["row_bf_b"][:, :]))
            pl.v(C("tensor_copy", scr_a[0:64, :], B["p_bc"][0:64, :]))
            pl.v(C("tensor_mul", oT[g][r0:r0 + 64, c0:c0 + CH],
                  scr_a[0:64, :], B["p_bc2"][0:64, :]))

    wo = dr["L1_attn_W_out"]
    for c in range(NCH):
        for g in range(8):
            pl.ld(*[C("dma_start", out=wt32[:, 128 * k:128 * (k + 1)],
                      in_=wo[128 * k:128 * (k + 1), 128 * g:128 * (g + 1)])
                    for k in range(8)])
            pl.v(C("tensor_copy", wtb[:, :], wt32[:, :]))
            pl.t(*[C("matmul", P[g % 4][:, :], wtb[:, 128 * k:128 * (k + 1)],
                     oT[k][:, CH * c:CH * (c + 1)],
                     start=(k == 0), stop=(k == 7)) for k in range(8)])
            pl.ld(C("dma_start", out=scr_a[:, :],
                    in_=xin[128 * g:128 * (g + 1), CH * c:CH * (c + 1)]))
            pl.v(C("tensor_add", B["xsum"][g][:, :], scr_a[:, :], P[g % 4][:, :]))
        def wrt(g, src, c=c):
            return [("st", C("dma_start",
                             out=xout[128 * g:128 * (g + 1), CH * c:CH * (c + 1)],
                             in_=src[:, :]))]
        ln_chunk(pl, B, wrt)
    emit_plan(nc, chain, state, pl)
    es.close()


def emit_l1_ffn(nc, chain, state, B, dr, aux, xin, xout, xwork):
    es = ExitStack()
    sb = lambda name, shape, dt: es.enter_context(nc.sbuf_tensor(name, shape, dt))
    xTb = [sb(f"c_xtb{g}", [128, L], BF16) for g in range(8)]
    fb = [sb(f"c_fb{m}", [128, L], BF16) for m in range(32)]
    wt32 = sb("c_wt32", [128, 128], F32)
    absb = sb("c_absb", [128, 128], BF16)
    sg = [sb(f"c_sg{k}", [128, 128], BF16) for k in range(8)]
    al_row = sb("c_alrow", [1, 128], F32)
    al_col = sb("c_alcol", [128, 1], F32)
    b_col = sb("c_bcol", [128, 1], F32)
    xstage = sb("c_xstage", [128, L], F32)

    P = B["p_main"]
    scr_a, scr_b = B["scr_a"], B["scr_b"]
    pl = PL()
    for g in range(8):
        pl.ld(C("dma_start", out=xstage[:, :], in_=xin[128 * g:128 * (g + 1), :]))
        pl.v(C("tensor_copy", xTb[g][:, :], xstage[:, :]))

    w1t = dr["L1_ffn_w1T"]
    for m in range(32):
        for k in range(8):
            pl.ld(C("dma_start", out=wt32[:, :],
                    in_=w1t[128 * k:128 * (k + 1), 128 * m:128 * (m + 1)]))
            pl.s(C("activation", absb[:, :], wt32[:, :], AF.Abs))
            pl.t(C("matmul", B["p_row"][0:1, 0:128], B["ones_col_b"][:, :],
                  absb[:, :], start=(k == 0), stop=(k == 7)))
            pl.s(C("activation", sg[k][:, :], wt32[:, :], AF.Sign))
        pl.v(C("tensor_scalar", al_row[:, :], B["p_row"][0:1, 0:128],
              1.0 / 1024.0, None, OP.mult))
        pl.st(C("dma_start", out=al_col[:, 0:1], in_=al_row[0:1, :]))
        pl.ld(C("dma_start", out=b_col[:, 0:1],
                in_=dr["L1_ffn_b1"][128 * m:128 * (m + 1), :]))
        for k in range(8):
            pl.t(*[C("matmul", P[c][:, :], sg[k][:, :],
                     xTb[k][:, CH * c:CH * (c + 1)],
                     start=(k == 0), stop=(k == 7)) for c in range(NCH)])
        for c in range(NCH):
            pl.v(C("tensor_scalar", scr_a[:, :], P[c][:, :], al_col[:, 0:1],
                  b_col[:, 0:1], OP.mult, OP.add))
            pl.s(C("activation", fb[m][:, CH * c:CH * (c + 1)], scr_a[:, :],
                  AF.Gelu))

    w2t = dr["L1_ffn_w2T"]
    for dt_ in range(8):
        for m in range(32):
            pl.ld(C("dma_start", out=wt32[:, :],
                    in_=w2t[128 * m:128 * (m + 1), 128 * dt_:128 * (dt_ + 1)]))
            pl.s(C("activation", absb[:, :], wt32[:, :], AF.Abs))
            pl.t(C("matmul", B["p_row"][0:1, 0:128], B["ones_col_b"][:, :],
                  absb[:, :], start=(m == 0), stop=(m == 31)))
            pl.s(C("activation", absb[:, :], wt32[:, :], AF.Sign))
            pl.t(*[C("matmul", P[c][:, :], absb[:, :],
                     fb[m][:, CH * c:CH * (c + 1)],
                     start=(m == 0), stop=(m == 31)) for c in range(NCH)])
        pl.v(C("tensor_scalar", al_row[:, :], B["p_row"][0:1, 0:128],
              1.0 / 4096.0, None, OP.mult))
        pl.st(C("dma_start", out=al_col[:, 0:1], in_=al_row[0:1, :]))
        pl.ld(C("dma_start", out=b_col[:, 0:1],
                in_=dr["L1_ffn_b2"][128 * dt_:128 * (dt_ + 1), :]))
        for c in range(NCH):
            pl.v(C("tensor_scalar", scr_a[:, :], P[c][:, :], al_col[:, 0:1],
                  b_col[:, 0:1], OP.mult, OP.add))
            pl.st(C("dma_start",
                    out=xwork[128 * dt_:128 * (dt_ + 1), CH * c:CH * (c + 1)],
                    in_=scr_a[:, :]))
    for c in range(NCH):
        for g in range(8):
            pl.ld(C("dma_start", out=scr_a[:, :],
                    in_=xin[128 * g:128 * (g + 1), CH * c:CH * (c + 1)]))
            pl.ld(C("dma_start", out=scr_b[:, :],
                    in_=xwork[128 * g:128 * (g + 1), CH * c:CH * (c + 1)]))
            pl.v(C("tensor_add", B["xsum"][g][:, :], scr_a[:, :], scr_b[:, :]))
        def wrt(g, src, c=c):
            return [("st", C("dma_start",
                             out=xout[128 * g:128 * (g + 1), CH * c:CH * (c + 1)],
                             in_=src[:, :]))]
        ln_chunk(pl, B, wrt)
    emit_plan(nc, chain, state, pl)
    es.close()


def build_aux_arrays():
    A = {}
    t = np.arange(L, dtype=np.float64)
    dist = np.empty((32, L), np.float32)
    dist[:28] = (L - 1.0) - t
    dist[28:] = t
    A["dist"] = dist

    selpw = np.zeros((32, 1024), np.float32)
    for g in range(8):
        for col in range(128):
            selpw[4 * g + col // 32, 128 * g + col] = 1.0
    A["selpw"] = selpw.astype(BF)

    selT = np.zeros((128, 256), np.float32)
    for g in range(8):
        for p in range(128):
            selT[p, 32 * g + 4 * g + p // 32] = 1.0
    A["selT"] = selT.astype(BF)

    selH = np.zeros((128, 2), np.float32)
    selH[0:64, 0] = 1.0
    selH[64:128, 1] = 1.0
    A["selH"] = selH.astype(BF)
    selB = np.zeros((2, 128), np.float32)
    selB[0, 0:64] = 1.0
    selB[1, 64:128] = 1.0
    A["selB"] = selB.astype(BF)

    prot = np.zeros((128, 128), np.float32)
    for blk in range(2):
        o = 64 * blk
        for m in range(64):
            if m < 32:
                prot[o + m + 32, o + m] = -1.0
            else:
                prot[o + m - 32, o + m] = 1.0
    A["prot"] = prot.astype(BF)

    DIM_HEAD = 64
    WINDOW = 512
    half = np.arange(0, DIM_HEAD, 2, dtype=np.float64)
    inv_freq = 1.0 / (10000.0 ** (half / DIM_HEAD))
    base = (half + 0.4 * DIM_HEAD) / (1.4 * DIM_HEAD)
    dh = np.arange(DIM_HEAD)
    invf_d = inv_freq[dh % 32]
    base_d = base[dh % 32]
    i = np.arange(L, dtype=np.float64)
    freqs = i[None, :] * invf_d[:, None]
    cw = 512.0 * (i // 512)
    srel = base_d[:, None] ** ((i[None, :] - cw[None, :]) / 256.0)
    qcos = np.cos(freqs) * srel * 8.0
    qsin = np.sin(freqs) * srel * 8.0
    A["qcos"] = np.tile(qcos, (2, 1)).astype(BF)
    A["qsin"] = np.tile(qsin, (2, 1)).astype(BF)
    kcos = np.zeros((DIM_HEAD, 4096), np.float64)
    ksin = np.zeros((DIM_HEAD, 4096), np.float64)
    for w in range(4):
        for jcol in range(1024):
            j = 512 * (w - 1) + jcol
            if j < 0:
                continue
            fr = j * invf_d
            sr = base_d ** (-(j - 512.0 * w) / 256.0)
            kcos[:, 1024 * w + jcol] = np.cos(fr) * sr
            ksin[:, 1024 * w + jcol] = np.sin(fr) * sr
    A["kcos"] = np.tile(kcos, (2, 1)).astype(BF)
    A["ksin"] = np.tile(ksin, (2, 1)).astype(BF)

    fm = np.zeros((8, 128, CH), np.float32)
    r = np.arange(128)[:, None]
    cc = np.arange(CH)[None, :]
    for jc in range(8):
        d = 512 + cc - 128 * jc - r
        fm[jc] = np.where((d >= 0) & (d <= 512), 0.0, NEG)
    A["fmask"] = fm

    A["dt1"] = (r - cc).astype(np.float32) * np.ones((128, CH), np.float32)
    dtm = np.zeros((4, 128, CH), np.float32)
    for v in range(4):
        val = r - cc + 128 * v
        dtm[v] = np.where(val >= 0, val, 1.0e33).astype(np.float32)
    A["dtm1"] = dtm
    return A


AUX_SPECS = {
    "dist": ([32, L], F32), "selpw": ([32, 1024], BF16),
    "selT": ([128, 256], BF16), "selH": ([128, 2], BF16),
    "selB": ([2, 128], BF16), "prot": ([128, 128], BF16),
    "qcos": ([128, 4 * CH], BF16), "qsin": ([128, 4 * CH], BF16),
    "kcos": ([128, 4096], BF16), "ksin": ([128, 4096], BF16),
    "fmask": ([8, 128, CH], F32), "dt1": ([128, CH], F32),
    "dtm1": ([4, 128, CH], F32),
}

W_SPECS = {
    "L0_attn_W_in": [1024, 2080],
    "L0_attn_conv_kernel": [4, 2080, 1],
    "L0_attn_theta": [1024, 1],
    "L0_attn_decay_slopes": [28, 1],
    "L0_attn_anchor_slopes": [4, 1],
    "L0_attn_score_scale": [32, 1],
    "L0_attn_W_re": [32, 32],
    "L0_attn_W_im": [32, 32],
    "L0_attn_norm_scale": [64, 1],
    "L0_attn_W_out": [1024, 1024],
    "L0_ffn_wqkv": [2, 1024, 1536],
    "L0_ffn_q_scale": [2, 64, 1],
    "L0_ffn_k_scale": [2, 64, 1],
    "L0_ffn_wo": [2, 512, 1024],
    "L0_ffn_ln_g": [2, 1024, 1],
    "L0_ffn_ln_b": [2, 1024, 1],
    "L0_ffn_w1": [2, 1024, 5460],
    "L0_ffn_b1": [2, 5460, 1],
    "L0_ffn_w2": [2, 2730, 1024],
    "L0_ffn_b2": [2, 1024, 1],
    "L1_attn_W_qkv": [1024, 3072],
    "L1_attn_W_out": [1024, 1024],
    "L1_ffn_w1T": [1024, 4096],
    "L1_ffn_b1": [4096, 1],
    "L1_ffn_w2T": [4096, 1024],
    "L1_ffn_b2": [1024, 1],
}

MOD_WEIGHTS = {
    "a0": [n for n in W_SPECS if n.startswith("L0_attn")],
    "f0": [n for n in W_SPECS if n.startswith("L0_ffn")],
    "a1": ["L1_attn_W_qkv", "L1_attn_W_out"],
    "f1": ["L1_ffn_w1T", "L1_ffn_b1", "L1_ffn_w2T", "L1_ffn_b2"],
}
MOD_AUX = {
    "a0": ["dist", "selpw", "selT"],
    "f0": ["selB", "selH", "prot", "qcos", "qsin", "kcos", "ksin", "fmask"],
    "a1": ["dt1", "dtm1"],
    "f1": [],
}


def build_mod_nc(mod):
    nc = bass.Bass("TRN2", target_bir_lowering=False)
    dr = {}
    for n in MOD_WEIGHTS[mod]:
        dr[n] = nc.dram_tensor(n, W_SPECS[n], F32, kind="ExternalInput").ap()
    aux = {}
    for n in MOD_AUX[mod]:
        shape, dt = AUX_SPECS[n]
        aux[n] = nc.dram_tensor("aux_" + n, shape, dt, kind="ExternalInput").ap()
    xin = nc.dram_tensor("xin", [D, L], F32, kind="ExternalInput").ap()
    xout = nc.dram_tensor("xout", [D, L], F32, kind="ExternalOutput").ap()
    xwork = None
    if mod in ("f0", "f1"):
        xwork = nc.dram_tensor("xwork", [D, L], F32, kind="Internal").ap()
    es = ExitStack()
    chain = es.enter_context(nc.semaphore("chain"))
    state = {"val": 0}
    B = build_common(nc, es, chain, state)
    if mod == "a0":
        emit_l0_attn(nc, chain, state, B, dr, aux, xin, xout)
    elif mod == "f0":
        emit_l0_ffn(nc, chain, state, B, dr, aux, xin, xout, xwork)
    elif mod == "a1":
        emit_l1_attn(nc, chain, state, B, dr, aux, xin, xout)
    elif mod == "f1":
        emit_l1_ffn(nc, chain, state, B, dr, aux, xin, xout, xwork)
    tot = state["val"]
    with nc.Block() as blk:
        blk.tensor(lambda e: e.wait_ge(chain, tot))
        blk.vector(lambda e: e.wait_ge(chain, tot))
        blk.scalar(lambda e: e.wait_ge(chain, tot))
        blk.sync(lambda e: e.wait_ge(chain, tot))
        blk.gpsimd(lambda e: e.wait_ge(chain, tot))
    es.close()
    return nc


def kernel_unfused(**inputs):
    inp = {k: np.asarray(v) for k, v in inputs.items()}
    arrs = {}
    arrs["L0_attn_W_in"] = inp["L0_attn_W_in"]
    arrs["L0_attn_conv_kernel"] = np.ascontiguousarray(
        inp["L0_attn_conv_kernel"].reshape(4, 2080, 1))
    arrs["L0_attn_theta"] = np.ascontiguousarray(
        inp["L0_attn_theta"].reshape(1024, 1))
    arrs["L0_attn_decay_slopes"] = inp["L0_attn_decay_slopes"].reshape(28, 1)
    arrs["L0_attn_anchor_slopes"] = inp["L0_attn_anchor_slopes"].reshape(4, 1)
    arrs["L0_attn_score_scale"] = inp["L0_attn_score_scale"].reshape(32, 1)
    arrs["L0_attn_W_re"] = inp["L0_attn_W_re"]
    arrs["L0_attn_W_im"] = inp["L0_attn_W_im"]
    arrs["L0_attn_norm_scale"] = inp["L0_attn_norm_scale"].reshape(64, 1)
    arrs["L0_attn_W_out"] = inp["L0_attn_W_out"]
    arrs["L0_ffn_wqkv"] = inp["L0_ffn_wqkv"]
    arrs["L0_ffn_q_scale"] = inp["L0_ffn_q_scale"].reshape(2, 64, 1)
    arrs["L0_ffn_k_scale"] = inp["L0_ffn_k_scale"].reshape(2, 64, 1)
    arrs["L0_ffn_wo"] = inp["L0_ffn_wo"]
    arrs["L0_ffn_ln_g"] = inp["L0_ffn_ln_g"].reshape(2, 1024, 1)
    arrs["L0_ffn_ln_b"] = inp["L0_ffn_ln_b"].reshape(2, 1024, 1)
    arrs["L0_ffn_w1"] = inp["L0_ffn_w1"]
    arrs["L0_ffn_b1"] = inp["L0_ffn_b1"].reshape(2, 5460, 1)
    arrs["L0_ffn_w2"] = inp["L0_ffn_w2"]
    arrs["L0_ffn_b2"] = inp["L0_ffn_b2"].reshape(2, 1024, 1)
    arrs["L1_attn_W_qkv"] = inp["L1_attn_W_qkv"]
    arrs["L1_attn_W_out"] = inp["L1_attn_W_out"]
    arrs["L1_ffn_w1T"] = np.ascontiguousarray(inp["L1_ffn_w1"].T)
    arrs["L1_ffn_b1"] = inp["L1_ffn_b1"].reshape(4096, 1)
    arrs["L1_ffn_w2T"] = np.ascontiguousarray(inp["L1_ffn_w2"].T)
    arrs["L1_ffn_b2"] = inp["L1_ffn_b2"].reshape(1024, 1)
    arrs = {k: np.ascontiguousarray(v.astype(np.float32)) for k, v in arrs.items()}
    auxarr = build_aux_arrays()

    X = np.ascontiguousarray(
        inp["L0_attn_x"].astype(np.float32).transpose(0, 2, 1))

    mods = ["a0", "f0", "a1", "f1"]
    ncs = {m: build_mod_nc(m) for m in mods}
    cur = X
    for mod in mods:
        nxt = np.empty_like(cur)
        const = {n: arrs[n] for n in MOD_WEIGHTS[mod]}
        const.update({"aux_" + n: auxarr[n] for n in MOD_AUX[mod]})
        for s in range(4):
            in_maps = []
            for cid in range(8):
                d = dict(const)
                d["xin"] = np.ascontiguousarray(cur[cid * 4 + s])
                in_maps.append(d)
            res = run_bass_kernel_spmd(ncs[mod], in_maps, core_ids=list(range(8)))
            for cid in range(8):
                nxt[cid * 4 + s] = res.results[cid]["xout"]
        cur = nxt
    out = np.ascontiguousarray(cur.transpose(0, 2, 1)).astype(np.float32)
    return out


class NCWrap:
    def __init__(self, nc):
        self._nc = nc
        self.tag = ""

    def __getattr__(self, a):
        return getattr(self._nc, a)

    def sbuf_tensor(self, name, shape, dt):
        return self._nc.sbuf_tensor(name + self.tag, shape, dt)

    def psum_tensor(self, name, shape, dt):
        return self._nc.psum_tensor(name + self.tag, shape, dt)


def build_fused_nc():
    nc0 = bass.Bass("TRN2", target_bir_lowering=False)
    nc = NCWrap(nc0)
    dr = {}
    for n, shape in W_SPECS.items():
        dr[n] = nc0.dram_tensor(n, shape, F32, kind="ExternalInput").ap()
    aux = {}
    for n, (shape, dt) in AUX_SPECS.items():
        aux[n] = nc0.dram_tensor("aux_" + n, shape, dt, kind="ExternalInput").ap()
    xin_all = nc0.dram_tensor("xin", [4, D, L], F32, kind="ExternalInput").ap()
    xout_all = nc0.dram_tensor("xout", [4, D, L], F32, kind="ExternalOutput").ap()
    xs1 = nc0.dram_tensor("xs1", [D, L], F32, kind="Internal").ap()
    xs2 = nc0.dram_tensor("xs2", [D, L], F32, kind="Internal").ap()
    xs3 = nc0.dram_tensor("xs3", [D, L], F32, kind="Internal").ap()
    xwork = nc0.dram_tensor("xwork", [D, L], F32, kind="Internal").ap()
    es = ExitStack()
    chain = es.enter_context(nc0.semaphore("chain"))
    state = {"val": 0}
    B = build_common(nc, es, chain, state)
    for s in range(4):
        nc.tag = f"_s{s}a"
        emit_l0_attn(nc, chain, state, B, dr, aux, xin_all[s], xs1)
        nc.tag = f"_s{s}b"
        emit_l0_ffn(nc, chain, state, B, dr, aux, xs1, xs2, xwork)
        nc.tag = f"_s{s}c"
        emit_l1_attn(nc, chain, state, B, dr, aux, xs2, xs3)
        nc.tag = f"_s{s}d"
        emit_l1_ffn(nc, chain, state, B, dr, aux, xs3, xout_all[s], xwork)
    tot = state["val"]
    with nc0.Block() as blk:
        blk.tensor(lambda e: e.wait_ge(chain, tot))
        blk.vector(lambda e: e.wait_ge(chain, tot))
        blk.scalar(lambda e: e.wait_ge(chain, tot))
        blk.sync(lambda e: e.wait_ge(chain, tot))
        blk.gpsimd(lambda e: e.wait_ge(chain, tot))
    es.close()
    return nc0


def _prep(inputs):
    inp = {k: np.asarray(v) for k, v in inputs.items()}
    arrs = {}
    arrs["L0_attn_W_in"] = inp["L0_attn_W_in"]
    arrs["L0_attn_conv_kernel"] = np.ascontiguousarray(
        inp["L0_attn_conv_kernel"].reshape(4, 2080, 1))
    arrs["L0_attn_theta"] = np.ascontiguousarray(
        inp["L0_attn_theta"].reshape(1024, 1))
    arrs["L0_attn_decay_slopes"] = inp["L0_attn_decay_slopes"].reshape(28, 1)
    arrs["L0_attn_anchor_slopes"] = inp["L0_attn_anchor_slopes"].reshape(4, 1)
    arrs["L0_attn_score_scale"] = inp["L0_attn_score_scale"].reshape(32, 1)
    arrs["L0_attn_W_re"] = inp["L0_attn_W_re"]
    arrs["L0_attn_W_im"] = inp["L0_attn_W_im"]
    arrs["L0_attn_norm_scale"] = inp["L0_attn_norm_scale"].reshape(64, 1)
    arrs["L0_attn_W_out"] = inp["L0_attn_W_out"]
    arrs["L0_ffn_wqkv"] = inp["L0_ffn_wqkv"]
    arrs["L0_ffn_q_scale"] = inp["L0_ffn_q_scale"].reshape(2, 64, 1)
    arrs["L0_ffn_k_scale"] = inp["L0_ffn_k_scale"].reshape(2, 64, 1)
    arrs["L0_ffn_wo"] = inp["L0_ffn_wo"]
    arrs["L0_ffn_ln_g"] = inp["L0_ffn_ln_g"].reshape(2, 1024, 1)
    arrs["L0_ffn_ln_b"] = inp["L0_ffn_ln_b"].reshape(2, 1024, 1)
    arrs["L0_ffn_w1"] = inp["L0_ffn_w1"]
    arrs["L0_ffn_b1"] = inp["L0_ffn_b1"].reshape(2, 5460, 1)
    arrs["L0_ffn_w2"] = inp["L0_ffn_w2"]
    arrs["L0_ffn_b2"] = inp["L0_ffn_b2"].reshape(2, 1024, 1)
    arrs["L1_attn_W_qkv"] = inp["L1_attn_W_qkv"]
    arrs["L1_attn_W_out"] = inp["L1_attn_W_out"]
    arrs["L1_ffn_w1T"] = np.ascontiguousarray(inp["L1_ffn_w1"].T)
    arrs["L1_ffn_b1"] = inp["L1_ffn_b1"].reshape(4096, 1)
    arrs["L1_ffn_w2T"] = np.ascontiguousarray(inp["L1_ffn_w2"].T)
    arrs["L1_ffn_b2"] = inp["L1_ffn_b2"].reshape(1024, 1)
    arrs = {k: np.ascontiguousarray(v.astype(np.float32)) for k, v in arrs.items()}
    X = np.ascontiguousarray(
        inp["L0_attn_x"].astype(np.float32).transpose(0, 2, 1))
    return arrs, X


def kernel_fused(**inputs):
    arrs, X = _prep(inputs)
    auxarr = build_aux_arrays()
    nc = build_fused_nc()
    const = dict(arrs)
    const.update({"aux_" + n: auxarr[n] for n in AUX_SPECS})
    in_maps = []
    for cid in range(8):
        d = dict(const)
        d["xin"] = np.ascontiguousarray(X[4 * cid:4 * (cid + 1)])
        in_maps.append(d)
    res = run_bass_kernel_spmd(nc, in_maps, core_ids=list(range(8)))
    out = np.empty_like(X)
    for cid in range(8):
        out[4 * cid:4 * (cid + 1)] = res.results[cid]["xout"]
    return np.ascontiguousarray(out.transpose(0, 2, 1)).astype(np.float32)


def kernel(**inputs):
    return kernel_fused(**inputs)


# version 04
# speedup vs baseline: 1.2827x; 1.2749x over previous
import math
import numpy as np
import ml_dtypes
from contextlib import ExitStack

import concourse.bass as bass
import concourse.mybir as mybir
from concourse.bass_utils import run_bass_kernel_spmd

F32 = mybir.dt.float32
BF16 = mybir.dt.bfloat16
AF = mybir.ActivationFunctionType
OP = mybir.AluOpType
AX = mybir.AxisListType
BF = ml_dtypes.bfloat16

L = 2048
D = 1024
CH = 512
NCH = 4
NEG = -1.0e30
TWO_PI = 2.0 * math.pi
INV_2PI = 1.0 / TWO_PI
MAGIC = 12582912.0


def C(method, *args, **kw):
    return (method, args, kw)


class PL:
    def __init__(self):
        self.steps = []

    def t(self, *calls):
        self.steps.append(("tensor", calls, 1))

    def v(self, *calls):
        self.steps.append(("vector", calls, 1))

    def s(self, *calls):
        self.steps.append(("scalar", calls, 1))

    def ld(self, *calls):
        self.steps.append(("sync", calls, 16 * len(calls)))

    def st(self, *calls):
        self.steps.append(("gpsimd", calls, 16 * len(calls)))


def emit_plan(nc, chain, state, pl):
    pres = []
    tot = state["val"]
    for eng, calls, inc in pl.steps:
        pres.append(tot)
        tot += inc
    state["val"] = tot
    with nc.Block() as blk:
        def mk(engname):
            def body(e):
                for i, (eng, calls, inc) in enumerate(pl.steps):
                    if eng != engname:
                        continue
                    if pres[i] > 0:
                        e.wait_ge(chain, pres[i])
                    insts = [getattr(e, m)(*a, **k) for (m, a, k) in calls]
                    if inc >= 16:
                        for ins in insts:
                            ins.then_inc(chain, 16)
                    else:
                        insts[-1].then_inc(chain, inc)
            return body
        blk.tensor(mk("tensor"))
        blk.vector(mk("vector"))
        blk.scalar(mk("scalar"))
        blk.sync(mk("sync"))
        blk.gpsimd(mk("gpsimd"))


def build_common(nc, es, chain, state):
    B = {}
    sb = lambda name, shape, dt: es.enter_context(nc.sbuf_tensor(name, shape, dt))
    ps = lambda name, shape: es.enter_context(nc.psum_tensor(name, shape, F32))
    B["ones_col_b"] = sb("g_ones_col", [128, 1], BF16)
    B["ones_row_b"] = sb("g_ones_row", [1, 128], BF16)
    B["zeros"] = sb("g_zeros", [128, CH], F32)
    B["scr_a"] = sb("g_scr_a", [128, CH], F32)
    B["scr_b"] = sb("g_scr_b", [128, CH], F32)
    B["scr_c"] = sb("g_scr_c", [128, CH], F32)
    B["scr_bf"] = sb("g_scr_bf", [128, CH], BF16)
    B["row_a"] = sb("g_row_a", [1, L], F32)
    B["row_bf_a"] = sb("g_row_bf_a", [1, CH], BF16)
    B["row_bf_b"] = sb("g_row_bf_b", [1, CH], BF16)
    B["xsum"] = [sb(f"g_xsum{g}", [128, CH], F32) for g in range(8)]
    B["p_main"] = [ps(f"g_pm{i}", [128, CH]) for i in range(4)]
    B["p_bc"] = ps("g_pbc", [128, CH])
    B["p_bc2"] = ps("g_pbc2", [128, CH])
    B["p_row"] = ps("g_prow", [32, CH])
    B["p_row2"] = ps("g_prow2", [32, CH])
    consts = {0.0, 1.0, -1.0, 0.125, 1e-4, 1e-5, 1e-12, 1.0 / 64.0,
              1.0 / 1024.0, -1.0 / 1024.0, 1.0 / 4096.0, INV_2PI, MAGIC,
              MAGIC + 0.25, -TWO_PI, math.pi / 2.0, NEG}
    for i in range(16):
        consts.add(-(2.0 ** (-0.5 * (i + 1))))
    for d_ in range(1, 16):
        consts.add(float(128 * d_))
    consts = sorted(consts)
    ctile = sb("g_consts", [128, len(consts)], F32)
    ctileb = sb("g_constsb", [128, len(consts)], BF16)
    pl = PL()
    pl.v(C("memset", B["ones_col_b"][:, :], 1.0))
    pl.v(C("memset", B["ones_row_b"][:, :], 1.0))
    pl.v(C("memset", B["zeros"][:, :], 0.0))
    for i, vv in enumerate(consts):
        pl.v(C("memset", ctile[:, i:i + 1], float(vv)))
        pl.v(C("memset", ctileb[:, i:i + 1], float(vv)))
        nc.const_aps.register(F32, float(vv), ctile[:, i:i + 1])
        nc.const_aps.register(BF16, float(vv), ctileb[:, i:i + 1])
    emit_plan(nc, chain, state, pl)
    return B


def ln_chunk(pl, B, write_calls_fn):
    xb = B["scr_bf"]
    sums = B["p_row"][0:1, :]
    sqs = B["p_row2"][0:1, :]
    mb = B["p_bc"]
    rb = B["p_bc2"]
    t2 = B["scr_b"]
    ra = B["row_a"]
    for g in range(8):
        xg = B["xsum"][g][:, :]
        pl.v(C("tensor_copy", xb[:, :], xg))
        pl.t(C("matmul", sums, B["ones_col_b"][:, :], xb[:, :],
              start=(g == 0), stop=(g == 7)))
        pl.s(C("activation", xb[:, :], xb[:, :], AF.Square))
        pl.t(C("matmul", sqs, B["ones_col_b"][:, :], xb[:, :],
              start=(g == 0), stop=(g == 7)))
    pl.v(C("tensor_scalar", ra[0:1, 0:CH], sums, -1.0 / 1024.0, None, OP.mult))
    pl.v(C("tensor_scalar", ra[0:1, CH:2 * CH], sqs, 1.0 / 1024.0, None, OP.mult))
    pl.s(C("activation", ra[0:1, 2 * CH:3 * CH], ra[0:1, 0:CH], AF.Square))
    pl.v(C("tensor_sub", ra[0:1, CH:2 * CH], ra[0:1, CH:2 * CH],
          ra[0:1, 2 * CH:3 * CH]))
    pl.s(C("activation", ra[0:1, CH:2 * CH], ra[0:1, CH:2 * CH], AF.Sqrt,
          bias=1e-5))
    pl.v(C("reciprocal", ra[0:1, CH:2 * CH], ra[0:1, CH:2 * CH]))
    pl.v(C("tensor_copy", B["row_bf_a"][:, :], ra[0:1, 0:CH]))
    pl.v(C("tensor_copy", B["row_bf_b"][:, :], ra[0:1, CH:2 * CH]))
    pl.t(C("matmul", mb[:, :], B["ones_row_b"][:, :], B["row_bf_a"][:, :]))
    pl.t(C("matmul", rb[:, :], B["ones_row_b"][:, :], B["row_bf_b"][:, :]))
    for g in range(8):
        pl.v(C("tensor_add", t2[:, :], B["xsum"][g][:, :], mb[:, :]))
        pl.v(C("tensor_mul", t2[:, :], t2[:, :], rb[:, :]))
        for eng, call in write_calls_fn(g, t2):
            getattr(pl, eng)(call)


def emit_l0_attn(nc, chain, state, B, dr, aux, xin, xout):
    ZC = 2080
    es = ExitStack()
    sb = lambda name, shape, dt: es.enter_context(nc.sbuf_tensor(name, shape, dt))
    zb = [sb(f"a_zb{g}", [128, L + 3], BF16) for g in range(16)]
    z16 = sb("a_z16", [32, L + 3], F32)
    gate = [sb(f"a_gate{g}", [128, CH], BF16) for g in range(8)]
    nre = [sb(f"a_nre{g}", [128, CH], F32) for g in range(8)]
    nim = [sb(f"a_nim{g}", [128, CH], F32) for g in range(8)]
    car_re = sb("a_car_re", [128, 8], F32)
    car_im = sb("a_car_im", [128, 8], F32)
    car_dn = sb("a_car_dn", [32, 1], F32)
    pw = sb("a_pw", [32, CH], F32)
    pwb = sb("a_pwb", [32, CH], BF16)
    den = sb("a_den", [32, CH], F32)
    cscb = sb("a_cscb", [32, CH], BF16)
    rates = sb("a_rates", [32, 1], F32)
    ssc_col = sb("a_ssc", [32, 1], F32)
    theta_col = sb("a_th", [128, 8], F32)
    nsre_col = sb("a_nsre", [128, 1], F32)
    nsim_col = sb("a_nsim", [128, 1], F32)
    ck_cols = sb("a_ck", [128, 68], F32)
    blk_re = sb("a_blkre", [128, 128], BF16)
    blk_im = sb("a_blkim", [128, 128], BF16)
    blk32 = sb("a_blk32", [128, 128], F32)
    wt32 = sb("a_wt32", [128, 1024], F32)
    wtb = sb("a_wtb", [128, 1024], BF16)
    sdist = sb("a_sdist", [32, L], F32)
    selpw = sb("a_selpw", [32, 1024], BF16)
    selT = sb("a_selT", [128, 256], BF16)
    tsc = sb("a_tsc", [32, 4], F32)
    es_x = ExitStack()
    sbx = lambda name, shape, dt: es_x.enter_context(nc.sbuf_tensor(name, shape, dt))
    xTb = [sbx(f"a_xtb{g}", [128, L], BF16) for g in range(8)]

    P = B["p_main"]
    scr_a, scr_b, scr_c = B["scr_a"], B["scr_b"], B["scr_c"]
    sqb = B["scr_bf"]
    pl = PL()

    pl.ld(C("dma_start", out=sdist[:, :], in_=aux["dist"][:, :]),
          C("dma_start", out=selpw[:, :], in_=aux["selpw"][:, :]),
          C("dma_start", out=selT[:, :], in_=aux["selT"][:, :]),
          C("dma_start", out=rates[0:28, 0:1], in_=dr["L0_attn_decay_slopes"][:, :]),
          C("dma_start", out=rates[28:32, 0:1], in_=dr["L0_attn_anchor_slopes"][:, :]),
          C("dma_start", out=ssc_col[:, 0:1], in_=dr["L0_attn_score_scale"][:, :]))
    pl.ld(*[C("dma_start", out=theta_col[:, g:g + 1],
              in_=dr["L0_attn_theta"][128 * g:128 * (g + 1), :]) for g in range(8)])
    pl.ld(*([C("dma_start", out=nsre_col[32 * r:32 * (r + 1), 0:1],
               in_=dr["L0_attn_norm_scale"][0:32, :]) for r in range(4)] +
            [C("dma_start", out=nsim_col[32 * r:32 * (r + 1), 0:1],
               in_=dr["L0_attn_norm_scale"][32:64, :]) for r in range(4)]))
    ckcalls = []
    for j in range(4):
        for g in range(17):
            n = min(128, ZC - 128 * g)
            ckcalls.append(C("dma_start", out=ck_cols[0:n, j * 17 + g:j * 17 + g + 1],
                             in_=dr["L0_attn_conv_kernel"][j, 128 * g:128 * g + n, :]))
    pl.ld(*ckcalls)
    pl.v(C("memset", blk32[:, :], 0.0))
    pl.ld(*[C("dma_start", out=blk32[32 * r:32 * (r + 1), 32 * r:32 * (r + 1)],
              in_=dr["L0_attn_W_re"][:, :]) for r in range(4)])
    pl.v(C("tensor_copy", blk_re[:, :], blk32[:, :]))
    pl.v(C("memset", blk32[:, :], 0.0))
    pl.ld(*[C("dma_start", out=blk32[32 * r:32 * (r + 1), 32 * r:32 * (r + 1)],
              in_=dr["L0_attn_W_im"][:, :]) for r in range(4)])
    pl.v(C("tensor_copy", blk_im[:, :], blk32[:, :]))
    pl.s(C("activation", tsc[:, 0:1], rates[:, 0:1], AF.Abs))
    pl.s(C("activation", tsc[:, 1:2], tsc[:, 0:1], AF.Exp, scale=-1.0))
    pl.s(C("activation", tsc[:, 2:3], tsc[:, 1:2], AF.Ln, bias=1.0))
    pl.v(C("tensor_scalar_max", tsc[:, 3:4], rates[:, 0:1], 0.0))
    pl.v(C("tensor_add", rates[:, 0:1], tsc[:, 2:3], tsc[:, 3:4]))
    pl.v(C("tensor_scalar_mul", rates[:, 0:1], rates[:, 0:1], -1.0))
    pl.v(*([C("memset", zb[g][:, 0:3], 0.0) for g in range(16)] +
           [C("memset", z16[:, 0:3], 0.0),
            C("memset", car_re[:, :], 0.0),
            C("memset", car_im[:, :], 0.0),
            C("memset", car_dn[:, :], 0.0)]))
    for g in range(8):
        for c in range(NCH):
            pl.ld(C("dma_start", out=B["scr_a"][:, :],
                    in_=xin[128 * g:128 * (g + 1), CH * c:CH * (c + 1)]))
            pl.v(C("tensor_copy", xTb[g][:, CH * c:CH * (c + 1)],
                  B["scr_a"][:, :]))

    win = dr["L0_attn_W_in"]
    for m in range(17):
        mm = min(128, ZC - 128 * m)
        pl.ld(*[C("dma_start", out=wt32[:, 128 * k:128 * k + mm],
                  in_=win[128 * k:128 * (k + 1), 128 * m:128 * m + mm])
                for k in range(8)])
        pl.v(C("tensor_copy", wtb[:, :], wt32[:, :]))
        pl.t(*[C("matmul", P[c][0:mm, :], wtb[0:128, 128 * k:128 * k + mm],
                 xTb[k][:, CH * c:CH * (c + 1)],
                 start=(k == 0), stop=(k == 7))
               for k in range(8) for c in range(NCH)])
        if m < 16:
            pl.v(*[C("tensor_copy", zb[m][0:mm, 3 + CH * c:3 + CH * (c + 1)],
                     P[c][0:mm, :]) for c in range(NCH)])
        else:
            pl.v(*[C("tensor_copy", z16[0:mm, 3 + CH * c:3 + CH * (c + 1)],
                     P[c][0:mm, :]) for c in range(NCH)])
    emit_plan(nc, chain, state, pl)
    es_x.close()
    gy = [sb(f"a_gy{g}", [128, L], BF16) for g in range(8)]
    pl = PL()

    sq_ps = B["p_row"]
    y_ps = B["p_bc2"]
    for c in range(NCH):
        t0 = CH * c
        pl.v(C("tensor_scalar", scr_a[0:32, :], z16[:, t0:t0 + CH],
              ck_cols[0:32, 16:17], None, OP.mult))
        for j in range(1, 4):
            pl.v(C("scalar_tensor_tensor", scr_a[0:32, :], z16[:, t0 + j:t0 + j + CH],
                  ck_cols[0:32, j * 17 + 16:j * 17 + 17], scr_a[0:32, :],
                  OP.mult, OP.add))
        pl.v(C("tensor_scalar", scr_b[0:32, :], sdist[:, t0:t0 + CH],
              rates[:, 0:1], None, OP.mult))
        pl.v(C("scalar_tensor_tensor", scr_c[0:32, :], scr_a[0:32, :],
              ssc_col[:, 0:1], scr_b[0:32, :], OP.mult, OP.add))
        pl.s(C("activation", pw[:, :], scr_c[0:32, :], AF.Exp))
        pl.v(C("tensor_copy", pwb[:, :], pw[:, :]))
        pl.v(C("tensor_tensor_scan", den[:, :], pw[:, :], B["zeros"][0:32, :],
              car_dn[:, 0:1] if c > 0 else 0.0, OP.add, OP.add))
        pl.v(C("tensor_copy", car_dn[:, 0:1], den[:, CH - 1:CH]))
        for g in range(8):
            zt = zb[8 + g]
            pl.v(C("tensor_scalar", scr_a[:, :], zt[:, t0:t0 + CH],
                  ck_cols[:, 8 + g:9 + g], None, OP.mult))
            for j in range(1, 4):
                pl.v(C("scalar_tensor_tensor", scr_a[:, :], zt[:, t0 + j:t0 + j + CH],
                      ck_cols[:, j * 17 + 8 + g:j * 17 + 9 + g], scr_a[:, :],
                      OP.mult, OP.add))
            pl.s(C("activation", gate[g][:, :], scr_a[:, :], AF.Silu))
        for g in range(8):
            zt = zb[g]
            pl.v(C("tensor_scalar", scr_a[:, :], zt[:, t0:t0 + CH],
                  ck_cols[:, g:g + 1], None, OP.mult))
            for j in range(1, 4):
                pl.v(C("scalar_tensor_tensor", scr_a[:, :], zt[:, t0 + j:t0 + j + CH],
                      ck_cols[:, j * 17 + g:j * 17 + 1 + g], scr_a[:, :],
                      OP.mult, OP.add))
            pl.v(C("tensor_scalar", scr_b[:, :], scr_a[:, :],
                  theta_col[:, g:g + 1], None, OP.mult))
            pl.v(C("tensor_scalar", scr_a[:, :], scr_b[:, :], INV_2PI, MAGIC,
                  OP.mult, OP.add))
            pl.v(C("tensor_scalar", scr_a[:, :], scr_a[:, :], MAGIC, None,
                  OP.subtract))
            pl.v(C("scalar_tensor_tensor", scr_c[:, :], scr_a[:, :], -TWO_PI,
                  scr_b[:, :], OP.mult, OP.add))
            pl.s(C("activation", scr_c[:, :], scr_c[:, :], AF.Sin))
            pl.v(C("tensor_scalar", scr_a[:, :], scr_b[:, :], INV_2PI,
                  MAGIC + 0.25, OP.mult, OP.add))
            pl.v(C("tensor_scalar", scr_a[:, :], scr_a[:, :], MAGIC, None,
                  OP.subtract))
            pl.v(C("scalar_tensor_tensor", scr_a[:, :], scr_a[:, :], -TWO_PI,
                  scr_b[:, :], OP.mult, OP.add))
            pl.v(C("tensor_scalar", scr_a[:, :], scr_a[:, :], math.pi / 2.0,
                  None, OP.add))
            pl.s(C("activation", scr_a[:, :], scr_a[:, :], AF.Sin))
            pl.t(C("matmul", B["p_bc"][:, :], selpw[:, 128 * g:128 * (g + 1)],
                  pwb[:, :]))
            pl.v(C("tensor_mul", scr_a[:, :], scr_a[:, :], B["p_bc"][:, :]))
            pl.v(C("tensor_mul", scr_c[:, :], scr_c[:, :], B["p_bc"][:, :]))
            pl.v(C("tensor_tensor_scan", nre[g][:, :], scr_a[:, :],
                  B["zeros"][:, :], car_re[:, g:g + 1] if c > 0 else 0.0,
                  OP.add, OP.add))
            pl.v(C("tensor_tensor_scan", nim[g][:, :], scr_c[:, :],
                  B["zeros"][:, :], car_im[:, g:g + 1] if c > 0 else 0.0,
                  OP.add, OP.add))
            pl.v(C("tensor_copy", car_re[:, g:g + 1], nre[g][:, CH - 1:CH]),
                 C("tensor_copy", car_im[:, g:g + 1], nim[g][:, CH - 1:CH]))
            pl.s(C("activation", sqb[:, :], nre[g][:, :], AF.Square))
            pl.t(C("matmul", sq_ps[:, :], selT[:, 32 * g:32 * (g + 1)], sqb[:, :],
                  start=(g == 0), stop=False))
            pl.s(C("activation", sqb[:, :], nim[g][:, :], AF.Square))
            pl.t(C("matmul", sq_ps[:, :], selT[:, 32 * g:32 * (g + 1)], sqb[:, :],
                  start=False, stop=(g == 7)))
        pl.v(C("tensor_scalar_max", scr_a[0:32, :], den[:, :], 1e-4))
        pl.v(C("reciprocal", scr_a[0:32, :], scr_a[0:32, :]))
        pl.s(C("activation", scr_b[0:32, :], scr_a[0:32, :], AF.Square))
        pl.v(C("tensor_mul", scr_b[0:32, :], scr_b[0:32, :], sq_ps[:, :]))
        pl.v(C("tensor_scalar", scr_b[0:32, :], scr_b[0:32, :], 1.0 / 64.0,
              1e-5, OP.mult, OP.add))
        pl.s(C("activation", scr_b[0:32, :], scr_b[0:32, :], AF.Sqrt))
        pl.v(C("reciprocal", scr_b[0:32, :], scr_b[0:32, :]))
        pl.v(C("tensor_mul", scr_b[0:32, :], scr_a[0:32, :], scr_b[0:32, :]))
        pl.v(C("tensor_copy", cscb[:, :], scr_b[0:32, :]))
        for g in range(8):
            pl.t(C("matmul", B["p_bc"][:, :], selpw[:, 128 * g:128 * (g + 1)],
                  cscb[:, :]))
            pl.v(C("scalar_tensor_tensor", sqb[:, :], nre[g][:, :],
                  nsre_col[:, 0:1], B["p_bc"][:, :], OP.mult, OP.mult))
            pl.t(C("matmul", y_ps[:, :], blk_re[:, :], sqb[:, :],
                  start=True, stop=False))
            pl.v(C("scalar_tensor_tensor", sqb[:, :], nim[g][:, :],
                  nsim_col[:, 0:1], B["p_bc"][:, :], OP.mult, OP.mult))
            pl.t(C("matmul", y_ps[:, :], blk_im[:, :], sqb[:, :],
                  start=False, stop=True))
            pl.v(C("tensor_mul", gy[g][:, t0:t0 + CH], gate[g][:, :],
                  y_ps[:, :]))

    wo = dr["L0_attn_W_out"]
    for c in range(NCH):
        for g in range(8):
            pl.ld(*[C("dma_start", out=wt32[:, 128 * k:128 * (k + 1)],
                      in_=wo[128 * k:128 * (k + 1), 128 * g:128 * (g + 1)])
                    for k in range(8)])
            pl.v(C("tensor_copy", wtb[:, :], wt32[:, :]))
            pl.t(*[C("matmul", P[g % 4][:, :], wtb[:, 128 * k:128 * (k + 1)],
                     gy[k][:, CH * c:CH * (c + 1)],
                     start=(k == 0), stop=(k == 7)) for k in range(8)])
            pl.ld(C("dma_start", out=scr_a[:, :],
                    in_=xin[128 * g:128 * (g + 1), CH * c:CH * (c + 1)]))
            pl.v(C("tensor_add", B["xsum"][g][:, :], scr_a[:, :], P[g % 4][:, :]))
        def wrt(g, src, c=c):
            return [("st", C("dma_start",
                             out=xout[128 * g:128 * (g + 1), CH * c:CH * (c + 1)],
                             in_=src[:, :]))]
        ln_chunk(pl, B, wrt)

    emit_plan(nc, chain, state, pl)
    es.close()


def emit_l0_ffn(nc, chain, state, B, dr, aux, xin, xout, xwork):
    HID = 2730
    P = B["p_main"]
    scr_a, scr_b, scr_c = B["scr_a"], B["scr_b"], B["scr_c"]

    pl = PL()
    for g in range(8):
        for c in range(NCH):
            pl.ld(C("dma_start", out=scr_a[:, :],
                    in_=xin[128 * g:128 * (g + 1), CH * c:CH * (c + 1)]))
            pl.st(C("dma_start", out=xwork[128 * g:128 * (g + 1), CH * c:CH * (c + 1)],
                    in_=scr_a[:, :]))
    emit_plan(nc, chain, state, pl)

    for lyr in range(2):
        es1 = ExitStack()
        sb1 = lambda name, shape, dt: es1.enter_context(nc.sbuf_tensor(f"{name}_{lyr}", shape, dt))
        vtok = [sb1(f"f_vt{t}", [128, 512], BF16) for t in range(16)]
        q2b = [sb1(f"f_q2b{g}", [128, L], BF16) for g in range(4)]
        k2b = [sb1(f"f_k2b{g}", [128, 4096], BF16) for g in range(4)]
        wt32 = sb1("f_wt32", [128, 1024], F32)
        wtb = sb1("f_wtb", [128, 1024], BF16)
        qs_col = sb1("f_qs", [128, 1], F32)
        qs_rot = sb1("f_qsr", [128, 1], F32)
        ks_col = sb1("f_ks", [128, 1], F32)
        ks_rot = sb1("f_ksr", [128, 1], F32)
        selB = sb1("f_selB", [2, 128], BF16)
        selH = sb1("f_selH", [128, 2], BF16)
        es1a = ExitStack()
        sb1a = lambda name, shape, dt: es1a.enter_context(nc.sbuf_tensor(f"{name}_{lyr}", shape, dt))
        qkb = [sb1a(f"f_qkb{g}", [128, L], BF16) for g in range(8)]
        rin = [sb1a(f"f_rin{i}", [2, L], BF16) for i in range(8)]
        es1a1 = ExitStack()
        shb = [es1a1.enter_context(nc.sbuf_tensor(f"f_shb{g}_{lyr}", [128, L], BF16))
               for g in range(8)]

        pl = PL()
        pl.ld(C("dma_start", out=selB[:, :], in_=aux["selB"][:, :]),
              C("dma_start", out=selH[:, :], in_=aux["selH"][:, :]))
        qsd = dr["L0_ffn_q_scale"]
        ksd = dr["L0_ffn_k_scale"]
        sc_calls = []
        for hl in range(2):
            sc_calls.append(C("dma_start", out=qs_col[64 * hl:64 * hl + 64, 0:1],
                              in_=qsd[lyr, :, :]))
            sc_calls.append(C("dma_start", out=ks_col[64 * hl:64 * hl + 64, 0:1],
                              in_=ksd[lyr, :, :]))
            for hf in range(2):
                sc_calls.append(C("dma_start",
                                  out=qs_rot[64 * hl + 32 * hf:64 * hl + 32 * hf + 32, 0:1],
                                  in_=qsd[lyr, 32 - 32 * hf:64 - 32 * hf, :]))
                sc_calls.append(C("dma_start",
                                  out=ks_rot[64 * hl + 32 * hf:64 * hl + 32 * hf + 32, 0:1],
                                  in_=ksd[lyr, 32 - 32 * hf:64 - 32 * hf, :]))
        pl.ld(*sc_calls)

        for g in range(8):
            for c in range(NCH):
                pl.ld(C("dma_start", out=scr_a[:, :],
                        in_=xwork[128 * g:128 * (g + 1), CH * c:CH * (c + 1)]))
                if g >= 4:
                    if c == 0:
                        pl.v(C("memset", shb[g][:, 0:1], 0.0))
                    else:
                        pl.ld(C("dma_start", out=scr_b[:, 0:1],
                                in_=xwork[128 * g:128 * (g + 1), CH * c - 1:CH * c],
                                allow_slow_non_contiguous=True))
                        pl.v(C("tensor_copy", shb[g][:, CH * c:CH * c + 1],
                              scr_b[:, 0:1]))
                    pl.v(C("tensor_copy", shb[g][:, CH * c + 1:CH * (c + 1)],
                          scr_a[:, 0:CH - 1]))
                else:
                    pl.v(C("tensor_copy", shb[g][:, CH * c:CH * (c + 1)],
                          scr_a[:, :]))

        wq = dr["L0_ffn_wqkv"]
        for m in range(8):
            pl.ld(*[C("dma_start", out=wt32[:, 128 * k:128 * (k + 1)],
                      in_=wq[lyr, 128 * k:128 * (k + 1), 128 * m:128 * (m + 1)])
                    for k in range(8)])
            pl.v(C("tensor_copy", wtb[:, :], wt32[:, :]))
            pl.t(*[C("matmul", P[c][:, :], wtb[:, 128 * k:128 * (k + 1)],
                     shb[k][:, CH * c:CH * (c + 1)],
                     start=(k == 0), stop=(k == 7))
                   for k in range(8) for c in range(NCH)])
            for c in range(NCH):
                pl.v(C("tensor_copy", qkb[m][:, CH * c:CH * (c + 1)], P[c][:, :]))
                pl.s(C("activation", B["scr_bf"][:, :], P[c][:, :], AF.Square))
                pl.t(C("matmul", B["p_row2"][0:2, :], selH[:, :], B["scr_bf"][:, :]))
                pl.s(C("activation", scr_a[0:2, :], B["p_row2"][0:2, :], AF.Sqrt))
                pl.v(C("tensor_scalar_max", scr_a[0:2, :], scr_a[0:2, :], 1e-12))
                pl.v(C("reciprocal", scr_a[0:2, :], scr_a[0:2, :]))
                pl.v(C("tensor_copy", rin[m][:, CH * c:CH * (c + 1)],
                      scr_a[0:2, :]))
        for tg in range(4):
            for kp in range(4):
                pl.ld(*[C("dma_start", out=wt32[:, 512 * h2:512 * (h2 + 1)],
                          in_=wq[lyr, 128 * (2 * kp + h2):128 * (2 * kp + h2 + 1),
                                 1024:1536]) for h2 in range(2)])
                pl.v(C("tensor_copy", wtb[:, :], wt32[:, :]))
                pl.t(*[C("matmul", P[t % 4][:, :],
                         shb[2 * kp + h2][:, 128 * t:128 * (t + 1)],
                         wtb[:, 512 * h2:512 * (h2 + 1)],
                         start=(kp == 0 and h2 == 0), stop=(kp == 3 and h2 == 1))
                       for h2 in range(2) for t in range(4 * tg, 4 * tg + 4)])
            pl.v(*[C("tensor_copy", vtok[t][:, :], P[t % 4][:, :])
                   for t in range(4 * tg, 4 * tg + 4)])
        emit_plan(nc, chain, state, pl)
        es1a1.close()
        es1a2 = ExitStack()
        sb1a2 = lambda name, shape, dt: es1a2.enter_context(nc.sbuf_tensor(f"{name}_{lyr}", shape, dt))
        prot = sb1a2("f_prot", [128, 128], BF16)
        qcosb = sb1a2("f_qcos", [128, 4 * CH], BF16)
        qsinb = sb1a2("f_qsin", [128, 4 * CH], BF16)
        kcosb = sb1a2("f_kcos", [128, 4096], BF16)
        ksinb = sb1a2("f_ksin", [128, 4096], BF16)
        pl = PL()
        pl.ld(C("dma_start", out=prot[:, :], in_=aux["prot"][:, :]),
              C("dma_start", out=qcosb[:, :], in_=aux["qcos"][:, :]),
              C("dma_start", out=qsinb[:, :], in_=aux["qsin"][:, :]),
              C("dma_start", out=kcosb[:, :], in_=aux["kcos"][:, :]),
              C("dma_start", out=ksinb[:, :], in_=aux["ksin"][:, :]))
        for g in range(4):
            for w in range(4):
                cw = CH * w
                pl.t(C("matmul", B["p_bc"][:, :], prot[:, :], qkb[g][:, cw:cw + CH]))
                pl.t(C("matmul", B["p_bc2"][:, :], selB[:, :],
                      rin[g][:, cw:cw + CH]))
                pl.v(C("scalar_tensor_tensor", scr_a[:, :], qkb[g][:, cw:cw + CH],
                      qs_col[:, 0:1], qcosb[:, cw:cw + CH], OP.mult, OP.mult))
                pl.v(C("scalar_tensor_tensor", scr_b[:, :], B["p_bc"][:, :],
                      qs_rot[:, 0:1], qsinb[:, cw:cw + CH], OP.mult, OP.mult))
                pl.v(C("tensor_add", scr_a[:, :], scr_a[:, :], scr_b[:, :]))
                pl.v(C("tensor_mul", q2b[g][:, cw:cw + CH], scr_a[:, :],
                      B["p_bc2"][:, :]))
                for hf in range(2):
                    j0 = cw - CH + hf * CH
                    if j0 < 0:
                        continue
                    kc = 1024 * w + CH * hf
                    pl.t(C("matmul", B["p_bc"][:, :], prot[:, :],
                          qkb[4 + g][:, j0:j0 + CH]))
                    pl.t(C("matmul", B["p_bc2"][:, :], selB[:, :],
                          rin[4 + g][:, j0:j0 + CH]))
                    pl.v(C("scalar_tensor_tensor", scr_a[:, :],
                          qkb[4 + g][:, j0:j0 + CH], ks_col[:, 0:1],
                          kcosb[:, kc:kc + CH], OP.mult, OP.mult))
                    pl.v(C("scalar_tensor_tensor", scr_b[:, :], B["p_bc"][:, :],
                          ks_rot[:, 0:1], ksinb[:, kc:kc + CH], OP.mult, OP.mult))
                    pl.v(C("tensor_add", scr_a[:, :], scr_a[:, :], scr_b[:, :]))
                    pl.v(C("tensor_mul", k2b[g][:, kc:kc + CH], scr_a[:, :],
                          B["p_bc2"][:, :]))
        emit_plan(nc, chain, state, pl)
        es1a2.close()
        es1a.close()
        es1b = ExitStack()
        sb1b = lambda name, shape, dt: es1b.enter_context(nc.sbuf_tensor(f"{name}_{lyr}", shape, dt))
        Pb = sb1b("f_Pb", [128, 8 * CH], BF16)
        maskm = [sb1b(f"f_mk{j}", [128, CH], F32) for j in range(8)]
        nrow = sb1b("f_nrow", [1, 1024], F32)
        oT = [sb1b(f"f_oT{g}", [128, L], BF16) for g in range(4)]
        pl = PL()
        pl.ld(*[C("dma_start", out=maskm[j][:, :], in_=aux["fmask"][j, :, :])
                for j in range(8)])
        for h in range(8):
            g = h // 2
            r0 = 64 * (h % 2)
            for w in range(4):
                cw = CH * w
                jcs = list(range(8)) if w > 0 else list(range(4, 8))
                pl.s(C("activation", B["scr_bf"][0:64, :],
                      q2b[g][r0:r0 + 64, cw:cw + CH], AF.Square))
                pl.t(C("matmul", B["p_row"][0:1, :], B["ones_col_b"][0:64, :],
                      B["scr_bf"][0:64, :]))
                for idx, jc in enumerate(jcs):
                    pl.s(C("activation", B["scr_bf"][0:64, 0:128],
                          k2b[g][r0:r0 + 64, 1024 * w + 128 * jc:1024 * w + 128 * jc + 128],
                          AF.Square))
                    pl.t(C("matmul", B["p_row2"][0:1, 0:128],
                          B["ones_col_b"][0:64, :], B["scr_bf"][0:64, 0:128]))
                    pl.v(C("tensor_copy", nrow[0:1, 128 * idx:128 * (idx + 1)],
                          B["p_row2"][0:1, 0:128]))
                nj = len(jcs) * 128
                pl.v(C("reduce_max", scr_c[0:1, 0:1], nrow[0:1, 0:nj], AX.X))
                pl.v(C("tensor_scalar", scr_b[0:1, :], B["p_row"][0:1, :],
                      scr_c[0:1, 0:1], None, OP.mult))
                pl.s(C("activation", scr_b[0:1, :], scr_b[0:1, :], AF.Sqrt))
                pl.v(C("tensor_scalar_mul", scr_b[0:1, :], scr_b[0:1, :], -1.0))
                pl.v(C("tensor_copy", B["row_bf_a"][:, :], scr_b[0:1, :]))
                pl.t(C("matmul", B["p_bc2"][:, :], B["ones_row_b"][:, :],
                      B["row_bf_a"][:, :]))
                for jc in jcs:
                    pl.t(C("matmul", P[jc % 4][:, :],
                          k2b[g][r0:r0 + 64, 1024 * w + 128 * jc:1024 * w + 128 * jc + 128],
                          q2b[g][r0:r0 + 64, cw:cw + CH]))
                    pl.v(C("tensor_add", scr_a[:, :], maskm[jc][:, :],
                          B["p_bc2"][:, :]))
                    pl.v(C("tensor_add", scr_a[:, :], scr_a[:, :],
                          P[jc % 4][:, :]))
                    pl.s(C("activation", Pb[:, CH * jc:CH * (jc + 1)],
                          scr_a[:, :], AF.Exp))
                pl.t(*[C("matmul", B["p_row"][0:1, :], B["ones_col_b"][:, :],
                         Pb[:, CH * jc:CH * (jc + 1)],
                         start=(i == 0), stop=(i == len(jcs) - 1))
                       for i, jc in enumerate(jcs)])
                pl.v(C("reciprocal", scr_c[0:1, :], B["p_row"][0:1, :]))
                pl.v(C("tensor_copy", B["row_bf_b"][:, :], scr_c[0:1, :]))
                pl.t(*[C("matmul", B["p_bc"][0:64, :],
                         vtok[4 * (w - 1) + jc][:, 64 * h:64 * (h + 1)],
                         Pb[:, CH * jc:CH * (jc + 1)],
                         start=(i == 0), stop=(i == len(jcs) - 1))
                       for i, jc in enumerate(jcs)])
                pl.t(C("matmul", B["p_bc2"][0:64, :], B["ones_row_b"][:, 0:64],
                      B["row_bf_b"][:, :]))
                pl.v(C("tensor_copy", scr_a[0:64, :], B["p_bc"][0:64, :]))
                pl.v(C("tensor_mul", oT[g][r0:r0 + 64, cw:cw + CH],
                      scr_a[0:64, :], B["p_bc2"][0:64, :]))
        wod = dr["L0_ffn_wo"]
        for m in range(8):
            pl.ld(*[C("dma_start", out=wt32[:, 128 * k:128 * (k + 1)],
                      in_=wod[lyr, 128 * k:128 * (k + 1), 128 * m:128 * (m + 1)])
                    for k in range(4)])
            pl.v(C("tensor_copy", wtb[:, 0:512], wt32[:, 0:512]))
            pl.t(*[C("matmul", P[c][:, :], wtb[:, 128 * k:128 * (k + 1)],
                     oT[k][:, CH * c:CH * (c + 1)],
                     start=(k == 0), stop=(k == 3))
                   for k in range(4) for c in range(NCH)])
            pl.ld(*[C("dma_start", out=B["xsum"][c][:, :],
                      in_=xwork[128 * m:128 * (m + 1), CH * c:CH * (c + 1)])
                    for c in range(NCH)])
            pl.v(*[C("tensor_add", B["xsum"][c][:, :], B["xsum"][c][:, :],
                     P[c][:, :]) for c in range(NCH)])
            pl.st(*[C("dma_start",
                      out=xwork[128 * m:128 * (m + 1), CH * c:CH * (c + 1)],
                      in_=B["xsum"][c][:, :]) for c in range(NCH)])
        emit_plan(nc, chain, state, pl)
        es1b.close()
        es1.close()

        es2 = ExitStack()
        sb2 = lambda name, shape, dt: es2.enter_context(nc.sbuf_tensor(f"{name}_{lyr}", shape, dt))
        hnb = [sb2(f"f_hn{g}", [128, L], BF16) for g in range(8)]
        fb = [sb2(f"f_fb{m}", [128, L], BF16) for m in range(22)]
        wt32 = sb2("f_wt32b", [128, 2816], F32)
        wtb = sb2("f_wtbb", [128, 2816], BF16)
        sgl = sb2("f_sgl", [128, 2048], F32)
        lg_col = sb2("f_lg", [128, 8], F32)
        lb_col = sb2("f_lb", [128, 8], F32)
        b2_col = sb2("f_b2", [128, 8], F32)
        b1a_col = sb2("f_b1a", [128, 22], F32)
        b1g_col = sb2("f_b1g", [128, 22], F32)
        pl = PL()
        pl.ld(*([C("dma_start", out=lg_col[:, g:g + 1],
                   in_=dr["L0_ffn_ln_g"][lyr, 128 * g:128 * (g + 1), :])
                 for g in range(8)] +
                [C("dma_start", out=lb_col[:, g:g + 1],
                   in_=dr["L0_ffn_ln_b"][lyr, 128 * g:128 * (g + 1), :])
                 for g in range(8)] +
                [C("dma_start", out=b2_col[:, g:g + 1],
                   in_=dr["L0_ffn_b2"][lyr, 128 * g:128 * (g + 1), :])
                 for g in range(8)]))
        MT = [(m, min(128, HID - 128 * m)) for m in range(22)]
        pl.ld(*([C("dma_start", out=b1a_col[0:mm, m:m + 1],
                   in_=dr["L0_ffn_b1"][lyr, 128 * m:128 * m + mm, :])
                 for m, mm in MT] +
                [C("dma_start", out=b1g_col[0:mm, m:m + 1],
                   in_=dr["L0_ffn_b1"][lyr, HID + 128 * m:HID + 128 * m + mm, :])
                 for m, mm in MT]))

        for c in range(NCH):
            for g in range(8):
                pl.ld(C("dma_start", out=scr_a[:, :],
                        in_=xwork[128 * g:128 * (g + 1), CH * c:CH * (c + 1)]))
                if g >= 4:
                    if c == 0:
                        pl.v(C("memset", B["xsum"][g][:, 0:1], 0.0))
                    else:
                        pl.ld(C("dma_start", out=scr_b[:, 0:1],
                                in_=xwork[128 * g:128 * (g + 1), CH * c - 1:CH * c],
                                allow_slow_non_contiguous=True))
                        pl.v(C("tensor_copy", B["xsum"][g][:, 0:1], scr_b[:, 0:1]))
                    pl.v(C("tensor_copy", B["xsum"][g][:, 1:CH], scr_a[:, 0:CH - 1]))
                else:
                    pl.v(C("tensor_copy", B["xsum"][g][:, :], scr_a[:, :]))
            def wr_hn(g, src, c=c):
                return [("v", C("tensor_scalar", hnb[g][:, CH * c:CH * (c + 1)],
                                src[:, :], lg_col[:, g:g + 1], lb_col[:, g:g + 1],
                                OP.mult, OP.add))]
            ln_chunk(pl, B, wr_hn)

        w1d = dr["L0_ffn_w1"]
        for m, mm in MT:
            pl.ld(*[C("dma_start", out=wt32[:, 128 * k:128 * k + mm],
                      in_=w1d[lyr, 128 * k:128 * (k + 1),
                              HID + 128 * m:HID + 128 * m + mm])
                    for k in range(8)])
            pl.v(C("tensor_copy", wtb[:, 0:1024], wt32[:, 0:1024]))
            pl.t(*[C("matmul", P[c][0:mm, :], wtb[:, 128 * k:128 * k + mm],
                     hnb[k][:, CH * c:CH * (c + 1)],
                     start=(k == 0), stop=(k == 7))
                   for k in range(8) for c in range(NCH)])
            pl.v(*[C("tensor_scalar", sgl[0:mm, CH * c:CH * (c + 1)],
                     P[c][0:mm, :], 1.0, b1g_col[0:mm, m:m + 1],
                     OP.mult, OP.add) for c in range(NCH)])
            pl.s(*[C("activation", sgl[0:mm, CH * c:CH * (c + 1)],
                     sgl[0:mm, CH * c:CH * (c + 1)], AF.Gelu)
                   for c in range(NCH)])
            pl.v(*[C("tensor_copy", fb[m][0:mm, CH * c:CH * (c + 1)],
                     sgl[0:mm, CH * c:CH * (c + 1)]) for c in range(NCH)])
        for m, mm in MT:
            pl.ld(*[C("dma_start", out=wt32[:, 128 * k:128 * k + mm],
                      in_=w1d[lyr, 128 * k:128 * (k + 1), 128 * m:128 * m + mm])
                    for k in range(8)])
            pl.v(C("tensor_copy", wtb[:, 0:1024], wt32[:, 0:1024]))
            pl.t(*[C("matmul", P[c][0:mm, :], wtb[:, 128 * k:128 * k + mm],
                     hnb[k][:, CH * c:CH * (c + 1)],
                     start=(k == 0), stop=(k == 7))
                   for k in range(8) for c in range(NCH)])
            pl.v(*[C("tensor_scalar", sgl[0:mm, CH * c:CH * (c + 1)],
                     P[c][0:mm, :], 1.0, b1a_col[0:mm, m:m + 1],
                     OP.mult, OP.add) for c in range(NCH)])
            pl.v(*[C("tensor_mul", fb[m][0:mm, CH * c:CH * (c + 1)],
                     sgl[0:mm, CH * c:CH * (c + 1)],
                     fb[m][0:mm, CH * c:CH * (c + 1)]) for c in range(NCH)])
        w2d = dr["L0_ffn_w2"]
        for dt_ in range(8):
            pl.ld(*[C("dma_start", out=wt32[0:mm, 128 * m:128 * (m + 1)],
                      in_=w2d[lyr, 128 * m:128 * m + mm,
                              128 * dt_:128 * (dt_ + 1)])
                    for m, mm in MT])
            pl.v(C("tensor_copy", wtb[:, :], wt32[:, :]))
            pl.t(*[C("matmul", P[c][:, :], wtb[0:mm, 128 * m:128 * (m + 1)],
                     fb[m][0:mm, CH * c:CH * (c + 1)],
                     start=(m == 0), stop=(m == 21))
                   for m, mm in MT for c in range(NCH)])
            pl.ld(*[C("dma_start", out=B["xsum"][c][:, :],
                      in_=xwork[128 * dt_:128 * (dt_ + 1), CH * c:CH * (c + 1)])
                    for c in range(NCH)])
            pl.v(*[C("tensor_scalar", sgl[:, CH * c:CH * (c + 1)], P[c][:, :],
                     1.0, b2_col[:, dt_:dt_ + 1], OP.mult, OP.add)
                   for c in range(NCH)])
            pl.v(*[C("tensor_add", B["xsum"][c][:, :], B["xsum"][c][:, :],
                     sgl[:, CH * c:CH * (c + 1)]) for c in range(NCH)])
            pl.st(*[C("dma_start",
                      out=xwork[128 * dt_:128 * (dt_ + 1), CH * c:CH * (c + 1)],
                      in_=B["xsum"][c][:, :]) for c in range(NCH)])
        emit_plan(nc, chain, state, pl)
        es2.close()

    pl = PL()
    for c in range(NCH):
        for g in range(8):
            pl.ld(C("dma_start", out=scr_a[:, :],
                    in_=xin[128 * g:128 * (g + 1), CH * c:CH * (c + 1)]))
            pl.ld(C("dma_start", out=scr_b[:, :],
                    in_=xwork[128 * g:128 * (g + 1), CH * c:CH * (c + 1)]))
            pl.v(C("tensor_add", B["xsum"][g][:, :], scr_a[:, :], scr_b[:, :]))
        def wrt(g, src, c=c):
            return [("st", C("dma_start",
                             out=xout[128 * g:128 * (g + 1), CH * c:CH * (c + 1)],
                             in_=src[:, :]))]
        ln_chunk(pl, B, wrt)
    emit_plan(nc, chain, state, pl)


def emit_l1_attn(nc, chain, state, B, dr, aux, xin, xout):
    es = ExitStack()
    sb = lambda name, shape, dt: es.enter_context(nc.sbuf_tensor(name, shape, dt))
    xTb = [sb(f"b_xtb{g}", [128, L], BF16) for g in range(8)]
    qb = [sb(f"b_qb{g}", [128, L], BF16) for g in range(8)]
    kb = [sb(f"b_kb{g}", [128, L], BF16) for g in range(8)]
    vtok = [sb(f"b_vt{t}", [128, 1024], BF16) for t in range(16)]
    Pb = sb("b_Pb", [128, 16 * CH], BF16)
    wt32 = sb("b_wt32", [128, 1024], F32)
    wtb = sb("b_wtb", [128, 1024], BF16)
    dtp = sb("b_dtp", [128, CH], F32)
    dtm = [sb(f"b_dtm{v}", [128, CH], F32) for v in range(4)]
    nrow = sb("b_nrow", [1, L], F32)
    mk2 = sb("b_mk2", [1, 16], F32)
    oT = xTb

    P = B["p_main"]
    scr_a, scr_b, scr_c = B["scr_a"], B["scr_b"], B["scr_c"]
    slopes = [2.0 ** (-0.5 * (i + 1)) for i in range(16)]

    pl = PL()
    pl.ld(C("dma_start", out=dtp[:, :], in_=aux["dt1"][:, :]))
    pl.ld(*[C("dma_start", out=dtm[v][:, :], in_=aux["dtm1"][v, :, :])
            for v in range(4)])
    for g in range(8):
        for c in range(NCH):
            pl.ld(C("dma_start", out=scr_a[:, :],
                    in_=xin[128 * g:128 * (g + 1), CH * c:CH * (c + 1)]))
            pl.v(C("tensor_copy", xTb[g][:, CH * c:CH * (c + 1)], scr_a[:, :]))
    wqk = dr["L1_attn_W_qkv"]
    for part, dst, fold in ((0, qb, 0.125), (1, kb, 1.0)):
        for m in range(8):
            for k in range(8):
                pl.ld(C("dma_start", out=wt32[:, 0:128],
                        in_=wqk[128 * k:128 * (k + 1),
                                1024 * part + 128 * m:1024 * part + 128 * (m + 1)]))
                pl.v(C("tensor_copy", wtb[:, 0:128], wt32[:, 0:128]))
                pl.t(*[C("matmul", P[c][:, :], wtb[:, 0:128],
                         xTb[k][:, CH * c:CH * (c + 1)],
                         start=(k == 0), stop=(k == 7)) for c in range(NCH)])
            for c in range(NCH):
                pl.v(C("tensor_scalar", dst[m][:, CH * c:CH * (c + 1)],
                      P[c][:, :], fold, None, OP.mult))
    for tg in range(4):
        for hf in range(2):
            for k in range(8):
                pl.ld(C("dma_start", out=wt32[:, 0:512],
                        in_=wqk[128 * k:128 * (k + 1),
                                2048 + 512 * hf:2048 + 512 * (hf + 1)]))
                pl.v(C("tensor_copy", wtb[:, 0:512], wt32[:, 0:512]))
                pl.t(*[C("matmul", P[t % 4][:, :],
                         xTb[k][:, 128 * t:128 * (t + 1)],
                         wtb[:, 0:512],
                         start=(k == 0), stop=(k == 7))
                       for t in range(4 * tg, 4 * tg + 4)])
            pl.v(*[C("tensor_copy", vtok[t][:, 512 * hf:512 * (hf + 1)],
                     P[t % 4][:, :]) for t in range(4 * tg, 4 * tg + 4)])
    emit_plan(nc, chain, state, pl)
    pl = PL()

    for h in range(16):
        g, r0 = h // 2, 64 * (h % 2)
        for c in range(NCH):
            pl.s(C("activation", B["scr_bf"][0:64, :],
                  kb[g][r0:r0 + 64, CH * c:CH * (c + 1)], AF.Square))
            pl.t(C("matmul", B["p_row"][0:1, :], B["ones_col_b"][0:64, :],
                  B["scr_bf"][0:64, :]))
            pl.v(C("tensor_copy", nrow[0:1, CH * c:CH * (c + 1)],
                  B["p_row"][0:1, :]))
        pl.v(C("reduce_max", mk2[0:1, h:h + 1], nrow[0:1, :], AX.X))

    for h in range(16):
        g, r0 = h // 2, 64 * (h % 2)
        sl = slopes[h]
        for ic in range(NCH):
            c0 = CH * ic
            jts = list(range(4 * ic, 16))
            pl.s(C("activation", B["scr_bf"][0:64, :],
                  qb[g][r0:r0 + 64, c0:c0 + CH], AF.Square))
            pl.t(C("matmul", B["p_row"][0:1, :], B["ones_col_b"][0:64, :],
                  B["scr_bf"][0:64, :]))
            pl.v(C("tensor_scalar", scr_b[0:1, :], B["p_row"][0:1, :],
                  mk2[0:1, h:h + 1], None, OP.mult))
            pl.s(C("activation", scr_b[0:1, :], scr_b[0:1, :], AF.Sqrt))
            pl.v(C("tensor_scalar_mul", scr_b[0:1, :], scr_b[0:1, :], -1.0))
            pl.v(C("tensor_copy", B["row_bf_a"][:, :], scr_b[0:1, :]))
            pl.t(C("matmul", B["p_bc2"][:, :], B["ones_row_b"][:, :],
                  B["row_bf_a"][:, :]))
            for jt in jts:
                v = jt - 4 * ic
                pl.t(C("matmul", P[jt % 4][:, :],
                      kb[g][r0:r0 + 64, 128 * jt:128 * (jt + 1)],
                      qb[g][r0:r0 + 64, c0:c0 + CH]))
                if v < 4:
                    pl.v(C("tensor_scalar", scr_a[:, :], dtm[v][:, :], -sl,
                          None, OP.mult))
                else:
                    pl.v(C("tensor_scalar", scr_a[:, :], dtp[:, :],
                          float(128 * jt - 512 * ic), -sl, OP.add, OP.mult))
                pl.v(C("tensor_add", scr_a[:, :], scr_a[:, :], B["p_bc2"][:, :]))
                pl.v(C("tensor_add", scr_a[:, :], scr_a[:, :], P[jt % 4][:, :]))
                pl.s(C("activation", Pb[:, CH * jt:CH * (jt + 1)], scr_a[:, :],
                      AF.Exp))
            pl.t(*[C("matmul", B["p_row"][0:1, :], B["ones_col_b"][:, :],
                     Pb[:, CH * jt:CH * (jt + 1)],
                     start=(i == 0), stop=(i == len(jts) - 1))
                   for i, jt in enumerate(jts)])
            pl.v(C("reciprocal", scr_c[0:1, :], B["p_row"][0:1, :]))
            pl.v(C("tensor_copy", B["row_bf_b"][:, :], scr_c[0:1, :]))
            pl.t(*[C("matmul", B["p_bc"][0:64, :],
                     vtok[jt][:, 64 * h:64 * (h + 1)],
                     Pb[:, CH * jt:CH * (jt + 1)],
                     start=(i == 0), stop=(i == len(jts) - 1))
                   for i, jt in enumerate(jts)])
            pl.t(C("matmul", B["p_bc2"][0:64, :], B["ones_row_b"][:, 0:64],
                  B["row_bf_b"][:, :]))
            pl.v(C("tensor_copy", scr_a[0:64, :], B["p_bc"][0:64, :]))
            pl.v(C("tensor_mul", oT[g][r0:r0 + 64, c0:c0 + CH],
                  scr_a[0:64, :], B["p_bc2"][0:64, :]))

    wo = dr["L1_attn_W_out"]
    for c in range(NCH):
        for g in range(8):
            pl.ld(*[C("dma_start", out=wt32[:, 128 * k:128 * (k + 1)],
                      in_=wo[128 * k:128 * (k + 1), 128 * g:128 * (g + 1)])
                    for k in range(8)])
            pl.v(C("tensor_copy", wtb[:, :], wt32[:, :]))
            pl.t(*[C("matmul", P[g % 4][:, :], wtb[:, 128 * k:128 * (k + 1)],
                     oT[k][:, CH * c:CH * (c + 1)],
                     start=(k == 0), stop=(k == 7)) for k in range(8)])
            pl.ld(C("dma_start", out=scr_a[:, :],
                    in_=xin[128 * g:128 * (g + 1), CH * c:CH * (c + 1)]))
            pl.v(C("tensor_add", B["xsum"][g][:, :], scr_a[:, :], P[g % 4][:, :]))
        def wrt(g, src, c=c):
            return [("st", C("dma_start",
                             out=xout[128 * g:128 * (g + 1), CH * c:CH * (c + 1)],
                             in_=src[:, :]))]
        ln_chunk(pl, B, wrt)
    emit_plan(nc, chain, state, pl)
    es.close()


def emit_l1_ffn(nc, chain, state, B, dr, aux, xin, xout, xwork):
    es = ExitStack()
    sb = lambda name, shape, dt: es.enter_context(nc.sbuf_tensor(name, shape, dt))
    xTb = [sb(f"c_xtb{g}", [128, L], BF16) for g in range(8)]
    fb = [sb(f"c_fb{m}", [128, L], BF16) for m in range(32)]
    wt32 = sb("c_wt32", [128, 1024], F32)
    absb = sb("c_absb", [128, 1024], BF16)
    sg = [sb(f"c_sg{k}", [128, 128], BF16) for k in range(8)]
    al_row = sb("c_alrow", [1, 128], F32)
    al_col = sb("c_alcol", [128, 1], F32)
    b_col = sb("c_bcol", [128, 1], F32)

    P = B["p_main"]
    scr_a, scr_b = B["scr_a"], B["scr_b"]
    pl = PL()
    for g in range(8):
        for c in range(NCH):
            pl.ld(C("dma_start", out=scr_a[:, :],
                    in_=xin[128 * g:128 * (g + 1), CH * c:CH * (c + 1)]))
            pl.v(C("tensor_copy", xTb[g][:, CH * c:CH * (c + 1)], scr_a[:, :]))

    w1t = dr["L1_ffn_w1T"]
    for m in range(32):
        pl.ld(*[C("dma_start", out=wt32[:, 128 * k:128 * (k + 1)],
                  in_=w1t[128 * k:128 * (k + 1), 128 * m:128 * (m + 1)])
                for k in range(8)])
        pl.s(C("activation", absb[:, :], wt32[:, :], AF.Abs))
        pl.t(*[C("matmul", B["p_row"][0:1, 0:128], B["ones_col_b"][:, :],
                 absb[:, 128 * k:128 * (k + 1)], start=(k == 0), stop=(k == 7))
               for k in range(8)])
        pl.s(*[C("activation", sg[k][:, :], wt32[:, 128 * k:128 * (k + 1)],
                 AF.Sign) for k in range(8)])
        pl.v(C("tensor_scalar", al_row[:, :], B["p_row"][0:1, 0:128],
              1.0 / 1024.0, None, OP.mult))
        pl.st(C("dma_start", out=al_col[:, 0:1], in_=al_row[0:1, :]))
        pl.ld(C("dma_start", out=b_col[:, 0:1],
                in_=dr["L1_ffn_b1"][128 * m:128 * (m + 1), :]))
        pl.t(*[C("matmul", P[c][:, :], sg[k][:, :],
                 xTb[k][:, CH * c:CH * (c + 1)],
                 start=(k == 0), stop=(k == 7))
               for k in range(8) for c in range(NCH)])
        pl.v(*[C("tensor_scalar", B["xsum"][c][:, :], P[c][:, :],
                 al_col[:, 0:1], b_col[:, 0:1], OP.mult, OP.add)
               for c in range(NCH)])
        pl.s(*[C("activation", fb[m][:, CH * c:CH * (c + 1)],
                 B["xsum"][c][:, :], AF.Gelu) for c in range(NCH)])

    w2t = dr["L1_ffn_w2T"]
    for dt_ in range(8):
        for grp in range(4):
            ms = list(range(8 * grp, 8 * grp + 8))
            pl.ld(*[C("dma_start", out=wt32[:, 128 * i:128 * (i + 1)],
                      in_=w2t[128 * m:128 * (m + 1), 128 * dt_:128 * (dt_ + 1)])
                    for i, m in enumerate(ms)])
            pl.s(C("activation", absb[:, :], wt32[:, :], AF.Abs))
            pl.t(*[C("matmul", B["p_row"][0:1, 0:128], B["ones_col_b"][:, :],
                     absb[:, 128 * i:128 * (i + 1)],
                     start=(m == 0), stop=(m == 31))
                   for i, m in enumerate(ms)])
            pl.s(*[C("activation", sg[i][:, :], wt32[:, 128 * i:128 * (i + 1)],
                     AF.Sign) for i in range(8)])
            pl.t(*[C("matmul", P[c][:, :], sg[i][:, :],
                     fb[m][:, CH * c:CH * (c + 1)],
                     start=(m == 0), stop=(m == 31))
                   for i, m in enumerate(ms) for c in range(NCH)])
        pl.v(C("tensor_scalar", al_row[:, :], B["p_row"][0:1, 0:128],
              1.0 / 4096.0, None, OP.mult))
        pl.st(C("dma_start", out=al_col[:, 0:1], in_=al_row[0:1, :]))
        pl.ld(C("dma_start", out=b_col[:, 0:1],
                in_=dr["L1_ffn_b2"][128 * dt_:128 * (dt_ + 1), :]))
        pl.v(*[C("tensor_scalar", B["xsum"][c][:, :], P[c][:, :],
                 al_col[:, 0:1], b_col[:, 0:1], OP.mult, OP.add)
               for c in range(NCH)])
        pl.st(*[C("dma_start",
                  out=xwork[128 * dt_:128 * (dt_ + 1), CH * c:CH * (c + 1)],
                  in_=B["xsum"][c][:, :]) for c in range(NCH)])
    for c in range(NCH):
        for g in range(8):
            pl.ld(C("dma_start", out=scr_a[:, :],
                    in_=xin[128 * g:128 * (g + 1), CH * c:CH * (c + 1)]))
            pl.ld(C("dma_start", out=scr_b[:, :],
                    in_=xwork[128 * g:128 * (g + 1), CH * c:CH * (c + 1)]))
            pl.v(C("tensor_add", B["xsum"][g][:, :], scr_a[:, :], scr_b[:, :]))
        def wrt(g, src, c=c):
            return [("st", C("dma_start",
                             out=xout[128 * g:128 * (g + 1), CH * c:CH * (c + 1)],
                             in_=src[:, :]))]
        ln_chunk(pl, B, wrt)
    emit_plan(nc, chain, state, pl)
    es.close()


def build_aux_arrays():
    A = {}
    t = np.arange(L, dtype=np.float64)
    dist = np.empty((32, L), np.float32)
    dist[:28] = (L - 1.0) - t
    dist[28:] = t
    A["dist"] = dist

    selpw = np.zeros((32, 1024), np.float32)
    for g in range(8):
        for col in range(128):
            selpw[4 * g + col // 32, 128 * g + col] = 1.0
    A["selpw"] = selpw.astype(BF)

    selT = np.zeros((128, 256), np.float32)
    for g in range(8):
        for p in range(128):
            selT[p, 32 * g + 4 * g + p // 32] = 1.0
    A["selT"] = selT.astype(BF)

    selH = np.zeros((128, 2), np.float32)
    selH[0:64, 0] = 1.0
    selH[64:128, 1] = 1.0
    A["selH"] = selH.astype(BF)
    selB = np.zeros((2, 128), np.float32)
    selB[0, 0:64] = 1.0
    selB[1, 64:128] = 1.0
    A["selB"] = selB.astype(BF)

    prot = np.zeros((128, 128), np.float32)
    for blk in range(2):
        o = 64 * blk
        for m in range(64):
            if m < 32:
                prot[o + m + 32, o + m] = -1.0
            else:
                prot[o + m - 32, o + m] = 1.0
    A["prot"] = prot.astype(BF)

    DIM_HEAD = 64
    WINDOW = 512
    half = np.arange(0, DIM_HEAD, 2, dtype=np.float64)
    inv_freq = 1.0 / (10000.0 ** (half / DIM_HEAD))
    base = (half + 0.4 * DIM_HEAD) / (1.4 * DIM_HEAD)
    dh = np.arange(DIM_HEAD)
    invf_d = inv_freq[dh % 32]
    base_d = base[dh % 32]
    i = np.arange(L, dtype=np.float64)
    freqs = i[None, :] * invf_d[:, None]
    cw = 512.0 * (i // 512)
    srel = base_d[:, None] ** ((i[None, :] - cw[None, :]) / 256.0)
    qcos = np.cos(freqs) * srel * 8.0
    qsin = np.sin(freqs) * srel * 8.0
    A["qcos"] = np.tile(qcos, (2, 1)).astype(BF)
    A["qsin"] = np.tile(qsin, (2, 1)).astype(BF)
    kcos = np.zeros((DIM_HEAD, 4096), np.float64)
    ksin = np.zeros((DIM_HEAD, 4096), np.float64)
    for w in range(4):
        for jcol in range(1024):
            j = 512 * (w - 1) + jcol
            if j < 0:
                continue
            fr = j * invf_d
            sr = base_d ** (-(j - 512.0 * w) / 256.0)
            kcos[:, 1024 * w + jcol] = np.cos(fr) * sr
            ksin[:, 1024 * w + jcol] = np.sin(fr) * sr
    A["kcos"] = np.tile(kcos, (2, 1)).astype(BF)
    A["ksin"] = np.tile(ksin, (2, 1)).astype(BF)

    fm = np.zeros((8, 128, CH), np.float32)
    r = np.arange(128)[:, None]
    cc = np.arange(CH)[None, :]
    for jc in range(8):
        d = 512 + cc - 128 * jc - r
        fm[jc] = np.where((d >= 0) & (d <= 512), 0.0, NEG)
    A["fmask"] = fm

    A["dt1"] = (r - cc).astype(np.float32) * np.ones((128, CH), np.float32)
    dtm = np.zeros((4, 128, CH), np.float32)
    for v in range(4):
        val = r - cc + 128 * v
        dtm[v] = np.where(val >= 0, val, 1.0e33).astype(np.float32)
    A["dtm1"] = dtm
    return A


AUX_SPECS = {
    "dist": ([32, L], F32), "selpw": ([32, 1024], BF16),
    "selT": ([128, 256], BF16), "selH": ([128, 2], BF16),
    "selB": ([2, 128], BF16), "prot": ([128, 128], BF16),
    "qcos": ([128, 4 * CH], BF16), "qsin": ([128, 4 * CH], BF16),
    "kcos": ([128, 4096], BF16), "ksin": ([128, 4096], BF16),
    "fmask": ([8, 128, CH], F32), "dt1": ([128, CH], F32),
    "dtm1": ([4, 128, CH], F32),
}

W_SPECS = {
    "L0_attn_W_in": [1024, 2080],
    "L0_attn_conv_kernel": [4, 2080, 1],
    "L0_attn_theta": [1024, 1],
    "L0_attn_decay_slopes": [28, 1],
    "L0_attn_anchor_slopes": [4, 1],
    "L0_attn_score_scale": [32, 1],
    "L0_attn_W_re": [32, 32],
    "L0_attn_W_im": [32, 32],
    "L0_attn_norm_scale": [64, 1],
    "L0_attn_W_out": [1024, 1024],
    "L0_ffn_wqkv": [2, 1024, 1536],
    "L0_ffn_q_scale": [2, 64, 1],
    "L0_ffn_k_scale": [2, 64, 1],
    "L0_ffn_wo": [2, 512, 1024],
    "L0_ffn_ln_g": [2, 1024, 1],
    "L0_ffn_ln_b": [2, 1024, 1],
    "L0_ffn_w1": [2, 1024, 5460],
    "L0_ffn_b1": [2, 5460, 1],
    "L0_ffn_w2": [2, 2730, 1024],
    "L0_ffn_b2": [2, 1024, 1],
    "L1_attn_W_qkv": [1024, 3072],
    "L1_attn_W_out": [1024, 1024],
    "L1_ffn_w1T": [1024, 4096],
    "L1_ffn_b1": [4096, 1],
    "L1_ffn_w2T": [4096, 1024],
    "L1_ffn_b2": [1024, 1],
}

MOD_WEIGHTS = {
    "a0": [n for n in W_SPECS if n.startswith("L0_attn")],
    "f0": [n for n in W_SPECS if n.startswith("L0_ffn")],
    "a1": ["L1_attn_W_qkv", "L1_attn_W_out"],
    "f1": ["L1_ffn_w1T", "L1_ffn_b1", "L1_ffn_w2T", "L1_ffn_b2"],
}
MOD_AUX = {
    "a0": ["dist", "selpw", "selT"],
    "f0": ["selB", "selH", "prot", "qcos", "qsin", "kcos", "ksin", "fmask"],
    "a1": ["dt1", "dtm1"],
    "f1": [],
}


def build_mod_nc(mod):
    nc = bass.Bass("TRN2", target_bir_lowering=False)
    dr = {}
    for n in MOD_WEIGHTS[mod]:
        dr[n] = nc.dram_tensor(n, W_SPECS[n], F32, kind="ExternalInput").ap()
    aux = {}
    for n in MOD_AUX[mod]:
        shape, dt = AUX_SPECS[n]
        aux[n] = nc.dram_tensor("aux_" + n, shape, dt, kind="ExternalInput").ap()
    xin = nc.dram_tensor("xin", [D, L], F32, kind="ExternalInput").ap()
    xout = nc.dram_tensor("xout", [D, L], F32, kind="ExternalOutput").ap()
    xwork = None
    if mod in ("f0", "f1"):
        xwork = nc.dram_tensor("xwork", [D, L], F32, kind="Internal").ap()
    es = ExitStack()
    chain = es.enter_context(nc.semaphore("chain"))
    state = {"val": 0}
    B = build_common(nc, es, chain, state)
    if mod == "a0":
        emit_l0_attn(nc, chain, state, B, dr, aux, xin, xout)
    elif mod == "f0":
        emit_l0_ffn(nc, chain, state, B, dr, aux, xin, xout, xwork)
    elif mod == "a1":
        emit_l1_attn(nc, chain, state, B, dr, aux, xin, xout)
    elif mod == "f1":
        emit_l1_ffn(nc, chain, state, B, dr, aux, xin, xout, xwork)
    tot = state["val"]
    with nc.Block() as blk:
        blk.tensor(lambda e: e.wait_ge(chain, tot))
        blk.vector(lambda e: e.wait_ge(chain, tot))
        blk.scalar(lambda e: e.wait_ge(chain, tot))
        blk.sync(lambda e: e.wait_ge(chain, tot))
        blk.gpsimd(lambda e: e.wait_ge(chain, tot))
    es.close()
    return nc


def kernel_unfused(**inputs):
    inp = {k: np.asarray(v) for k, v in inputs.items()}
    arrs = {}
    arrs["L0_attn_W_in"] = inp["L0_attn_W_in"]
    arrs["L0_attn_conv_kernel"] = np.ascontiguousarray(
        inp["L0_attn_conv_kernel"].reshape(4, 2080, 1))
    arrs["L0_attn_theta"] = np.ascontiguousarray(
        inp["L0_attn_theta"].reshape(1024, 1))
    arrs["L0_attn_decay_slopes"] = inp["L0_attn_decay_slopes"].reshape(28, 1)
    arrs["L0_attn_anchor_slopes"] = inp["L0_attn_anchor_slopes"].reshape(4, 1)
    arrs["L0_attn_score_scale"] = inp["L0_attn_score_scale"].reshape(32, 1)
    arrs["L0_attn_W_re"] = inp["L0_attn_W_re"]
    arrs["L0_attn_W_im"] = inp["L0_attn_W_im"]
    arrs["L0_attn_norm_scale"] = inp["L0_attn_norm_scale"].reshape(64, 1)
    arrs["L0_attn_W_out"] = inp["L0_attn_W_out"]
    arrs["L0_ffn_wqkv"] = inp["L0_ffn_wqkv"]
    arrs["L0_ffn_q_scale"] = inp["L0_ffn_q_scale"].reshape(2, 64, 1)
    arrs["L0_ffn_k_scale"] = inp["L0_ffn_k_scale"].reshape(2, 64, 1)
    arrs["L0_ffn_wo"] = inp["L0_ffn_wo"]
    arrs["L0_ffn_ln_g"] = inp["L0_ffn_ln_g"].reshape(2, 1024, 1)
    arrs["L0_ffn_ln_b"] = inp["L0_ffn_ln_b"].reshape(2, 1024, 1)
    arrs["L0_ffn_w1"] = inp["L0_ffn_w1"]
    arrs["L0_ffn_b1"] = inp["L0_ffn_b1"].reshape(2, 5460, 1)
    arrs["L0_ffn_w2"] = inp["L0_ffn_w2"]
    arrs["L0_ffn_b2"] = inp["L0_ffn_b2"].reshape(2, 1024, 1)
    arrs["L1_attn_W_qkv"] = inp["L1_attn_W_qkv"]
    arrs["L1_attn_W_out"] = inp["L1_attn_W_out"]
    arrs["L1_ffn_w1T"] = np.ascontiguousarray(inp["L1_ffn_w1"].T)
    arrs["L1_ffn_b1"] = inp["L1_ffn_b1"].reshape(4096, 1)
    arrs["L1_ffn_w2T"] = np.ascontiguousarray(inp["L1_ffn_w2"].T)
    arrs["L1_ffn_b2"] = inp["L1_ffn_b2"].reshape(1024, 1)
    arrs = {k: np.ascontiguousarray(v.astype(np.float32)) for k, v in arrs.items()}
    auxarr = build_aux_arrays()

    X = np.ascontiguousarray(
        inp["L0_attn_x"].astype(np.float32).transpose(0, 2, 1))

    mods = ["a0", "f0", "a1", "f1"]
    ncs = {m: build_mod_nc(m) for m in mods}
    cur = X
    for mod in mods:
        nxt = np.empty_like(cur)
        const = {n: arrs[n] for n in MOD_WEIGHTS[mod]}
        const.update({"aux_" + n: auxarr[n] for n in MOD_AUX[mod]})
        for s in range(4):
            in_maps = []
            for cid in range(8):
                d = dict(const)
                d["xin"] = np.ascontiguousarray(cur[cid * 4 + s])
                in_maps.append(d)
            res = run_bass_kernel_spmd(ncs[mod], in_maps, core_ids=list(range(8)))
            for cid in range(8):
                nxt[cid * 4 + s] = res.results[cid]["xout"]
        cur = nxt
    out = np.ascontiguousarray(cur.transpose(0, 2, 1)).astype(np.float32)
    return out


class NCWrap:
    def __init__(self, nc):
        self._nc = nc
        self.tag = ""

    def __getattr__(self, a):
        return getattr(self._nc, a)

    def sbuf_tensor(self, name, shape, dt):
        return self._nc.sbuf_tensor(name + self.tag, shape, dt)

    def psum_tensor(self, name, shape, dt):
        return self._nc.psum_tensor(name + self.tag, shape, dt)


def build_fused_nc():
    nc0 = bass.Bass("TRN2", target_bir_lowering=False)
    nc = NCWrap(nc0)
    dr = {}
    for n, shape in W_SPECS.items():
        dr[n] = nc0.dram_tensor(n, shape, F32, kind="ExternalInput").ap()
    aux = {}
    for n, (shape, dt) in AUX_SPECS.items():
        aux[n] = nc0.dram_tensor("aux_" + n, shape, dt, kind="ExternalInput").ap()
    xin_all = nc0.dram_tensor("xin", [4, D, L], F32, kind="ExternalInput").ap()
    xout_all = nc0.dram_tensor("xout", [4, D, L], F32, kind="ExternalOutput").ap()
    xs1 = nc0.dram_tensor("xs1", [D, L], F32, kind="Internal").ap()
    xs2 = nc0.dram_tensor("xs2", [D, L], F32, kind="Internal").ap()
    xs3 = nc0.dram_tensor("xs3", [D, L], F32, kind="Internal").ap()
    xwork = nc0.dram_tensor("xwork", [D, L], F32, kind="Internal").ap()
    es = ExitStack()
    chain = es.enter_context(nc0.semaphore("chain"))
    state = {"val": 0}
    B = build_common(nc, es, chain, state)
    for s in range(4):
        nc.tag = f"_s{s}a"
        emit_l0_attn(nc, chain, state, B, dr, aux, xin_all[s], xs1)
        nc.tag = f"_s{s}b"
        emit_l0_ffn(nc, chain, state, B, dr, aux, xs1, xs2, xwork)
        nc.tag = f"_s{s}c"
        emit_l1_attn(nc, chain, state, B, dr, aux, xs2, xs3)
        nc.tag = f"_s{s}d"
        emit_l1_ffn(nc, chain, state, B, dr, aux, xs3, xout_all[s], xwork)
    tot = state["val"]
    with nc0.Block() as blk:
        blk.tensor(lambda e: e.wait_ge(chain, tot))
        blk.vector(lambda e: e.wait_ge(chain, tot))
        blk.scalar(lambda e: e.wait_ge(chain, tot))
        blk.sync(lambda e: e.wait_ge(chain, tot))
        blk.gpsimd(lambda e: e.wait_ge(chain, tot))
    es.close()
    return nc0


def _prep(inputs):
    inp = {k: np.asarray(v) for k, v in inputs.items()}
    arrs = {}
    arrs["L0_attn_W_in"] = inp["L0_attn_W_in"]
    arrs["L0_attn_conv_kernel"] = np.ascontiguousarray(
        inp["L0_attn_conv_kernel"].reshape(4, 2080, 1))
    arrs["L0_attn_theta"] = np.ascontiguousarray(
        inp["L0_attn_theta"].reshape(1024, 1))
    arrs["L0_attn_decay_slopes"] = inp["L0_attn_decay_slopes"].reshape(28, 1)
    arrs["L0_attn_anchor_slopes"] = inp["L0_attn_anchor_slopes"].reshape(4, 1)
    arrs["L0_attn_score_scale"] = inp["L0_attn_score_scale"].reshape(32, 1)
    arrs["L0_attn_W_re"] = inp["L0_attn_W_re"]
    arrs["L0_attn_W_im"] = inp["L0_attn_W_im"]
    arrs["L0_attn_norm_scale"] = inp["L0_attn_norm_scale"].reshape(64, 1)
    arrs["L0_attn_W_out"] = inp["L0_attn_W_out"]
    arrs["L0_ffn_wqkv"] = inp["L0_ffn_wqkv"]
    arrs["L0_ffn_q_scale"] = inp["L0_ffn_q_scale"].reshape(2, 64, 1)
    arrs["L0_ffn_k_scale"] = inp["L0_ffn_k_scale"].reshape(2, 64, 1)
    arrs["L0_ffn_wo"] = inp["L0_ffn_wo"]
    arrs["L0_ffn_ln_g"] = inp["L0_ffn_ln_g"].reshape(2, 1024, 1)
    arrs["L0_ffn_ln_b"] = inp["L0_ffn_ln_b"].reshape(2, 1024, 1)
    arrs["L0_ffn_w1"] = inp["L0_ffn_w1"]
    arrs["L0_ffn_b1"] = inp["L0_ffn_b1"].reshape(2, 5460, 1)
    arrs["L0_ffn_w2"] = inp["L0_ffn_w2"]
    arrs["L0_ffn_b2"] = inp["L0_ffn_b2"].reshape(2, 1024, 1)
    arrs["L1_attn_W_qkv"] = inp["L1_attn_W_qkv"]
    arrs["L1_attn_W_out"] = inp["L1_attn_W_out"]
    arrs["L1_ffn_w1T"] = np.ascontiguousarray(inp["L1_ffn_w1"].T)
    arrs["L1_ffn_b1"] = inp["L1_ffn_b1"].reshape(4096, 1)
    arrs["L1_ffn_w2T"] = np.ascontiguousarray(inp["L1_ffn_w2"].T)
    arrs["L1_ffn_b2"] = inp["L1_ffn_b2"].reshape(1024, 1)
    arrs = {k: np.ascontiguousarray(v.astype(np.float32)) for k, v in arrs.items()}
    X = np.ascontiguousarray(
        inp["L0_attn_x"].astype(np.float32).transpose(0, 2, 1))
    return arrs, X


def kernel_fused(**inputs):
    arrs, X = _prep(inputs)
    auxarr = build_aux_arrays()
    nc = build_fused_nc()
    const = dict(arrs)
    const.update({"aux_" + n: auxarr[n] for n in AUX_SPECS})
    in_maps = []
    for cid in range(8):
        d = dict(const)
        d["xin"] = np.ascontiguousarray(X[4 * cid:4 * (cid + 1)])
        in_maps.append(d)
    res = run_bass_kernel_spmd(nc, in_maps, core_ids=list(range(8)))
    out = np.empty_like(X)
    for cid in range(8):
        out[4 * cid:4 * (cid + 1)] = res.results[cid]["xout"]
    return np.ascontiguousarray(out.transpose(0, 2, 1)).astype(np.float32)


def kernel(**inputs):
    return kernel_fused(**inputs)


# version 05 (final)
# speedup vs baseline: 1.3701x; 1.0682x over previous
import math
import numpy as np
import ml_dtypes
from contextlib import ExitStack

import concourse.bass as bass
import concourse.mybir as mybir
from concourse.bass_utils import run_bass_kernel_spmd

F32 = mybir.dt.float32
BF16 = mybir.dt.bfloat16
AF = mybir.ActivationFunctionType
OP = mybir.AluOpType
AX = mybir.AxisListType
BF = ml_dtypes.bfloat16

L = 2048
D = 1024
CH = 512
NCH = 4
NEG = -1.0e30
TWO_PI = 2.0 * math.pi
INV_2PI = 1.0 / TWO_PI
MAGIC = 12582912.0


def C(method, *args, **kw):
    return (method, args, kw)


class PL:
    def __init__(self):
        self.steps = []

    def t(self, *calls):
        self.steps.append(("tensor", calls, 1))

    def v(self, *calls):
        self.steps.append(("vector", calls, 1))

    def s(self, *calls):
        self.steps.append(("scalar", calls, 1))

    def ld(self, *calls):
        self.steps.append(("sync", calls, 16 * len(calls)))

    def st(self, *calls):
        self.steps.append(("gpsimd", calls, 16 * len(calls)))


def emit_plan(nc, chain, state, pl):
    pres = []
    tot = state["val"]
    for eng, calls, inc in pl.steps:
        pres.append(tot)
        tot += inc
    state["val"] = tot
    with nc.Block() as blk:
        def mk(engname):
            def body(e):
                for i, (eng, calls, inc) in enumerate(pl.steps):
                    if eng != engname:
                        continue
                    if pres[i] > 0:
                        e.wait_ge(chain, pres[i])
                    insts = [getattr(e, m)(*a, **k) for (m, a, k) in calls]
                    if inc >= 16:
                        for ins in insts:
                            ins.then_inc(chain, 16)
                    else:
                        insts[-1].then_inc(chain, inc)
            return body
        blk.tensor(mk("tensor"))
        blk.vector(mk("vector"))
        blk.scalar(mk("scalar"))
        blk.sync(mk("sync"))
        blk.gpsimd(mk("gpsimd"))


def build_common(nc, es, chain, state):
    B = {}
    sb = lambda name, shape, dt: es.enter_context(nc.sbuf_tensor(name, shape, dt))
    ps = lambda name, shape: es.enter_context(nc.psum_tensor(name, shape, F32))
    B["ones_col_b"] = sb("g_ones_col", [128, 1], BF16)
    B["ones_row_b"] = sb("g_ones_row", [1, 128], BF16)
    B["zeros"] = sb("g_zeros", [128, CH], F32)
    B["scr_a"] = sb("g_scr_a", [128, CH], F32)
    B["scr_b"] = sb("g_scr_b", [128, CH], F32)
    B["scr_c"] = sb("g_scr_c", [128, CH], F32)
    B["scr_bf"] = sb("g_scr_bf", [128, CH], BF16)
    B["row_a"] = sb("g_row_a", [1, L], F32)
    B["row_bf_a"] = sb("g_row_bf_a", [1, CH], BF16)
    B["row_bf_b"] = sb("g_row_bf_b", [1, CH], BF16)
    B["xsum"] = [sb(f"g_xsum{g}", [128, CH], F32) for g in range(8)]
    B["p_main"] = [ps(f"g_pm{i}", [128, CH]) for i in range(4)]
    B["p_bc"] = ps("g_pbc", [128, CH])
    B["p_bc2"] = ps("g_pbc2", [128, CH])
    B["p_row"] = ps("g_prow", [32, CH])
    B["p_row2"] = ps("g_prow2", [32, CH])
    consts = {0.0, 1.0, -1.0, 0.125, 1e-4, 1e-5, 1e-12, 1.0 / 64.0,
              1.0 / 1024.0, -1.0 / 1024.0, 1.0 / 4096.0, INV_2PI, MAGIC,
              MAGIC + 0.25, -TWO_PI, math.pi / 2.0, NEG}
    for i in range(16):
        consts.add(-(2.0 ** (-0.5 * (i + 1))))
    for d_ in range(1, 16):
        consts.add(float(128 * d_))
    consts = sorted(consts)
    ctile = sb("g_consts", [128, len(consts)], F32)
    ctileb = sb("g_constsb", [128, len(consts)], BF16)
    pl = PL()
    pl.v(C("memset", B["ones_col_b"][:, :], 1.0))
    pl.v(C("memset", B["ones_row_b"][:, :], 1.0))
    pl.v(C("memset", B["zeros"][:, :], 0.0))
    for i, vv in enumerate(consts):
        pl.v(C("memset", ctile[:, i:i + 1], float(vv)))
        pl.v(C("memset", ctileb[:, i:i + 1], float(vv)))
        nc.const_aps.register(F32, float(vv), ctile[:, i:i + 1])
        nc.const_aps.register(BF16, float(vv), ctileb[:, i:i + 1])
    emit_plan(nc, chain, state, pl)
    return B


def ln_chunk(pl, B, write_calls_fn):
    xb = B["scr_bf"]
    sums = B["p_row"][0:1, :]
    sqs = B["p_row2"][0:1, :]
    mb = B["p_bc"]
    rb = B["p_bc2"]
    t2 = B["scr_b"]
    ra = B["row_a"]
    for g in range(8):
        xg = B["xsum"][g][:, :]
        pl.v(C("tensor_copy", xb[:, :], xg))
        pl.t(C("matmul", sums, B["ones_col_b"][:, :], xb[:, :],
              start=(g == 0), stop=(g == 7)))
        pl.s(C("activation", xb[:, :], xb[:, :], AF.Square))
        pl.t(C("matmul", sqs, B["ones_col_b"][:, :], xb[:, :],
              start=(g == 0), stop=(g == 7)))
    pl.v(C("tensor_scalar", ra[0:1, 0:CH], sums, -1.0 / 1024.0, None, OP.mult))
    pl.v(C("tensor_scalar", ra[0:1, CH:2 * CH], sqs, 1.0 / 1024.0, None, OP.mult))
    pl.s(C("activation", ra[0:1, 2 * CH:3 * CH], ra[0:1, 0:CH], AF.Square))
    pl.v(C("tensor_sub", ra[0:1, CH:2 * CH], ra[0:1, CH:2 * CH],
          ra[0:1, 2 * CH:3 * CH]))
    pl.s(C("activation", ra[0:1, CH:2 * CH], ra[0:1, CH:2 * CH], AF.Sqrt,
          bias=1e-5))
    pl.v(C("reciprocal", ra[0:1, CH:2 * CH], ra[0:1, CH:2 * CH]))
    pl.v(C("tensor_copy", B["row_bf_a"][:, :], ra[0:1, 0:CH]))
    pl.v(C("tensor_copy", B["row_bf_b"][:, :], ra[0:1, CH:2 * CH]))
    pl.t(C("matmul", mb[:, :], B["ones_row_b"][:, :], B["row_bf_a"][:, :]))
    pl.t(C("matmul", rb[:, :], B["ones_row_b"][:, :], B["row_bf_b"][:, :]))
    for g in range(8):
        pl.v(C("tensor_add", t2[:, :], B["xsum"][g][:, :], mb[:, :]))
        pl.v(C("tensor_mul", t2[:, :], t2[:, :], rb[:, :]))
        for eng, call in write_calls_fn(g, t2):
            getattr(pl, eng)(call)


def emit_l0_attn(nc, chain, state, B, dr, aux, xin, xout):
    ZC = 2080
    es = ExitStack()
    sb = lambda name, shape, dt: es.enter_context(nc.sbuf_tensor(name, shape, dt))
    zb = [sb(f"a_zb{g}", [128, L + 3], BF16) for g in range(16)]
    z16 = sb("a_z16", [32, L + 3], F32)
    gate = [sb(f"a_gate{g}", [128, CH], BF16) for g in range(8)]
    nre = [sb(f"a_nre{g}", [128, CH], F32) for g in range(8)]
    nim = [sb(f"a_nim{g}", [128, CH], F32) for g in range(8)]
    car_re = sb("a_car_re", [128, 8], F32)
    car_im = sb("a_car_im", [128, 8], F32)
    car_dn = sb("a_car_dn", [32, 1], F32)
    pw = sb("a_pw", [32, CH], F32)
    pwb = sb("a_pwb", [32, CH], BF16)
    den = sb("a_den", [32, CH], F32)
    cscb = sb("a_cscb", [32, CH], BF16)
    rates = sb("a_rates", [32, 1], F32)
    ssc_col = sb("a_ssc", [32, 1], F32)
    theta_col = sb("a_th", [128, 8], F32)
    nsre_col = sb("a_nsre", [128, 1], F32)
    nsim_col = sb("a_nsim", [128, 1], F32)
    ck_cols = sb("a_ck", [128, 68], F32)
    blk_re = sb("a_blkre", [128, 128], BF16)
    blk_im = sb("a_blkim", [128, 128], BF16)
    blk32 = sb("a_blk32", [128, 128], F32)
    wt32 = sb("a_wt32", [128, 1024], F32)
    wtb = sb("a_wtb", [128, 1024], BF16)
    sdist = sb("a_sdist", [32, L], F32)
    selpw = sb("a_selpw", [32, 1024], BF16)
    selT = sb("a_selT", [128, 256], BF16)
    tsc = sb("a_tsc", [32, 4], F32)
    es_x = ExitStack()
    sbx = lambda name, shape, dt: es_x.enter_context(nc.sbuf_tensor(name, shape, dt))
    xTb = [sbx(f"a_xtb{g}", [128, L], BF16) for g in range(8)]

    P = B["p_main"]
    scr_a, scr_b, scr_c = B["scr_a"], B["scr_b"], B["scr_c"]
    sqb = B["scr_bf"]
    pl = PL()

    pl.ld(C("dma_start", out=sdist[:, :], in_=aux["dist"][:, :]),
          C("dma_start", out=selpw[:, :], in_=aux["selpw"][:, :]),
          C("dma_start", out=selT[:, :], in_=aux["selT"][:, :]),
          C("dma_start", out=rates[0:28, 0:1], in_=dr["L0_attn_decay_slopes"][:, :]),
          C("dma_start", out=rates[28:32, 0:1], in_=dr["L0_attn_anchor_slopes"][:, :]),
          C("dma_start", out=ssc_col[:, 0:1], in_=dr["L0_attn_score_scale"][:, :]))
    pl.ld(*[C("dma_start", out=theta_col[:, g:g + 1],
              in_=dr["L0_attn_theta"][128 * g:128 * (g + 1), :]) for g in range(8)])
    pl.ld(*([C("dma_start", out=nsre_col[32 * r:32 * (r + 1), 0:1],
               in_=dr["L0_attn_norm_scale"][0:32, :]) for r in range(4)] +
            [C("dma_start", out=nsim_col[32 * r:32 * (r + 1), 0:1],
               in_=dr["L0_attn_norm_scale"][32:64, :]) for r in range(4)]))
    ckcalls = []
    for j in range(4):
        for g in range(17):
            n = min(128, ZC - 128 * g)
            ckcalls.append(C("dma_start", out=ck_cols[0:n, j * 17 + g:j * 17 + g + 1],
                             in_=dr["L0_attn_conv_kernel"][j, 128 * g:128 * g + n, :]))
    pl.ld(*ckcalls)
    pl.v(C("memset", blk32[:, :], 0.0))
    pl.ld(*[C("dma_start", out=blk32[32 * r:32 * (r + 1), 32 * r:32 * (r + 1)],
              in_=dr["L0_attn_W_re"][:, :]) for r in range(4)])
    pl.v(C("tensor_copy", blk_re[:, :], blk32[:, :]))
    pl.v(C("memset", blk32[:, :], 0.0))
    pl.ld(*[C("dma_start", out=blk32[32 * r:32 * (r + 1), 32 * r:32 * (r + 1)],
              in_=dr["L0_attn_W_im"][:, :]) for r in range(4)])
    pl.v(C("tensor_copy", blk_im[:, :], blk32[:, :]))
    pl.s(C("activation", tsc[:, 0:1], rates[:, 0:1], AF.Abs))
    pl.s(C("activation", tsc[:, 1:2], tsc[:, 0:1], AF.Exp, scale=-1.0))
    pl.s(C("activation", tsc[:, 2:3], tsc[:, 1:2], AF.Ln, bias=1.0))
    pl.v(C("tensor_scalar_max", tsc[:, 3:4], rates[:, 0:1], 0.0))
    pl.v(C("tensor_add", rates[:, 0:1], tsc[:, 2:3], tsc[:, 3:4]))
    pl.v(C("tensor_scalar_mul", rates[:, 0:1], rates[:, 0:1], -1.0))
    pl.v(*([C("memset", zb[g][:, 0:3], 0.0) for g in range(16)] +
           [C("memset", z16[:, 0:3], 0.0),
            C("memset", car_re[:, :], 0.0),
            C("memset", car_im[:, :], 0.0),
            C("memset", car_dn[:, :], 0.0)]))
    for g in range(8):
        for c in range(NCH):
            pl.ld(C("dma_start", out=B["scr_a"][:, :],
                    in_=xin[128 * g:128 * (g + 1), CH * c:CH * (c + 1)]))
            pl.v(C("tensor_copy", xTb[g][:, CH * c:CH * (c + 1)],
                  B["scr_a"][:, :]))

    win = dr["L0_attn_W_in"]
    for m in range(17):
        mm = min(128, ZC - 128 * m)
        pl.ld(*[C("dma_start", out=wt32[:, 128 * k:128 * k + mm],
                  in_=win[128 * k:128 * (k + 1), 128 * m:128 * m + mm])
                for k in range(8)])
        pl.v(C("tensor_copy", wtb[:, :], wt32[:, :]))
        pl.t(*[C("matmul", P[c][0:mm, :], wtb[0:128, 128 * k:128 * k + mm],
                 xTb[k][:, CH * c:CH * (c + 1)],
                 start=(k == 0), stop=(k == 7))
               for k in range(8) for c in range(NCH)])
        if m < 16:
            pl.v(*[C("tensor_copy", zb[m][0:mm, 3 + CH * c:3 + CH * (c + 1)],
                     P[c][0:mm, :]) for c in range(NCH)])
        else:
            pl.v(*[C("tensor_copy", z16[0:mm, 3 + CH * c:3 + CH * (c + 1)],
                     P[c][0:mm, :]) for c in range(NCH)])
    emit_plan(nc, chain, state, pl)
    es_x.close()
    gy = [sb(f"a_gy{g}", [128, L], BF16) for g in range(8)]
    pl = PL()

    sq_ps = B["p_row"]
    y_ps = B["p_bc2"]
    for c in range(NCH):
        t0 = CH * c
        pl.v(C("tensor_scalar", scr_a[0:32, :], z16[:, t0:t0 + CH],
              ck_cols[0:32, 16:17], None, OP.mult))
        for j in range(1, 4):
            pl.v(C("scalar_tensor_tensor", scr_a[0:32, :], z16[:, t0 + j:t0 + j + CH],
                  ck_cols[0:32, j * 17 + 16:j * 17 + 17], scr_a[0:32, :],
                  OP.mult, OP.add))
        pl.v(C("tensor_scalar", scr_b[0:32, :], sdist[:, t0:t0 + CH],
              rates[:, 0:1], None, OP.mult))
        pl.v(C("scalar_tensor_tensor", scr_c[0:32, :], scr_a[0:32, :],
              ssc_col[:, 0:1], scr_b[0:32, :], OP.mult, OP.add))
        pl.s(C("activation", pw[:, :], scr_c[0:32, :], AF.Exp))
        pl.v(C("tensor_copy", pwb[:, :], pw[:, :]))
        pl.v(C("tensor_tensor_scan", den[:, :], pw[:, :], B["zeros"][0:32, :],
              car_dn[:, 0:1] if c > 0 else 0.0, OP.add, OP.add))
        pl.v(C("tensor_copy", car_dn[:, 0:1], den[:, CH - 1:CH]))
        for g in range(8):
            zt = zb[8 + g]
            pl.v(C("tensor_scalar", scr_a[:, :], zt[:, t0:t0 + CH],
                  ck_cols[:, 8 + g:9 + g], None, OP.mult))
            for j in range(1, 4):
                pl.v(C("scalar_tensor_tensor", scr_a[:, :], zt[:, t0 + j:t0 + j + CH],
                      ck_cols[:, j * 17 + 8 + g:j * 17 + 9 + g], scr_a[:, :],
                      OP.mult, OP.add))
            pl.s(C("activation", gate[g][:, :], scr_a[:, :], AF.Silu))
        for g in range(8):
            zt = zb[g]
            pl.v(C("tensor_scalar", scr_a[:, :], zt[:, t0:t0 + CH],
                  ck_cols[:, g:g + 1], None, OP.mult))
            for j in range(1, 4):
                pl.v(C("scalar_tensor_tensor", scr_a[:, :], zt[:, t0 + j:t0 + j + CH],
                      ck_cols[:, j * 17 + g:j * 17 + 1 + g], scr_a[:, :],
                      OP.mult, OP.add))
            pl.v(C("tensor_scalar", scr_b[:, :], scr_a[:, :],
                  theta_col[:, g:g + 1], None, OP.mult))
            pl.v(C("tensor_scalar", scr_a[:, :], scr_b[:, :], INV_2PI, MAGIC,
                  OP.mult, OP.add))
            pl.v(C("tensor_scalar", scr_a[:, :], scr_a[:, :], MAGIC, None,
                  OP.subtract))
            pl.v(C("scalar_tensor_tensor", scr_c[:, :], scr_a[:, :], -TWO_PI,
                  scr_b[:, :], OP.mult, OP.add))
            pl.s(C("activation", scr_c[:, :], scr_c[:, :], AF.Sin))
            pl.v(C("tensor_scalar", scr_a[:, :], scr_b[:, :], INV_2PI,
                  MAGIC + 0.25, OP.mult, OP.add))
            pl.v(C("tensor_scalar", scr_a[:, :], scr_a[:, :], MAGIC, None,
                  OP.subtract))
            pl.v(C("scalar_tensor_tensor", scr_a[:, :], scr_a[:, :], -TWO_PI,
                  scr_b[:, :], OP.mult, OP.add))
            pl.v(C("tensor_scalar", scr_a[:, :], scr_a[:, :], math.pi / 2.0,
                  None, OP.add))
            pl.s(C("activation", scr_a[:, :], scr_a[:, :], AF.Sin))
            pl.t(C("matmul", B["p_bc"][:, :], selpw[:, 128 * g:128 * (g + 1)],
                  pwb[:, :]))
            pl.v(C("tensor_mul", scr_a[:, :], scr_a[:, :], B["p_bc"][:, :]))
            pl.v(C("tensor_mul", scr_c[:, :], scr_c[:, :], B["p_bc"][:, :]))
            pl.v(C("tensor_tensor_scan", nre[g][:, :], scr_a[:, :],
                  B["zeros"][:, :], car_re[:, g:g + 1] if c > 0 else 0.0,
                  OP.add, OP.add))
            pl.v(C("tensor_tensor_scan", nim[g][:, :], scr_c[:, :],
                  B["zeros"][:, :], car_im[:, g:g + 1] if c > 0 else 0.0,
                  OP.add, OP.add))
            pl.v(C("tensor_copy", car_re[:, g:g + 1], nre[g][:, CH - 1:CH]),
                 C("tensor_copy", car_im[:, g:g + 1], nim[g][:, CH - 1:CH]))
            pl.s(C("activation", sqb[:, :], nre[g][:, :], AF.Square))
            pl.t(C("matmul", sq_ps[:, :], selT[:, 32 * g:32 * (g + 1)], sqb[:, :],
                  start=(g == 0), stop=False))
            pl.s(C("activation", sqb[:, :], nim[g][:, :], AF.Square))
            pl.t(C("matmul", sq_ps[:, :], selT[:, 32 * g:32 * (g + 1)], sqb[:, :],
                  start=False, stop=(g == 7)))
        pl.v(C("tensor_scalar_max", scr_a[0:32, :], den[:, :], 1e-4))
        pl.v(C("reciprocal", scr_a[0:32, :], scr_a[0:32, :]))
        pl.s(C("activation", scr_b[0:32, :], scr_a[0:32, :], AF.Square))
        pl.v(C("tensor_mul", scr_b[0:32, :], scr_b[0:32, :], sq_ps[:, :]))
        pl.v(C("tensor_scalar", scr_b[0:32, :], scr_b[0:32, :], 1.0 / 64.0,
              1e-5, OP.mult, OP.add))
        pl.s(C("activation", scr_b[0:32, :], scr_b[0:32, :], AF.Sqrt))
        pl.v(C("reciprocal", scr_b[0:32, :], scr_b[0:32, :]))
        pl.v(C("tensor_mul", scr_b[0:32, :], scr_a[0:32, :], scr_b[0:32, :]))
        pl.v(C("tensor_copy", cscb[:, :], scr_b[0:32, :]))
        for g in range(8):
            pl.t(C("matmul", B["p_bc"][:, :], selpw[:, 128 * g:128 * (g + 1)],
                  cscb[:, :]))
            pl.v(C("scalar_tensor_tensor", sqb[:, :], nre[g][:, :],
                  nsre_col[:, 0:1], B["p_bc"][:, :], OP.mult, OP.mult))
            pl.t(C("matmul", y_ps[:, :], blk_re[:, :], sqb[:, :],
                  start=True, stop=False))
            pl.v(C("scalar_tensor_tensor", sqb[:, :], nim[g][:, :],
                  nsim_col[:, 0:1], B["p_bc"][:, :], OP.mult, OP.mult))
            pl.t(C("matmul", y_ps[:, :], blk_im[:, :], sqb[:, :],
                  start=False, stop=True))
            pl.v(C("tensor_mul", gy[g][:, t0:t0 + CH], gate[g][:, :],
                  y_ps[:, :]))

    wo = dr["L0_attn_W_out"]
    for c in range(NCH):
        for g in range(8):
            pl.ld(*[C("dma_start", out=wt32[:, 128 * k:128 * (k + 1)],
                      in_=wo[128 * k:128 * (k + 1), 128 * g:128 * (g + 1)])
                    for k in range(8)])
            pl.v(C("tensor_copy", wtb[:, :], wt32[:, :]))
            pl.t(*[C("matmul", P[g % 4][:, :], wtb[:, 128 * k:128 * (k + 1)],
                     gy[k][:, CH * c:CH * (c + 1)],
                     start=(k == 0), stop=(k == 7)) for k in range(8)])
            pl.ld(C("dma_start", out=scr_a[:, :],
                    in_=xin[128 * g:128 * (g + 1), CH * c:CH * (c + 1)]))
            pl.v(C("tensor_add", B["xsum"][g][:, :], scr_a[:, :], P[g % 4][:, :]))
        def wrt(g, src, c=c):
            return [("st", C("dma_start",
                             out=xout[128 * g:128 * (g + 1), CH * c:CH * (c + 1)],
                             in_=src[:, :]))]
        ln_chunk(pl, B, wrt)

    emit_plan(nc, chain, state, pl)
    es.close()


def emit_l0_ffn(nc, chain, state, B, dr, aux, xin, xout, xwork):
    HID = 2730
    P = B["p_main"]
    scr_a, scr_b, scr_c = B["scr_a"], B["scr_b"], B["scr_c"]

    pl = PL()
    for g in range(8):
        for c in range(NCH):
            pl.ld(C("dma_start", out=scr_a[:, :],
                    in_=xin[128 * g:128 * (g + 1), CH * c:CH * (c + 1)]))
            pl.st(C("dma_start", out=xwork[128 * g:128 * (g + 1), CH * c:CH * (c + 1)],
                    in_=scr_a[:, :]))
    emit_plan(nc, chain, state, pl)

    for lyr in range(2):
        es1 = ExitStack()
        sb1 = lambda name, shape, dt: es1.enter_context(nc.sbuf_tensor(f"{name}_{lyr}", shape, dt))
        vtok = [sb1(f"f_vt{t}", [128, 512], BF16) for t in range(16)]
        q2b = [sb1(f"f_q2b{g}", [128, L], BF16) for g in range(4)]
        k2b = [sb1(f"f_k2b{g}", [128, 4096], BF16) for g in range(4)]
        wt32 = sb1("f_wt32", [128, 1024], F32)
        wtb = sb1("f_wtb", [128, 1024], BF16)
        qs_col = sb1("f_qs", [128, 1], F32)
        qs_rot = sb1("f_qsr", [128, 1], F32)
        ks_col = sb1("f_ks", [128, 1], F32)
        ks_rot = sb1("f_ksr", [128, 1], F32)
        selB = sb1("f_selB", [2, 128], BF16)
        selH = sb1("f_selH", [128, 2], BF16)
        es1a = ExitStack()
        sb1a = lambda name, shape, dt: es1a.enter_context(nc.sbuf_tensor(f"{name}_{lyr}", shape, dt))
        qkb = [sb1a(f"f_qkb{g}", [128, L], BF16) for g in range(8)]
        rin = [sb1a(f"f_rin{i}", [2, L], BF16) for i in range(8)]
        es1a1 = ExitStack()
        shb = [es1a1.enter_context(nc.sbuf_tensor(f"f_shb{g}_{lyr}", [128, L], BF16))
               for g in range(8)]

        pl = PL()
        pl.ld(C("dma_start", out=selB[:, :], in_=aux["selB"][:, :]),
              C("dma_start", out=selH[:, :], in_=aux["selH"][:, :]))
        qsd = dr["L0_ffn_q_scale"]
        ksd = dr["L0_ffn_k_scale"]
        sc_calls = []
        for hl in range(2):
            sc_calls.append(C("dma_start", out=qs_col[64 * hl:64 * hl + 64, 0:1],
                              in_=qsd[lyr, :, :]))
            sc_calls.append(C("dma_start", out=ks_col[64 * hl:64 * hl + 64, 0:1],
                              in_=ksd[lyr, :, :]))
            for hf in range(2):
                sc_calls.append(C("dma_start",
                                  out=qs_rot[64 * hl + 32 * hf:64 * hl + 32 * hf + 32, 0:1],
                                  in_=qsd[lyr, 32 - 32 * hf:64 - 32 * hf, :]))
                sc_calls.append(C("dma_start",
                                  out=ks_rot[64 * hl + 32 * hf:64 * hl + 32 * hf + 32, 0:1],
                                  in_=ksd[lyr, 32 - 32 * hf:64 - 32 * hf, :]))
        pl.ld(*sc_calls)

        for g in range(8):
            for c in range(NCH):
                pl.ld(C("dma_start", out=scr_a[:, :],
                        in_=xwork[128 * g:128 * (g + 1), CH * c:CH * (c + 1)]))
                if g >= 4:
                    if c == 0:
                        pl.v(C("memset", shb[g][:, 0:1], 0.0))
                    else:
                        pl.ld(C("dma_start", out=scr_b[:, 0:1],
                                in_=xwork[128 * g:128 * (g + 1), CH * c - 1:CH * c],
                                allow_slow_non_contiguous=True))
                        pl.v(C("tensor_copy", shb[g][:, CH * c:CH * c + 1],
                              scr_b[:, 0:1]))
                    pl.v(C("tensor_copy", shb[g][:, CH * c + 1:CH * (c + 1)],
                          scr_a[:, 0:CH - 1]))
                else:
                    pl.v(C("tensor_copy", shb[g][:, CH * c:CH * (c + 1)],
                          scr_a[:, :]))

        wq = dr["L0_ffn_wqkv"]
        for m in range(8):
            pl.ld(*[C("dma_start", out=wt32[:, 128 * k:128 * (k + 1)],
                      in_=wq[lyr, 128 * k:128 * (k + 1), 128 * m:128 * (m + 1)])
                    for k in range(8)])
            pl.v(C("tensor_copy", wtb[:, :], wt32[:, :]))
            pl.t(*[C("matmul", P[c][:, :], wtb[:, 128 * k:128 * (k + 1)],
                     shb[k][:, CH * c:CH * (c + 1)],
                     start=(k == 0), stop=(k == 7))
                   for k in range(8) for c in range(NCH)])
            for c in range(NCH):
                pl.v(C("tensor_copy", qkb[m][:, CH * c:CH * (c + 1)], P[c][:, :]))
                pl.s(C("activation", B["scr_bf"][:, :], P[c][:, :], AF.Square))
                pl.t(C("matmul", B["p_row2"][0:2, :], selH[:, :], B["scr_bf"][:, :]))
                pl.s(C("activation", scr_a[0:2, :], B["p_row2"][0:2, :], AF.Sqrt))
                pl.v(C("tensor_scalar_max", scr_a[0:2, :], scr_a[0:2, :], 1e-12))
                pl.v(C("reciprocal", scr_a[0:2, :], scr_a[0:2, :]))
                pl.v(C("tensor_copy", rin[m][:, CH * c:CH * (c + 1)],
                      scr_a[0:2, :]))
        for tg in range(4):
            for kp in range(4):
                pl.ld(*[C("dma_start", out=wt32[:, 512 * h2:512 * (h2 + 1)],
                          in_=wq[lyr, 128 * (2 * kp + h2):128 * (2 * kp + h2 + 1),
                                 1024:1536]) for h2 in range(2)])
                pl.v(C("tensor_copy", wtb[:, :], wt32[:, :]))
                pl.t(*[C("matmul", P[t % 4][:, :],
                         shb[2 * kp + h2][:, 128 * t:128 * (t + 1)],
                         wtb[:, 512 * h2:512 * (h2 + 1)],
                         start=(kp == 0 and h2 == 0), stop=(kp == 3 and h2 == 1))
                       for h2 in range(2) for t in range(4 * tg, 4 * tg + 4)])
            pl.v(*[C("tensor_copy", vtok[t][:, :], P[t % 4][:, :])
                   for t in range(4 * tg, 4 * tg + 4)])
        emit_plan(nc, chain, state, pl)
        es1a1.close()
        es1a2 = ExitStack()
        sb1a2 = lambda name, shape, dt: es1a2.enter_context(nc.sbuf_tensor(f"{name}_{lyr}", shape, dt))
        prot = sb1a2("f_prot", [128, 128], BF16)
        qcosb = sb1a2("f_qcos", [128, 4 * CH], BF16)
        qsinb = sb1a2("f_qsin", [128, 4 * CH], BF16)
        kcosb = sb1a2("f_kcos", [128, 4096], BF16)
        ksinb = sb1a2("f_ksin", [128, 4096], BF16)
        pl = PL()
        pl.ld(C("dma_start", out=prot[:, :], in_=aux["prot"][:, :]),
              C("dma_start", out=qcosb[:, :], in_=aux["qcos"][:, :]),
              C("dma_start", out=qsinb[:, :], in_=aux["qsin"][:, :]),
              C("dma_start", out=kcosb[:, :], in_=aux["kcos"][:, :]),
              C("dma_start", out=ksinb[:, :], in_=aux["ksin"][:, :]))
        for g in range(4):
            for w in range(4):
                cw = CH * w
                pl.t(C("matmul", B["p_bc"][:, :], prot[:, :], qkb[g][:, cw:cw + CH]))
                pl.t(C("matmul", B["p_bc2"][:, :], selB[:, :],
                      rin[g][:, cw:cw + CH]))
                pl.v(C("scalar_tensor_tensor", scr_a[:, :], qkb[g][:, cw:cw + CH],
                      qs_col[:, 0:1], qcosb[:, cw:cw + CH], OP.mult, OP.mult))
                pl.v(C("scalar_tensor_tensor", scr_b[:, :], B["p_bc"][:, :],
                      qs_rot[:, 0:1], qsinb[:, cw:cw + CH], OP.mult, OP.mult))
                pl.v(C("tensor_add", scr_a[:, :], scr_a[:, :], scr_b[:, :]))
                pl.v(C("tensor_mul", q2b[g][:, cw:cw + CH], scr_a[:, :],
                      B["p_bc2"][:, :]))
                for hf in range(2):
                    j0 = cw - CH + hf * CH
                    if j0 < 0:
                        continue
                    kc = 1024 * w + CH * hf
                    pl.t(C("matmul", B["p_bc"][:, :], prot[:, :],
                          qkb[4 + g][:, j0:j0 + CH]))
                    pl.t(C("matmul", B["p_bc2"][:, :], selB[:, :],
                          rin[4 + g][:, j0:j0 + CH]))
                    pl.v(C("scalar_tensor_tensor", scr_a[:, :],
                          qkb[4 + g][:, j0:j0 + CH], ks_col[:, 0:1],
                          kcosb[:, kc:kc + CH], OP.mult, OP.mult))
                    pl.v(C("scalar_tensor_tensor", scr_b[:, :], B["p_bc"][:, :],
                          ks_rot[:, 0:1], ksinb[:, kc:kc + CH], OP.mult, OP.mult))
                    pl.v(C("tensor_add", scr_a[:, :], scr_a[:, :], scr_b[:, :]))
                    pl.v(C("tensor_mul", k2b[g][:, kc:kc + CH], scr_a[:, :],
                          B["p_bc2"][:, :]))
        emit_plan(nc, chain, state, pl)
        es1a2.close()
        es1a.close()
        es1b = ExitStack()
        sb1b = lambda name, shape, dt: es1b.enter_context(nc.sbuf_tensor(f"{name}_{lyr}", shape, dt))
        Pb = sb1b("f_Pb", [128, 8 * CH], BF16)
        maskm = [sb1b(f"f_mk{j}", [128, CH], F32) for j in range(8)]
        nrow = sb1b("f_nrow", [1, 1024], F32)
        sqk = sb1b("f_sqk", [64, 1024], BF16)
        oT = [sb1b(f"f_oT{g}", [128, L], BF16) for g in range(4)]
        pl = PL()
        pl.ld(*[C("dma_start", out=maskm[j][:, :], in_=aux["fmask"][j, :, :])
                for j in range(8)])
        for h in range(8):
            g = h // 2
            r0 = 64 * (h % 2)
            for w in range(4):
                cw = CH * w
                jcs = list(range(8)) if w > 0 else list(range(4, 8))
                pl.s(C("activation", B["scr_bf"][0:64, :],
                      q2b[g][r0:r0 + 64, cw:cw + CH], AF.Square))
                pl.t(C("matmul", B["p_row"][0:1, :], B["ones_col_b"][0:64, :],
                      B["scr_bf"][0:64, :]))
                kc0 = 1024 * w + (512 if w == 0 else 0)
                nk = 1024 - (512 if w == 0 else 0)
                pl.s(C("activation", sqk[0:64, 0:nk],
                      k2b[g][r0:r0 + 64, kc0:kc0 + nk], AF.Square))
                for hf in range(nk // 512):
                    pl.t(C("matmul", B["p_row2"][0:1, :],
                          B["ones_col_b"][0:64, :],
                          sqk[0:64, 512 * hf:512 * (hf + 1)]))
                    pl.v(C("tensor_copy", nrow[0:1, 512 * hf:512 * (hf + 1)],
                          B["p_row2"][0:1, 0:CH]))
                pl.v(C("reduce_max", scr_c[0:1, 0:1], nrow[0:1, 0:nk], AX.X))
                pl.v(C("tensor_scalar", scr_b[0:1, :], B["p_row"][0:1, :],
                      scr_c[0:1, 0:1], None, OP.mult))
                pl.s(C("activation", scr_b[0:1, :], scr_b[0:1, :], AF.Sqrt))
                pl.v(C("tensor_scalar", B["row_bf_a"][:, :], scr_b[0:1, :],
                      -1.0, None, OP.mult))
                pl.t(C("matmul", B["p_bc2"][:, :], B["ones_row_b"][:, :],
                      B["row_bf_a"][:, :]))
                XS = B["xsum"]
                for g0 in range(0, len(jcs), 4):
                    grp = jcs[g0:g0 + 4]
                    pl.t(*[C("matmul", P[jc % 4][:, :],
                             k2b[g][r0:r0 + 64,
                                    1024 * w + 128 * jc:1024 * w + 128 * jc + 128],
                             q2b[g][r0:r0 + 64, cw:cw + CH]) for jc in grp])
                    pl.v(*[C("tensor_add", XS[jc % 4][:, :], maskm[jc][:, :],
                             B["p_bc2"][:, :]) for jc in grp])
                    pl.v(*[C("tensor_add", XS[jc % 4][:, :], XS[jc % 4][:, :],
                             P[jc % 4][:, :]) for jc in grp])
                    pl.s(*[C("activation", Pb[:, CH * jc:CH * (jc + 1)],
                             XS[jc % 4][:, :], AF.Exp) for jc in grp])
                pl.t(*[C("matmul", B["p_row"][0:1, :], B["ones_col_b"][:, :],
                         Pb[:, CH * jc:CH * (jc + 1)],
                         start=(i == 0), stop=(i == len(jcs) - 1))
                       for i, jc in enumerate(jcs)])
                pl.v(C("reciprocal", scr_c[0:1, :], B["p_row"][0:1, :]))
                pl.v(C("tensor_copy", B["row_bf_b"][:, :], scr_c[0:1, :]))
                pl.t(*[C("matmul", B["p_bc"][0:64, :],
                         vtok[4 * (w - 1) + jc][:, 64 * h:64 * (h + 1)],
                         Pb[:, CH * jc:CH * (jc + 1)],
                         start=(i == 0), stop=(i == len(jcs) - 1))
                       for i, jc in enumerate(jcs)])
                pl.t(C("matmul", B["p_bc2"][0:64, :], B["ones_row_b"][:, 0:64],
                      B["row_bf_b"][:, :]))
                pl.v(C("tensor_copy", scr_a[0:64, :], B["p_bc"][0:64, :]))
                pl.v(C("tensor_mul", oT[g][r0:r0 + 64, cw:cw + CH],
                      scr_a[0:64, :], B["p_bc2"][0:64, :]))
        wod = dr["L0_ffn_wo"]
        for m in range(8):
            pl.ld(*[C("dma_start", out=wt32[:, 128 * k:128 * (k + 1)],
                      in_=wod[lyr, 128 * k:128 * (k + 1), 128 * m:128 * (m + 1)])
                    for k in range(4)])
            pl.v(C("tensor_copy", wtb[:, 0:512], wt32[:, 0:512]))
            pl.t(*[C("matmul", P[c][:, :], wtb[:, 128 * k:128 * (k + 1)],
                     oT[k][:, CH * c:CH * (c + 1)],
                     start=(k == 0), stop=(k == 3))
                   for k in range(4) for c in range(NCH)])
            pl.ld(*[C("dma_start", out=B["xsum"][c][:, :],
                      in_=xwork[128 * m:128 * (m + 1), CH * c:CH * (c + 1)])
                    for c in range(NCH)])
            pl.v(*[C("tensor_add", B["xsum"][c][:, :], B["xsum"][c][:, :],
                     P[c][:, :]) for c in range(NCH)])
            pl.st(*[C("dma_start",
                      out=xwork[128 * m:128 * (m + 1), CH * c:CH * (c + 1)],
                      in_=B["xsum"][c][:, :]) for c in range(NCH)])
        emit_plan(nc, chain, state, pl)
        es1b.close()
        es1.close()

        es2 = ExitStack()
        sb2 = lambda name, shape, dt: es2.enter_context(nc.sbuf_tensor(f"{name}_{lyr}", shape, dt))
        hnb = [sb2(f"f_hn{g}", [128, L], BF16) for g in range(8)]
        fb = [sb2(f"f_fb{m}", [128, L], BF16) for m in range(22)]
        wt32 = sb2("f_wt32b", [128, 2816], F32)
        wtb = sb2("f_wtbb", [128, 2816], BF16)
        sgl = sb2("f_sgl", [128, 2048], F32)
        lg_col = sb2("f_lg", [128, 8], F32)
        lb_col = sb2("f_lb", [128, 8], F32)
        b2_col = sb2("f_b2", [128, 8], F32)
        b1a_col = sb2("f_b1a", [128, 22], F32)
        b1g_col = sb2("f_b1g", [128, 22], F32)
        pl = PL()
        pl.ld(*([C("dma_start", out=lg_col[:, g:g + 1],
                   in_=dr["L0_ffn_ln_g"][lyr, 128 * g:128 * (g + 1), :])
                 for g in range(8)] +
                [C("dma_start", out=lb_col[:, g:g + 1],
                   in_=dr["L0_ffn_ln_b"][lyr, 128 * g:128 * (g + 1), :])
                 for g in range(8)] +
                [C("dma_start", out=b2_col[:, g:g + 1],
                   in_=dr["L0_ffn_b2"][lyr, 128 * g:128 * (g + 1), :])
                 for g in range(8)]))
        MT = [(m, min(128, HID - 128 * m)) for m in range(22)]
        pl.ld(*([C("dma_start", out=b1a_col[0:mm, m:m + 1],
                   in_=dr["L0_ffn_b1"][lyr, 128 * m:128 * m + mm, :])
                 for m, mm in MT] +
                [C("dma_start", out=b1g_col[0:mm, m:m + 1],
                   in_=dr["L0_ffn_b1"][lyr, HID + 128 * m:HID + 128 * m + mm, :])
                 for m, mm in MT]))

        for c in range(NCH):
            for g in range(8):
                pl.ld(C("dma_start", out=scr_a[:, :],
                        in_=xwork[128 * g:128 * (g + 1), CH * c:CH * (c + 1)]))
                if g >= 4:
                    if c == 0:
                        pl.v(C("memset", B["xsum"][g][:, 0:1], 0.0))
                    else:
                        pl.ld(C("dma_start", out=scr_b[:, 0:1],
                                in_=xwork[128 * g:128 * (g + 1), CH * c - 1:CH * c],
                                allow_slow_non_contiguous=True))
                        pl.v(C("tensor_copy", B["xsum"][g][:, 0:1], scr_b[:, 0:1]))
                    pl.v(C("tensor_copy", B["xsum"][g][:, 1:CH], scr_a[:, 0:CH - 1]))
                else:
                    pl.v(C("tensor_copy", B["xsum"][g][:, :], scr_a[:, :]))
            def wr_hn(g, src, c=c):
                return [("v", C("tensor_scalar", hnb[g][:, CH * c:CH * (c + 1)],
                                src[:, :], lg_col[:, g:g + 1], lb_col[:, g:g + 1],
                                OP.mult, OP.add))]
            ln_chunk(pl, B, wr_hn)

        w1d = dr["L0_ffn_w1"]
        for m, mm in MT:
            pl.ld(*[C("dma_start", out=wt32[:, 128 * k:128 * k + mm],
                      in_=w1d[lyr, 128 * k:128 * (k + 1),
                              HID + 128 * m:HID + 128 * m + mm])
                    for k in range(8)])
            pl.v(C("tensor_copy", wtb[:, 0:1024], wt32[:, 0:1024]))
            pl.t(*[C("matmul", P[c][0:mm, :], wtb[:, 128 * k:128 * k + mm],
                     hnb[k][:, CH * c:CH * (c + 1)],
                     start=(k == 0), stop=(k == 7))
                   for k in range(8) for c in range(NCH)])
            pl.v(*[C("tensor_scalar", sgl[0:mm, CH * c:CH * (c + 1)],
                     P[c][0:mm, :], 1.0, b1g_col[0:mm, m:m + 1],
                     OP.mult, OP.add) for c in range(NCH)])
            pl.s(*[C("activation", sgl[0:mm, CH * c:CH * (c + 1)],
                     sgl[0:mm, CH * c:CH * (c + 1)], AF.Gelu)
                   for c in range(NCH)])
            pl.v(*[C("tensor_copy", fb[m][0:mm, CH * c:CH * (c + 1)],
                     sgl[0:mm, CH * c:CH * (c + 1)]) for c in range(NCH)])
        for m, mm in MT:
            pl.ld(*[C("dma_start", out=wt32[:, 128 * k:128 * k + mm],
                      in_=w1d[lyr, 128 * k:128 * (k + 1), 128 * m:128 * m + mm])
                    for k in range(8)])
            pl.v(C("tensor_copy", wtb[:, 0:1024], wt32[:, 0:1024]))
            pl.t(*[C("matmul", P[c][0:mm, :], wtb[:, 128 * k:128 * k + mm],
                     hnb[k][:, CH * c:CH * (c + 1)],
                     start=(k == 0), stop=(k == 7))
                   for k in range(8) for c in range(NCH)])
            pl.v(*[C("tensor_scalar", sgl[0:mm, CH * c:CH * (c + 1)],
                     P[c][0:mm, :], 1.0, b1a_col[0:mm, m:m + 1],
                     OP.mult, OP.add) for c in range(NCH)])
            pl.v(*[C("tensor_mul", fb[m][0:mm, CH * c:CH * (c + 1)],
                     sgl[0:mm, CH * c:CH * (c + 1)],
                     fb[m][0:mm, CH * c:CH * (c + 1)]) for c in range(NCH)])
        w2d = dr["L0_ffn_w2"]
        for dt_ in range(8):
            pl.ld(*[C("dma_start", out=wt32[0:mm, 128 * m:128 * (m + 1)],
                      in_=w2d[lyr, 128 * m:128 * m + mm,
                              128 * dt_:128 * (dt_ + 1)])
                    for m, mm in MT])
            pl.v(C("tensor_copy", wtb[:, :], wt32[:, :]))
            pl.t(*[C("matmul", P[c][:, :], wtb[0:mm, 128 * m:128 * (m + 1)],
                     fb[m][0:mm, CH * c:CH * (c + 1)],
                     start=(m == 0), stop=(m == 21))
                   for m, mm in MT for c in range(NCH)])
            pl.ld(*[C("dma_start", out=B["xsum"][c][:, :],
                      in_=xwork[128 * dt_:128 * (dt_ + 1), CH * c:CH * (c + 1)])
                    for c in range(NCH)])
            pl.v(*[C("tensor_scalar", sgl[:, CH * c:CH * (c + 1)], P[c][:, :],
                     1.0, b2_col[:, dt_:dt_ + 1], OP.mult, OP.add)
                   for c in range(NCH)])
            pl.v(*[C("tensor_add", B["xsum"][c][:, :], B["xsum"][c][:, :],
                     sgl[:, CH * c:CH * (c + 1)]) for c in range(NCH)])
            pl.st(*[C("dma_start",
                      out=xwork[128 * dt_:128 * (dt_ + 1), CH * c:CH * (c + 1)],
                      in_=B["xsum"][c][:, :]) for c in range(NCH)])
        emit_plan(nc, chain, state, pl)
        es2.close()

    pl = PL()
    for c in range(NCH):
        for g in range(8):
            pl.ld(C("dma_start", out=scr_a[:, :],
                    in_=xin[128 * g:128 * (g + 1), CH * c:CH * (c + 1)]))
            pl.ld(C("dma_start", out=scr_b[:, :],
                    in_=xwork[128 * g:128 * (g + 1), CH * c:CH * (c + 1)]))
            pl.v(C("tensor_add", B["xsum"][g][:, :], scr_a[:, :], scr_b[:, :]))
        def wrt(g, src, c=c):
            return [("st", C("dma_start",
                             out=xout[128 * g:128 * (g + 1), CH * c:CH * (c + 1)],
                             in_=src[:, :]))]
        ln_chunk(pl, B, wrt)
    emit_plan(nc, chain, state, pl)


def emit_l1_attn(nc, chain, state, B, dr, aux, xin, xout):
    es = ExitStack()
    sb = lambda name, shape, dt: es.enter_context(nc.sbuf_tensor(name, shape, dt))
    xTb = [sb(f"b_xtb{g}", [128, L], BF16) for g in range(8)]
    qb = [sb(f"b_qb{g}", [128, L], BF16) for g in range(8)]
    kb = [sb(f"b_kb{g}", [128, L], BF16) for g in range(8)]
    vtok = [sb(f"b_vt{t}", [128, 1024], BF16) for t in range(16)]
    Pb = sb("b_Pb", [128, 16 * CH], BF16)
    wt32 = sb("b_wt32", [128, 1024], F32)
    wtb = sb("b_wtb", [128, 1024], BF16)
    dtp = sb("b_dtp", [128, CH], F32)
    dtm = [sb(f"b_dtm{v}", [128, CH], F32) for v in range(4)]
    nrow = sb("b_nrow", [1, L], F32)
    mk2 = sb("b_mk2", [1, 16], F32)
    oT = xTb

    P = B["p_main"]
    scr_a, scr_b, scr_c = B["scr_a"], B["scr_b"], B["scr_c"]
    slopes = [2.0 ** (-0.5 * (i + 1)) for i in range(16)]

    pl = PL()
    pl.ld(C("dma_start", out=dtp[:, :], in_=aux["dt1"][:, :]))
    pl.ld(*[C("dma_start", out=dtm[v][:, :], in_=aux["dtm1"][v, :, :])
            for v in range(4)])
    for g in range(8):
        for c in range(NCH):
            pl.ld(C("dma_start", out=scr_a[:, :],
                    in_=xin[128 * g:128 * (g + 1), CH * c:CH * (c + 1)]))
            pl.v(C("tensor_copy", xTb[g][:, CH * c:CH * (c + 1)], scr_a[:, :]))
    wqk = dr["L1_attn_W_qkv"]
    for part, dst, fold in ((0, qb, 0.125), (1, kb, 1.0)):
        for m in range(8):
            for k in range(8):
                pl.ld(C("dma_start", out=wt32[:, 0:128],
                        in_=wqk[128 * k:128 * (k + 1),
                                1024 * part + 128 * m:1024 * part + 128 * (m + 1)]))
                pl.v(C("tensor_copy", wtb[:, 0:128], wt32[:, 0:128]))
                pl.t(*[C("matmul", P[c][:, :], wtb[:, 0:128],
                         xTb[k][:, CH * c:CH * (c + 1)],
                         start=(k == 0), stop=(k == 7)) for c in range(NCH)])
            for c in range(NCH):
                pl.v(C("tensor_scalar", dst[m][:, CH * c:CH * (c + 1)],
                      P[c][:, :], fold, None, OP.mult))
    for tg in range(4):
        for hf in range(2):
            for k in range(8):
                pl.ld(C("dma_start", out=wt32[:, 0:512],
                        in_=wqk[128 * k:128 * (k + 1),
                                2048 + 512 * hf:2048 + 512 * (hf + 1)]))
                pl.v(C("tensor_copy", wtb[:, 0:512], wt32[:, 0:512]))
                pl.t(*[C("matmul", P[t % 4][:, :],
                         xTb[k][:, 128 * t:128 * (t + 1)],
                         wtb[:, 0:512],
                         start=(k == 0), stop=(k == 7))
                       for t in range(4 * tg, 4 * tg + 4)])
            pl.v(*[C("tensor_copy", vtok[t][:, 512 * hf:512 * (hf + 1)],
                     P[t % 4][:, :]) for t in range(4 * tg, 4 * tg + 4)])
    emit_plan(nc, chain, state, pl)
    pl = PL()

    for h in range(16):
        g, r0 = h // 2, 64 * (h % 2)
        for c in range(NCH):
            pl.s(C("activation", B["scr_bf"][0:64, :],
                  kb[g][r0:r0 + 64, CH * c:CH * (c + 1)], AF.Square))
            pl.t(C("matmul", B["p_row"][0:1, :], B["ones_col_b"][0:64, :],
                  B["scr_bf"][0:64, :]))
            pl.v(C("tensor_copy", nrow[0:1, CH * c:CH * (c + 1)],
                  B["p_row"][0:1, :]))
        pl.v(C("reduce_max", mk2[0:1, h:h + 1], nrow[0:1, :], AX.X))

    for h in range(16):
        g, r0 = h // 2, 64 * (h % 2)
        sl = slopes[h]
        for ic in range(NCH):
            c0 = CH * ic
            jts = list(range(4 * ic, 16))
            pl.s(C("activation", B["scr_bf"][0:64, :],
                  qb[g][r0:r0 + 64, c0:c0 + CH], AF.Square))
            pl.t(C("matmul", B["p_row"][0:1, :], B["ones_col_b"][0:64, :],
                  B["scr_bf"][0:64, :]))
            pl.v(C("tensor_scalar", scr_b[0:1, :], B["p_row"][0:1, :],
                  mk2[0:1, h:h + 1], None, OP.mult))
            pl.s(C("activation", scr_b[0:1, :], scr_b[0:1, :], AF.Sqrt))
            pl.v(C("tensor_scalar_mul", scr_b[0:1, :], scr_b[0:1, :], -1.0))
            pl.v(C("tensor_copy", B["row_bf_a"][:, :], scr_b[0:1, :]))
            pl.t(C("matmul", B["p_bc2"][:, :], B["ones_row_b"][:, :],
                  B["row_bf_a"][:, :]))
            XS = B["xsum"]
            for g0 in range(0, len(jts), 4):
                grp = jts[g0:g0 + 4]
                pl.t(*[C("matmul", P[jt % 4][:, :],
                         kb[g][r0:r0 + 64, 128 * jt:128 * (jt + 1)],
                         qb[g][r0:r0 + 64, c0:c0 + CH]) for jt in grp])
                calls = []
                nd = []
                for jt in grp:
                    v = jt - 4 * ic
                    if v < 4:
                        calls.append(C("scalar_tensor_tensor",
                                       XS[jt % 4][:, :], dtm[v][:, :], -sl,
                                       B["p_bc2"][:, :], OP.mult, OP.add))
                    else:
                        calls.append(C("tensor_scalar", XS[jt % 4][:, :],
                                       dtp[:, :], float(128 * jt - 512 * ic),
                                       -sl, OP.add, OP.mult))
                        nd.append(jt)
                pl.v(*calls)
                if nd:
                    pl.v(*[C("tensor_add", XS[jt % 4][:, :], XS[jt % 4][:, :],
                             B["p_bc2"][:, :]) for jt in nd])
                pl.v(*[C("tensor_add", XS[jt % 4][:, :], XS[jt % 4][:, :],
                         P[jt % 4][:, :]) for jt in grp])
                pl.s(*[C("activation", Pb[:, CH * jt:CH * (jt + 1)],
                         XS[jt % 4][:, :], AF.Exp) for jt in grp])
            pl.t(*[C("matmul", B["p_row"][0:1, :], B["ones_col_b"][:, :],
                     Pb[:, CH * jt:CH * (jt + 1)],
                     start=(i == 0), stop=(i == len(jts) - 1))
                   for i, jt in enumerate(jts)])
            pl.v(C("reciprocal", scr_c[0:1, :], B["p_row"][0:1, :]))
            pl.v(C("tensor_copy", B["row_bf_b"][:, :], scr_c[0:1, :]))
            pl.t(*[C("matmul", B["p_bc"][0:64, :],
                     vtok[jt][:, 64 * h:64 * (h + 1)],
                     Pb[:, CH * jt:CH * (jt + 1)],
                     start=(i == 0), stop=(i == len(jts) - 1))
                   for i, jt in enumerate(jts)])
            pl.t(C("matmul", B["p_bc2"][0:64, :], B["ones_row_b"][:, 0:64],
                  B["row_bf_b"][:, :]))
            pl.v(C("tensor_copy", scr_a[0:64, :], B["p_bc"][0:64, :]))
            pl.v(C("tensor_mul", oT[g][r0:r0 + 64, c0:c0 + CH],
                  scr_a[0:64, :], B["p_bc2"][0:64, :]))

    wo = dr["L1_attn_W_out"]
    for c in range(NCH):
        for g in range(8):
            pl.ld(*[C("dma_start", out=wt32[:, 128 * k:128 * (k + 1)],
                      in_=wo[128 * k:128 * (k + 1), 128 * g:128 * (g + 1)])
                    for k in range(8)])
            pl.v(C("tensor_copy", wtb[:, :], wt32[:, :]))
            pl.t(*[C("matmul", P[g % 4][:, :], wtb[:, 128 * k:128 * (k + 1)],
                     oT[k][:, CH * c:CH * (c + 1)],
                     start=(k == 0), stop=(k == 7)) for k in range(8)])
            pl.ld(C("dma_start", out=scr_a[:, :],
                    in_=xin[128 * g:128 * (g + 1), CH * c:CH * (c + 1)]))
            pl.v(C("tensor_add", B["xsum"][g][:, :], scr_a[:, :], P[g % 4][:, :]))
        def wrt(g, src, c=c):
            return [("st", C("dma_start",
                             out=xout[128 * g:128 * (g + 1), CH * c:CH * (c + 1)],
                             in_=src[:, :]))]
        ln_chunk(pl, B, wrt)
    emit_plan(nc, chain, state, pl)
    es.close()


def emit_l1_ffn(nc, chain, state, B, dr, aux, xin, xout, xwork):
    es = ExitStack()
    sb = lambda name, shape, dt: es.enter_context(nc.sbuf_tensor(name, shape, dt))
    xTb = [sb(f"c_xtb{g}", [128, L], BF16) for g in range(8)]
    fb = [sb(f"c_fb{m}", [128, L], BF16) for m in range(32)]
    wt32 = sb("c_wt32", [128, 1024], F32)
    absb = sb("c_absb", [128, 1024], BF16)
    sg = [sb(f"c_sg{k}", [128, 128], BF16) for k in range(8)]
    al_row = sb("c_alrow", [1, 128], F32)
    al_col = sb("c_alcol", [128, 1], F32)
    b_col = sb("c_bcol", [128, 1], F32)

    P = B["p_main"]
    scr_a, scr_b = B["scr_a"], B["scr_b"]
    pl = PL()
    for g in range(8):
        for c in range(NCH):
            pl.ld(C("dma_start", out=scr_a[:, :],
                    in_=xin[128 * g:128 * (g + 1), CH * c:CH * (c + 1)]))
            pl.v(C("tensor_copy", xTb[g][:, CH * c:CH * (c + 1)], scr_a[:, :]))

    w1t = dr["L1_ffn_w1T"]
    for m in range(32):
        pl.ld(*[C("dma_start", out=wt32[:, 128 * k:128 * (k + 1)],
                  in_=w1t[128 * k:128 * (k + 1), 128 * m:128 * (m + 1)])
                for k in range(8)])
        pl.s(C("activation", absb[:, :], wt32[:, :], AF.Abs))
        pl.t(*[C("matmul", B["p_row"][0:1, 0:128], B["ones_col_b"][:, :],
                 absb[:, 128 * k:128 * (k + 1)], start=(k == 0), stop=(k == 7))
               for k in range(8)])
        pl.s(*[C("activation", sg[k][:, :], wt32[:, 128 * k:128 * (k + 1)],
                 AF.Sign) for k in range(8)])
        pl.v(C("tensor_scalar", al_row[:, :], B["p_row"][0:1, 0:128],
              1.0 / 1024.0, None, OP.mult))
        pl.st(C("dma_start", out=al_col[:, 0:1], in_=al_row[0:1, :]))
        pl.ld(C("dma_start", out=b_col[:, 0:1],
                in_=dr["L1_ffn_b1"][128 * m:128 * (m + 1), :]))
        pl.t(*[C("matmul", P[c][:, :], sg[k][:, :],
                 xTb[k][:, CH * c:CH * (c + 1)],
                 start=(k == 0), stop=(k == 7))
               for k in range(8) for c in range(NCH)])
        pl.v(*[C("tensor_scalar", B["xsum"][c][:, :], P[c][:, :],
                 al_col[:, 0:1], b_col[:, 0:1], OP.mult, OP.add)
               for c in range(NCH)])
        pl.s(*[C("activation", fb[m][:, CH * c:CH * (c + 1)],
                 B["xsum"][c][:, :], AF.Gelu) for c in range(NCH)])

    w2t = dr["L1_ffn_w2T"]
    for dt_ in range(8):
        for grp in range(4):
            ms = list(range(8 * grp, 8 * grp + 8))
            pl.ld(*[C("dma_start", out=wt32[:, 128 * i:128 * (i + 1)],
                      in_=w2t[128 * m:128 * (m + 1), 128 * dt_:128 * (dt_ + 1)])
                    for i, m in enumerate(ms)])
            pl.s(C("activation", absb[:, :], wt32[:, :], AF.Abs))
            pl.t(*[C("matmul", B["p_row"][0:1, 0:128], B["ones_col_b"][:, :],
                     absb[:, 128 * i:128 * (i + 1)],
                     start=(m == 0), stop=(m == 31))
                   for i, m in enumerate(ms)])
            pl.s(*[C("activation", sg[i][:, :], wt32[:, 128 * i:128 * (i + 1)],
                     AF.Sign) for i in range(8)])
            pl.t(*[C("matmul", P[c][:, :], sg[i][:, :],
                     fb[m][:, CH * c:CH * (c + 1)],
                     start=(m == 0), stop=(m == 31))
                   for i, m in enumerate(ms) for c in range(NCH)])
        pl.v(C("tensor_scalar", al_row[:, :], B["p_row"][0:1, 0:128],
              1.0 / 4096.0, None, OP.mult))
        pl.st(C("dma_start", out=al_col[:, 0:1], in_=al_row[0:1, :]))
        pl.ld(C("dma_start", out=b_col[:, 0:1],
                in_=dr["L1_ffn_b2"][128 * dt_:128 * (dt_ + 1), :]))
        pl.v(*[C("tensor_scalar", B["xsum"][c][:, :], P[c][:, :],
                 al_col[:, 0:1], b_col[:, 0:1], OP.mult, OP.add)
               for c in range(NCH)])
        pl.st(*[C("dma_start",
                  out=xwork[128 * dt_:128 * (dt_ + 1), CH * c:CH * (c + 1)],
                  in_=B["xsum"][c][:, :]) for c in range(NCH)])
    for c in range(NCH):
        for g in range(8):
            pl.ld(C("dma_start", out=scr_a[:, :],
                    in_=xin[128 * g:128 * (g + 1), CH * c:CH * (c + 1)]))
            pl.ld(C("dma_start", out=scr_b[:, :],
                    in_=xwork[128 * g:128 * (g + 1), CH * c:CH * (c + 1)]))
            pl.v(C("tensor_add", B["xsum"][g][:, :], scr_a[:, :], scr_b[:, :]))
        def wrt(g, src, c=c):
            return [("st", C("dma_start",
                             out=xout[128 * g:128 * (g + 1), CH * c:CH * (c + 1)],
                             in_=src[:, :]))]
        ln_chunk(pl, B, wrt)
    emit_plan(nc, chain, state, pl)
    es.close()


def build_aux_arrays():
    A = {}
    t = np.arange(L, dtype=np.float64)
    dist = np.empty((32, L), np.float32)
    dist[:28] = (L - 1.0) - t
    dist[28:] = t
    A["dist"] = dist

    selpw = np.zeros((32, 1024), np.float32)
    for g in range(8):
        for col in range(128):
            selpw[4 * g + col // 32, 128 * g + col] = 1.0
    A["selpw"] = selpw.astype(BF)

    selT = np.zeros((128, 256), np.float32)
    for g in range(8):
        for p in range(128):
            selT[p, 32 * g + 4 * g + p // 32] = 1.0
    A["selT"] = selT.astype(BF)

    selH = np.zeros((128, 2), np.float32)
    selH[0:64, 0] = 1.0
    selH[64:128, 1] = 1.0
    A["selH"] = selH.astype(BF)
    selB = np.zeros((2, 128), np.float32)
    selB[0, 0:64] = 1.0
    selB[1, 64:128] = 1.0
    A["selB"] = selB.astype(BF)

    prot = np.zeros((128, 128), np.float32)
    for blk in range(2):
        o = 64 * blk
        for m in range(64):
            if m < 32:
                prot[o + m + 32, o + m] = -1.0
            else:
                prot[o + m - 32, o + m] = 1.0
    A["prot"] = prot.astype(BF)

    DIM_HEAD = 64
    WINDOW = 512
    half = np.arange(0, DIM_HEAD, 2, dtype=np.float64)
    inv_freq = 1.0 / (10000.0 ** (half / DIM_HEAD))
    base = (half + 0.4 * DIM_HEAD) / (1.4 * DIM_HEAD)
    dh = np.arange(DIM_HEAD)
    invf_d = inv_freq[dh % 32]
    base_d = base[dh % 32]
    i = np.arange(L, dtype=np.float64)
    freqs = i[None, :] * invf_d[:, None]
    cw = 512.0 * (i // 512)
    srel = base_d[:, None] ** ((i[None, :] - cw[None, :]) / 256.0)
    qcos = np.cos(freqs) * srel * 8.0
    qsin = np.sin(freqs) * srel * 8.0
    A["qcos"] = np.tile(qcos, (2, 1)).astype(BF)
    A["qsin"] = np.tile(qsin, (2, 1)).astype(BF)
    kcos = np.zeros((DIM_HEAD, 4096), np.float64)
    ksin = np.zeros((DIM_HEAD, 4096), np.float64)
    for w in range(4):
        for jcol in range(1024):
            j = 512 * (w - 1) + jcol
            if j < 0:
                continue
            fr = j * invf_d
            sr = base_d ** (-(j - 512.0 * w) / 256.0)
            kcos[:, 1024 * w + jcol] = np.cos(fr) * sr
            ksin[:, 1024 * w + jcol] = np.sin(fr) * sr
    A["kcos"] = np.tile(kcos, (2, 1)).astype(BF)
    A["ksin"] = np.tile(ksin, (2, 1)).astype(BF)

    fm = np.zeros((8, 128, CH), np.float32)
    r = np.arange(128)[:, None]
    cc = np.arange(CH)[None, :]
    for jc in range(8):
        d = 512 + cc - 128 * jc - r
        fm[jc] = np.where((d >= 0) & (d <= 512), 0.0, NEG)
    A["fmask"] = fm

    A["dt1"] = (r - cc).astype(np.float32) * np.ones((128, CH), np.float32)
    dtm = np.zeros((4, 128, CH), np.float32)
    for v in range(4):
        val = r - cc + 128 * v
        dtm[v] = np.where(val >= 0, val, 1.0e33).astype(np.float32)
    A["dtm1"] = dtm
    return A


AUX_SPECS = {
    "dist": ([32, L], F32), "selpw": ([32, 1024], BF16),
    "selT": ([128, 256], BF16), "selH": ([128, 2], BF16),
    "selB": ([2, 128], BF16), "prot": ([128, 128], BF16),
    "qcos": ([128, 4 * CH], BF16), "qsin": ([128, 4 * CH], BF16),
    "kcos": ([128, 4096], BF16), "ksin": ([128, 4096], BF16),
    "fmask": ([8, 128, CH], F32), "dt1": ([128, CH], F32),
    "dtm1": ([4, 128, CH], F32),
}

W_SPECS = {
    "L0_attn_W_in": [1024, 2080],
    "L0_attn_conv_kernel": [4, 2080, 1],
    "L0_attn_theta": [1024, 1],
    "L0_attn_decay_slopes": [28, 1],
    "L0_attn_anchor_slopes": [4, 1],
    "L0_attn_score_scale": [32, 1],
    "L0_attn_W_re": [32, 32],
    "L0_attn_W_im": [32, 32],
    "L0_attn_norm_scale": [64, 1],
    "L0_attn_W_out": [1024, 1024],
    "L0_ffn_wqkv": [2, 1024, 1536],
    "L0_ffn_q_scale": [2, 64, 1],
    "L0_ffn_k_scale": [2, 64, 1],
    "L0_ffn_wo": [2, 512, 1024],
    "L0_ffn_ln_g": [2, 1024, 1],
    "L0_ffn_ln_b": [2, 1024, 1],
    "L0_ffn_w1": [2, 1024, 5460],
    "L0_ffn_b1": [2, 5460, 1],
    "L0_ffn_w2": [2, 2730, 1024],
    "L0_ffn_b2": [2, 1024, 1],
    "L1_attn_W_qkv": [1024, 3072],
    "L1_attn_W_out": [1024, 1024],
    "L1_ffn_w1T": [1024, 4096],
    "L1_ffn_b1": [4096, 1],
    "L1_ffn_w2T": [4096, 1024],
    "L1_ffn_b2": [1024, 1],
}

MOD_WEIGHTS = {
    "a0": [n for n in W_SPECS if n.startswith("L0_attn")],
    "f0": [n for n in W_SPECS if n.startswith("L0_ffn")],
    "a1": ["L1_attn_W_qkv", "L1_attn_W_out"],
    "f1": ["L1_ffn_w1T", "L1_ffn_b1", "L1_ffn_w2T", "L1_ffn_b2"],
}
MOD_AUX = {
    "a0": ["dist", "selpw", "selT"],
    "f0": ["selB", "selH", "prot", "qcos", "qsin", "kcos", "ksin", "fmask"],
    "a1": ["dt1", "dtm1"],
    "f1": [],
}


def build_mod_nc(mod):
    nc = bass.Bass("TRN2", target_bir_lowering=False)
    dr = {}
    for n in MOD_WEIGHTS[mod]:
        dr[n] = nc.dram_tensor(n, W_SPECS[n], F32, kind="ExternalInput").ap()
    aux = {}
    for n in MOD_AUX[mod]:
        shape, dt = AUX_SPECS[n]
        aux[n] = nc.dram_tensor("aux_" + n, shape, dt, kind="ExternalInput").ap()
    xin = nc.dram_tensor("xin", [D, L], F32, kind="ExternalInput").ap()
    xout = nc.dram_tensor("xout", [D, L], F32, kind="ExternalOutput").ap()
    xwork = None
    if mod in ("f0", "f1"):
        xwork = nc.dram_tensor("xwork", [D, L], F32, kind="Internal").ap()
    es = ExitStack()
    chain = es.enter_context(nc.semaphore("chain"))
    state = {"val": 0}
    B = build_common(nc, es, chain, state)
    if mod == "a0":
        emit_l0_attn(nc, chain, state, B, dr, aux, xin, xout)
    elif mod == "f0":
        emit_l0_ffn(nc, chain, state, B, dr, aux, xin, xout, xwork)
    elif mod == "a1":
        emit_l1_attn(nc, chain, state, B, dr, aux, xin, xout)
    elif mod == "f1":
        emit_l1_ffn(nc, chain, state, B, dr, aux, xin, xout, xwork)
    tot = state["val"]
    with nc.Block() as blk:
        blk.tensor(lambda e: e.wait_ge(chain, tot))
        blk.vector(lambda e: e.wait_ge(chain, tot))
        blk.scalar(lambda e: e.wait_ge(chain, tot))
        blk.sync(lambda e: e.wait_ge(chain, tot))
        blk.gpsimd(lambda e: e.wait_ge(chain, tot))
    es.close()
    return nc


def kernel_unfused(**inputs):
    inp = {k: np.asarray(v) for k, v in inputs.items()}
    arrs = {}
    arrs["L0_attn_W_in"] = inp["L0_attn_W_in"]
    arrs["L0_attn_conv_kernel"] = np.ascontiguousarray(
        inp["L0_attn_conv_kernel"].reshape(4, 2080, 1))
    arrs["L0_attn_theta"] = np.ascontiguousarray(
        inp["L0_attn_theta"].reshape(1024, 1))
    arrs["L0_attn_decay_slopes"] = inp["L0_attn_decay_slopes"].reshape(28, 1)
    arrs["L0_attn_anchor_slopes"] = inp["L0_attn_anchor_slopes"].reshape(4, 1)
    arrs["L0_attn_score_scale"] = inp["L0_attn_score_scale"].reshape(32, 1)
    arrs["L0_attn_W_re"] = inp["L0_attn_W_re"]
    arrs["L0_attn_W_im"] = inp["L0_attn_W_im"]
    arrs["L0_attn_norm_scale"] = inp["L0_attn_norm_scale"].reshape(64, 1)
    arrs["L0_attn_W_out"] = inp["L0_attn_W_out"]
    arrs["L0_ffn_wqkv"] = inp["L0_ffn_wqkv"]
    arrs["L0_ffn_q_scale"] = inp["L0_ffn_q_scale"].reshape(2, 64, 1)
    arrs["L0_ffn_k_scale"] = inp["L0_ffn_k_scale"].reshape(2, 64, 1)
    arrs["L0_ffn_wo"] = inp["L0_ffn_wo"]
    arrs["L0_ffn_ln_g"] = inp["L0_ffn_ln_g"].reshape(2, 1024, 1)
    arrs["L0_ffn_ln_b"] = inp["L0_ffn_ln_b"].reshape(2, 1024, 1)
    arrs["L0_ffn_w1"] = inp["L0_ffn_w1"]
    arrs["L0_ffn_b1"] = inp["L0_ffn_b1"].reshape(2, 5460, 1)
    arrs["L0_ffn_w2"] = inp["L0_ffn_w2"]
    arrs["L0_ffn_b2"] = inp["L0_ffn_b2"].reshape(2, 1024, 1)
    arrs["L1_attn_W_qkv"] = inp["L1_attn_W_qkv"]
    arrs["L1_attn_W_out"] = inp["L1_attn_W_out"]
    arrs["L1_ffn_w1T"] = np.ascontiguousarray(inp["L1_ffn_w1"].T)
    arrs["L1_ffn_b1"] = inp["L1_ffn_b1"].reshape(4096, 1)
    arrs["L1_ffn_w2T"] = np.ascontiguousarray(inp["L1_ffn_w2"].T)
    arrs["L1_ffn_b2"] = inp["L1_ffn_b2"].reshape(1024, 1)
    arrs = {k: np.ascontiguousarray(v.astype(np.float32)) for k, v in arrs.items()}
    auxarr = build_aux_arrays()

    X = np.ascontiguousarray(
        inp["L0_attn_x"].astype(np.float32).transpose(0, 2, 1))

    mods = ["a0", "f0", "a1", "f1"]
    ncs = {m: build_mod_nc(m) for m in mods}
    cur = X
    for mod in mods:
        nxt = np.empty_like(cur)
        const = {n: arrs[n] for n in MOD_WEIGHTS[mod]}
        const.update({"aux_" + n: auxarr[n] for n in MOD_AUX[mod]})
        for s in range(4):
            in_maps = []
            for cid in range(8):
                d = dict(const)
                d["xin"] = np.ascontiguousarray(cur[cid * 4 + s])
                in_maps.append(d)
            res = run_bass_kernel_spmd(ncs[mod], in_maps, core_ids=list(range(8)))
            for cid in range(8):
                nxt[cid * 4 + s] = res.results[cid]["xout"]
        cur = nxt
    out = np.ascontiguousarray(cur.transpose(0, 2, 1)).astype(np.float32)
    return out


class NCWrap:
    def __init__(self, nc):
        self._nc = nc
        self.tag = ""

    def __getattr__(self, a):
        return getattr(self._nc, a)

    def sbuf_tensor(self, name, shape, dt):
        return self._nc.sbuf_tensor(name + self.tag, shape, dt)

    def psum_tensor(self, name, shape, dt):
        return self._nc.psum_tensor(name + self.tag, shape, dt)


def build_fused_nc():
    nc0 = bass.Bass("TRN2", target_bir_lowering=False)
    nc = NCWrap(nc0)
    dr = {}
    for n, shape in W_SPECS.items():
        dr[n] = nc0.dram_tensor(n, shape, F32, kind="ExternalInput").ap()
    aux = {}
    for n, (shape, dt) in AUX_SPECS.items():
        aux[n] = nc0.dram_tensor("aux_" + n, shape, dt, kind="ExternalInput").ap()
    xin_all = nc0.dram_tensor("xin", [4, D, L], F32, kind="ExternalInput").ap()
    xout_all = nc0.dram_tensor("xout", [4, D, L], F32, kind="ExternalOutput").ap()
    xs1 = nc0.dram_tensor("xs1", [D, L], F32, kind="Internal").ap()
    xs2 = nc0.dram_tensor("xs2", [D, L], F32, kind="Internal").ap()
    xs3 = nc0.dram_tensor("xs3", [D, L], F32, kind="Internal").ap()
    xwork = nc0.dram_tensor("xwork", [D, L], F32, kind="Internal").ap()
    es = ExitStack()
    chain = es.enter_context(nc0.semaphore("chain"))
    state = {"val": 0}
    B = build_common(nc, es, chain, state)
    for s in range(4):
        nc.tag = f"_s{s}a"
        emit_l0_attn(nc, chain, state, B, dr, aux, xin_all[s], xs1)
        nc.tag = f"_s{s}b"
        emit_l0_ffn(nc, chain, state, B, dr, aux, xs1, xs2, xwork)
        nc.tag = f"_s{s}c"
        emit_l1_attn(nc, chain, state, B, dr, aux, xs2, xs3)
        nc.tag = f"_s{s}d"
        emit_l1_ffn(nc, chain, state, B, dr, aux, xs3, xout_all[s], xwork)
    tot = state["val"]
    with nc0.Block() as blk:
        blk.tensor(lambda e: e.wait_ge(chain, tot))
        blk.vector(lambda e: e.wait_ge(chain, tot))
        blk.scalar(lambda e: e.wait_ge(chain, tot))
        blk.sync(lambda e: e.wait_ge(chain, tot))
        blk.gpsimd(lambda e: e.wait_ge(chain, tot))
    es.close()
    return nc0


def _prep(inputs):
    inp = {k: np.asarray(v) for k, v in inputs.items()}
    arrs = {}
    arrs["L0_attn_W_in"] = inp["L0_attn_W_in"]
    arrs["L0_attn_conv_kernel"] = np.ascontiguousarray(
        inp["L0_attn_conv_kernel"].reshape(4, 2080, 1))
    arrs["L0_attn_theta"] = np.ascontiguousarray(
        inp["L0_attn_theta"].reshape(1024, 1))
    arrs["L0_attn_decay_slopes"] = inp["L0_attn_decay_slopes"].reshape(28, 1)
    arrs["L0_attn_anchor_slopes"] = inp["L0_attn_anchor_slopes"].reshape(4, 1)
    arrs["L0_attn_score_scale"] = inp["L0_attn_score_scale"].reshape(32, 1)
    arrs["L0_attn_W_re"] = inp["L0_attn_W_re"]
    arrs["L0_attn_W_im"] = inp["L0_attn_W_im"]
    arrs["L0_attn_norm_scale"] = inp["L0_attn_norm_scale"].reshape(64, 1)
    arrs["L0_attn_W_out"] = inp["L0_attn_W_out"]
    arrs["L0_ffn_wqkv"] = inp["L0_ffn_wqkv"]
    arrs["L0_ffn_q_scale"] = inp["L0_ffn_q_scale"].reshape(2, 64, 1)
    arrs["L0_ffn_k_scale"] = inp["L0_ffn_k_scale"].reshape(2, 64, 1)
    arrs["L0_ffn_wo"] = inp["L0_ffn_wo"]
    arrs["L0_ffn_ln_g"] = inp["L0_ffn_ln_g"].reshape(2, 1024, 1)
    arrs["L0_ffn_ln_b"] = inp["L0_ffn_ln_b"].reshape(2, 1024, 1)
    arrs["L0_ffn_w1"] = inp["L0_ffn_w1"]
    arrs["L0_ffn_b1"] = inp["L0_ffn_b1"].reshape(2, 5460, 1)
    arrs["L0_ffn_w2"] = inp["L0_ffn_w2"]
    arrs["L0_ffn_b2"] = inp["L0_ffn_b2"].reshape(2, 1024, 1)
    arrs["L1_attn_W_qkv"] = inp["L1_attn_W_qkv"]
    arrs["L1_attn_W_out"] = inp["L1_attn_W_out"]
    arrs["L1_ffn_w1T"] = np.ascontiguousarray(inp["L1_ffn_w1"].T)
    arrs["L1_ffn_b1"] = inp["L1_ffn_b1"].reshape(4096, 1)
    arrs["L1_ffn_w2T"] = np.ascontiguousarray(inp["L1_ffn_w2"].T)
    arrs["L1_ffn_b2"] = inp["L1_ffn_b2"].reshape(1024, 1)
    arrs = {k: np.ascontiguousarray(v.astype(np.float32)) for k, v in arrs.items()}
    X = np.ascontiguousarray(
        inp["L0_attn_x"].astype(np.float32).transpose(0, 2, 1))
    return arrs, X


def kernel_fused(**inputs):
    arrs, X = _prep(inputs)
    auxarr = build_aux_arrays()
    nc = build_fused_nc()
    const = dict(arrs)
    const.update({"aux_" + n: auxarr[n] for n in AUX_SPECS})
    in_maps = []
    for cid in range(8):
        d = dict(const)
        d["xin"] = np.ascontiguousarray(X[4 * cid:4 * (cid + 1)])
        in_maps.append(d)
    res = run_bass_kernel_spmd(nc, in_maps, core_ids=list(range(8)))
    out = np.empty_like(X)
    for cid in range(8):
        out[4 * cid:4 * (cid + 1)] = res.results[cid]["xout"]
    return np.ascontiguousarray(out.transpose(0, 2, 1)).astype(np.float32)


def kernel(**inputs):
    return kernel_fused(**inputs)
